# version 36
# baseline (speedup 1.0000x reference)
import numpy as np
import concourse.bass as bass
import concourse.mybir as mybir
from concourse.bass_utils import run_bass_kernel_spmd

F32 = mybir.dt.float32
BF16 = mybir.dt.bfloat16
AF = mybir.ActivationFunctionType
ALU = mybir.AluOpType
AX = mybir.AxisListType

C, D, H, S, KD, V = 4096, 1024, 8, 512, 64, 128
NCORES = 8

DMA_SEMS = ("ld", "gbf", "gb2", "gw", "wT", "xts", "xt2", "xvw", "xvr",
            "cast", "xres", "atok", "x1w", "x1t", "out")
ENG_SEMS = ("pe", "act", "dve", "gpc")


def build_program(debug_taps=False):
    nc = bass.Bass("TRN2", target_bir_lowering=False, dynamic_dma_scratch_size=16384)

    x_h = nc.dram_tensor("x", [C, D], F32, kind="ExternalInput").ap()
    mq_h = nc.dram_tensor("mq", [D, 512], F32, kind="ExternalInput").ap()
    mk_h = nc.dram_tensor("mk", [D, 512], F32, kind="ExternalInput").ap()
    mv_h = nc.dram_tensor("mv", [D, D], F32, kind="ExternalInput").ap()
    wsh_h = nc.dram_tensor("wsh", [D, D], F32, kind="ExternalInput").ap()
    wex_h = nc.dram_tensor("wex", [D, D], F32, kind="ExternalInput").ap()
    wg_h = nc.dram_tensor("wg", [D, 7], F32, kind="ExternalInput").ap()
    bias_h = nc.dram_tensor("bias", [1, D], F32, kind="ExternalInput").ap()
    out_h = nc.dram_tensor("out", [C, D], F32, kind="ExternalOutput").ap()
    xbf_h = nc.dram_tensor("xbf", [C, D], BF16, kind="Internal").ap()
    xv_h = nc.dram_tensor("xvh", [H, C, V], BF16, kind="Internal").ap()
    att_h = nc.dram_tensor("atth", [D, S], BF16, kind="Internal").ap()
    x1_h = nc.dram_tensor("x1h", [S, D], BF16, kind="Internal").ap()
    ident_h = nc.dram_tensor("identh", [1, 16640], BF16, kind="Internal").ap()
    wshbf_h = nc.dram_tensor("wshbf", [D, D], BF16, kind="Internal").ap()
    wexbf_h = nc.dram_tensor("wexbf", [D, D], BF16, kind="Internal").ap()
    taps = {}
    if debug_taps:
        taps["o_xv"] = nc.dram_tensor("o_xv", [H, C, V], BF16, kind="ExternalOutput").ap()
        taps["o_attacc"] = nc.dram_tensor("o_attacc", [128, 8 * 512], F32, kind="ExternalOutput").ap()
        taps["o_A"] = nc.dram_tensor("o_A", [128, 4 * 10 * 512], BF16, kind="ExternalOutput").ap()
        taps["o_zinv"] = nc.dram_tensor("o_zinv", [128, 4], F32, kind="ExternalOutput").ap()
        taps["o_qs"] = nc.dram_tensor("o_qs", [128, 2 * 4 * 512], BF16, kind="ExternalOutput").ap()
        taps["o_ks"] = nc.dram_tensor("o_ks", [128, 4 * 512], BF16, kind="ExternalOutput").ap()
        taps["o_q64"] = nc.dram_tensor("o_q64", [128, 5 * 512], BF16, kind="ExternalOutput").ap()
        taps["o_stat"] = nc.dram_tensor("o_stat", [128, 80], F32, kind="ExternalOutput").ap()
        taps["o_xbf"] = nc.dram_tensor("o_xbf", [C, D], BF16, kind="ExternalOutput").ap()

    from contextlib import ExitStack
    with ExitStack() as ctx:
        E = ctx.enter_context
        mq_bf = E(nc.sbuf_tensor("mq_bf", [128, 8, 512], BF16))
        mk_bf = E(nc.sbuf_tensor("mk_bf", [128, 8, 512], BF16))
        wshT = E(nc.sbuf_tensor("wshT", [128, 8, 1024], BF16))
        wexT = E(nc.sbuf_tensor("wexT", [128, 8, 1024], BF16))
        wg_bf = E(nc.sbuf_tensor("wg_bf", [128, 8, 7], BF16))
        bias_bc = E(nc.sbuf_tensor("bias_bc", [128, D], F32))
        A_buf = E(nc.sbuf_tensor("A_buf", [128, 4, 10, 512], BF16))
        qs = E(nc.sbuf_tensor("qs", [128, 2, 4, 512], BF16))
        ks = E(nc.sbuf_tensor("ks", [128, 4, 512], BF16))
        ks64 = E(nc.sbuf_tensor("ks64", [128, 5, 512], BF16))
        xts = E(nc.sbuf_tensor("xts", [128, 8, 512], BF16))
        xv_raw = E(nc.sbuf_tensor("xv_raw", [128, 32, 128], BF16))
        xv_sl = E(nc.sbuf_tensor("xv_sl", [128, 32, 128], BF16))
        att_acc = E(nc.sbuf_tensor("att_acc", [128, 8, 512], F32))
        atok = E(nc.sbuf_tensor("atok", [128, 4, 1024], BF16))
        x1T = E(nc.sbuf_tensor("x1T", [128, 8, 512], BF16))
        xres = E(nc.sbuf_tensor("xres", [128, 4, 1024], F32))
        r_buf = E(nc.sbuf_tensor("r_buf", [128, 4, 1024], F32))
        zacc = E(nc.sbuf_tensor("zacc", [128, 4, 4, 2], F32))
        zinv = E(nc.sbuf_tensor("zinv", [128, 4], F32))
        stat = E(nc.sbuf_tensor("stat", [128, 80], F32))
        gate_t = E(nc.sbuf_tensor("gate_t", [128, 7], F32))
        gate_p4 = E(nc.sbuf_tensor("gate_p4", [128, 4, 7], F32))
        gate_e4 = E(nc.sbuf_tensor("gate_e4", [128, 4, 7], F32))
        gate_c = E(nc.sbuf_tensor("gate_c", [128, 7], F32))
        csh_bc = E(nc.sbuf_tensor("csh_bc", [128, 1024], BF16))
        cex_bc = E(nc.sbuf_tensor("cex_bc", [128, 1024], BF16))
        ident = E(nc.sbuf_tensor("ident", [128, 1, 128], BF16))
        ps_a = E(nc.psum_tensor("ps_a", [128, 2048], F32))
        ps_av = E(nc.psum_tensor("ps_av", [128, 1024], F32))
        ps_d = E(nc.psum_tensor("ps_d", [128, 512], F32))
        ps_e = E(nc.psum_tensor("ps_e", [128, 512], F32))
        sems = {}
        for name in DMA_SEMS + ENG_SEMS:
            sems[name] = E(nc.semaphore("s_" + name))
        block = E(nc.Block())

        stage_xv = atok[:, :, :].rearrange("p t (g v) -> p t g v", g=8)
        st = stat
        SUMS, SSQ, RSTD, NEGM = 0, 4, 8, 12
        SUMS2, SUMS2B, SSQ2, RSTD2, NEGM2, WSUM = 16, 20, 24, 28, 32, 36
        T1, T2, T3 = 40, 44, 48
        G_NM, G_ES, G_RS, G_M1N, G_M2N, G_S, G_A, G_B = 52, 53, 54, 55, 56, 57, 58, 59
        Q, QW, RW = 64, 68, 72

        def col(c, n=1):
            return st[:, c:c + n]

        def prog(me, eng):
            cnt = {k: 0 for k in sems}
            last = {}

            def dma(q, sem, dst, src, **kw):
                cnt[sem] += 16
                if me == q:
                    eng.dma_start(dst, src, **kw).then_inc(sems[sem], 16)
                return cnt[sem]

            def tdma(sem, dst, src, q="sync"):
                cnt[sem] += 16
                if me == q:
                    eng.dma_start(dst, src, transpose=True).then_inc(sems[sem], 16)
                return cnt[sem]

            def wait(who, semname, val):
                if val > 0 and me == who:
                    eng.wait_ge(sems[semname], val)

            def wait_full(who, semname):
                wait(who, semname, cnt[semname])

            def op(engkey, fn, *a, **kw):
                if me == engkey:
                    last[engkey] = getattr(eng, fn)(*a, **kw)

            def sig(engkey):
                cnt[engkey] += 1
                if me == engkey:
                    last[engkey].then_inc(sems[engkey], 1)
                return cnt[engkey]

            def vop(fn, *a, dep=0, **kw):
                wait("dve", "dve", dep)
                op("dve", fn, *a, **kw)
                return sig("dve")

            def gop(fn, *a, dep=0, **kw):
                cnt["gpc"] += 1
                if me == "gp":
                    if dep:
                        eng.wait_ge(sems["gpc"], dep)
                    getattr(eng, fn)(*a, **kw).then_inc(sems["gpc"], 1)
                return cnt["gpc"]

            def aop(fn, *a, dep=0, **kw):
                wait("act", "act", cnt["act"])
                op("act", fn, *a, **kw)
                return sig("act")

            def vbar():
                wait("dve", "dve", cnt["dve"])

            def mm(out, lhsT, rhs, **kw):
                op("pe", "matmul", out, lhsT, rhs, **kw)

            m_bias = dma("sync", "ld", bias_bc[0:1, :], bias_h[:, :])
            vop("memset", stat[:, :], 0.0, dep=0)

            for ch in range(8):
                dma("gp", "gbf" if ch < 4 else "gb2",
                    xbf_h[S * ch:S * (ch + 1), :],
                    x_h[S * ch:S * (ch + 1), :])
            for (w_hbm, wbf_hbm) in ((wsh_h, wshbf_h), (wex_h, wexbf_h)):
                for half in range(2):
                    rows = slice(512 * half, 512 * (half + 1))
                    dma("gp", "gw", wbf_hbm[rows, :], w_hbm[rows, :])

            v_z = vop("memset", atok[:, 0, 0:128], 0.0)
            v_o = vop("memset", atok[0:1, 1, 0:128], 1.0, dep=v_z)
            wait("sync", "dve", v_z)
            dma("sync", "ld", ident_h[0:1, 0:16384]
                .rearrange("o (p c) -> p o c", p=128), atok[:, 0:1, 0:128])
            wait_full("sync", "ld")
            wait("sync", "dve", v_o)
            dma("sync", "ld", ident_h[0:1, 0:16512]
                .rearrange("o (a b) -> o a b", a=128)[0:1, :, 0:1],
                atok[0:1, 1:2, 0:128].rearrange("p a c -> p c a"),
                allow_slow_non_contiguous=True)
            wait_full("sync", "ld")
            dma("sync", "ld", ident[:, :, :],
                ident_h[0:1, 0:16384].rearrange("o (p c) -> p o c", p=128))

            r8 = r_buf[:, :, :].rearrange("p t d -> p (t d)").rearrange(
                "p (a b) -> p a b", a=8)
            mv_view = A_buf[:, :, :, :].rearrange("p a b c -> p (a b c)")[:, 0:8192] \
                .rearrange("p (a b) -> p a b", a=8)
            v_w = 0
            for half in range(2):
                cols = slice(512 * half, 512 * (half + 1))
                m_w = dma("sync", "ld", r8,
                          mv_h.rearrange("(a p) m -> p a m", p=128)[:, :, cols])
                wait("dve", "ld", m_w)
                v_w = vop("tensor_copy", mv_view[:, :, cols], r8, dep=v_w)
                wait("sync", "dve", v_w)

            ones_ap = att_acc[0:1, 0, 0:128]
            v_ones = vop("memset", ones_ap, 1.0)
            wait_full("pe", "ld")
            wait("pe", "dve", v_ones)
            mm(ps_d[:, :], ones_ap, bias_bc[0:1, 0:512], start=True, stop=True)
            sig("pe")
            mm(ps_e[:, :], ones_ap, bias_bc[0:1, 512:1024], start=True, stop=True)
            v_pe = sig("pe")
            wait("dve", "pe", v_pe)
            vop("tensor_copy", bias_bc[:, 0:512], ps_d[:, :])
            v_bias = vop("tensor_copy", bias_bc[:, 512:1024], ps_e[:, :])

            rg = r_buf[:, :, :].rearrange("p t d -> p (t d)")[:, 0:56] \
                .rearrange("p (a b) -> p a b", a=8)
            prev_mm = {0: v_pe, 1: v_pe}
            prev_evac = {0: v_bias, 1: v_bias}
            mark_p1mm = 0
            def mk_simple(w_view, w_sb, stg):
                return (lambda: dma("sync", "ld", stg, w_view),
                        lambda dep: vop("tensor_copy", w_sb, stg, dep=dep))

            wjobs = [
                mk_simple(mq_h.rearrange("(a p) m -> p a m", p=128),
                          mq_bf[:, :, :], r8),
                mk_simple(mk_h.rearrange("(a p) m -> p a m", p=128),
                          mk_bf[:, :, :], r8),
                mk_simple(wg_h.rearrange("(a p) e -> p a e", p=128),
                          wg_bf[:, :, :], rg),
            ]
            v_wcp = v_w
            m_wld = 0
            mark_par = {0: 0, 1: 0}
            for ch in range(8):
                if ch == 0:
                    wait_full("sync", "gbf")
                elif ch == 4:
                    wait_full("sync", "gb2")
                pbuf = xts if ch % 2 == 0 else x1T
                psem = "xts" if ch % 2 == 0 else "xt2"
                wait("sync", "pe", mark_par[ch % 2])
                for dt in range(8):
                    m_xts = tdma(psem, pbuf[:, dt, :],
                                 xbf_h[S * ch:S * (ch + 1), 128 * dt:128 * (dt + 1)])
                if 1 <= ch <= len(wjobs):
                    wait("dve", "ld", m_wld)
                    v_wcp = wjobs[ch - 1][1](v_wcp)
                if ch < len(wjobs):
                    wait("sync", "dve", v_wcp)
                    m_wld = wjobs[ch][0]()
                vbar()
                for tt in range(4):
                    for dvc in range(2):
                        sel = (tt * 2 + dvc) % 2
                        psel = ps_d if sel == 0 else ps_e
                        wait("pe", psem, m_xts)
                        wait("pe", "dve", prev_evac[sel])
                        for dt in range(8):
                            mm(psel[:, :], pbuf[:, dt, 128 * tt:128 * (tt + 1)],
                               mv_view[:, dt, 512 * dvc:512 * (dvc + 1)],
                               start=(dt == 0), stop=(dt == 7))
                        prev_mm[sel] = sig("pe")
                        wait("dve", "pe", prev_mm[sel])
                        wait_full("dve", "xvw")
                        prev_evac[sel] = vop(
                            "tensor_copy", stage_xv[:, tt, 4 * dvc:4 * (dvc + 1), :],
                            psel[:, :])
                mark_p1mm = max(prev_mm[0], prev_mm[1])
                mark_par[ch % 2] = mark_p1mm
                v_stage = max(prev_evac[0], prev_evac[1])
                wait("gp", "dve", v_stage)
                for g2 in range(8):
                    dma("gp", "xvw", xv_h[g2, S * ch:S * (ch + 1), :]
                        .rearrange("(t p) v -> p t v", p=128), stage_xv[:, :, g2, :])

            class NS:
                pass
            ns = NS()
            ns.v_scale = 0
            ns.v_zfin = 0
            ns.v_avadd = {0: 0, 1: 0}
            ns.mark_proj = mark_p1mm
            ns.mark_av_all = 0
            ns.mark_av_hp = [0] * 8
            ns.v_exp = 0
            ns.v_exp_slot = {0: 0, 1: 0}
            ns.v_out2 = {0: 0, 1: 0, 2: 0, 3: 0}
            ns.v_gate_exp = 0
            ns.v_gatepost = 0
            ns.v_q_evac = 0
            ns.v_sq2 = 0
            ns.v_p2 = 0
            ns.v_resid = 0
            ns.v_x1 = 0
            ns.v_ln1 = 0
            ns.m_xres = 0
            ns.m_xts = 0
            ns.m_xt2 = 0
            ns.m_atok = 0
            ns.m_x1T = 0
            ns.m_cast = 0
            ns.mark_ffn = mark_p1mm
            ns.mark_gate = {}
            ns.v_lnlast = 0
            ns.avsig = {}
            ns.v_zfin_par = {0: 0, 1: 0}
            ns.proj_sel = 0
            ns.v_ve1 = 0
            ns.v_ve2 = 0
            ns.v_x1T = 0
            ns.v_gate_fix = 0
            ns.v_ccp = 0
            ns.evacs = {}
            ns.ev_lhsT = 0
            ns.v_sq1 = 0
            ns.gfix = {0: 0, 1: 0}
            ns.v_vg1 = 0

            def ln_stats_a(SUMS_, SSQ_, NEGM_, dve_dep):
                va = vop("tensor_scalar", col(NEGM_, 4), col(SUMS_, 4),
                         -1.0 / D, None, ALU.mult,
                         dep=max(dve_dep, ns.v_lnlast))
                vb = vop("tensor_scalar", col(T1, 4), col(SSQ_, 4),
                         1.0 / D, None, ALU.mult, dep=va)
                vc = vop("tensor_tensor", col(T2, 4), col(NEGM_, 4),
                         col(NEGM_, 4), ALU.mult, dep=va)
                vd = vop("tensor_tensor", col(T1, 4), col(T1, 4), col(T2, 4),
                         ALU.subtract, dep=vc)
                ve = vop("tensor_scalar", col(T1, 4), col(T1, 4), 1e-5, None,
                         ALU.add, dep=vd)
                return ve

            def ln_stats_b(RSTD_, ve):
                wait("act", "dve", ve)
                vf = aop("activation", col(T2, 4), col(T1, 4), AF.Ln, 0.0, 1.0)
                vg = aop("activation", col(T3, 4), col(T2, 4), AF.Exp, 0.0, -0.5,
                         dep=vf)
                wait("dve", "act", vg)
                vh = vop("tensor_tensor", col(T2, 4), col(T3, 4), col(T3, 4),
                         ALU.mult, dep=ve)
                vi = vop("tensor_tensor", col(T2, 4), col(T2, 4), col(T1, 4),
                         ALU.mult, dep=vh)
                vj = vop("tensor_scalar", col(T2, 4), col(T2, 4), -0.5, 1.5,
                         ALU.mult, ALU.add, dep=vi)
                ns.v_lnlast = vop("tensor_tensor", col(RSTD_, 4), col(T3, 4),
                                  col(T2, 4), ALU.mult, dep=vj)
                return ns.v_lnlast

            def ln_stats(SUMS_, SSQ_, RSTD_, NEGM_, dve_dep):
                return ln_stats_b(RSTD_, ln_stats_a(SUMS_, SSQ_, NEGM_, dve_dep))

            def k64phys(i, fh):
                return 3 + (fh % 2) if i == 3 else i

            def xts_load(fh):
                wait("sync", "pe", ns.mark_proj)
                wait("act", "pe", ns.mark_proj)
                wait_full("act", "gbf")
                wait_full("act", "gb2")
                for dt in range(8):
                    if dt % 2 == 0:
                        ns.m_xts = tdma("xts", xts[:, dt, :],
                                        xbf_h[S * fh:S * (fh + 1),
                                              128 * dt:128 * (dt + 1)])
                    else:
                        ns.m_xt2 = tdma("xt2", xts[:, dt, :],
                                        xbf_h[S * fh:S * (fh + 1),
                                              128 * dt:128 * (dt + 1)])

            def proj_group(dest_ap, kind, pt, ekey=None, force_sel=None):
                if force_sel is None:
                    sel = ns.proj_sel
                    ns.proj_sel ^= 1
                else:
                    sel = force_sel
                psel = ps_d if sel == 0 else ps_e
                wait("pe", "xts", ns.m_xts)
                wait("pe", "xt2", ns.m_xt2)
                wait("pe", "dve", prev_evac[sel])
                wait("pe", "dve", ns.v_p2)
                wait("pe", "dve", ns.gfix[sel])
                if kind == "q":
                    for dt in range(8):
                        mm(psel[:, :], mq_bf[:, dt, 128 * pt:128 * (pt + 1)],
                           xts[:, dt, :], start=(dt == 0), stop=(dt == 7))
                elif kind == "k":
                    for dt in range(8):
                        mm(psel[:, :], mk_bf[:, dt, 128 * pt:128 * (pt + 1)],
                           xts[:, dt, :], start=(dt == 0), stop=(dt == 7))
                else:
                    c0 = 64 + 128 * pt
                    if pt < 3:
                        for dt in range(8):
                            mm(psel[:, :], mk_bf[:, dt, c0:c0 + 128],
                               xts[:, dt, :], start=(dt == 0), stop=(dt == 7))
                    else:
                        for dt in range(8):
                            mm(psel[0:64, :], mk_bf[:, dt, 448:512],
                               xts[:, dt, :], start=(dt == 0), stop=(dt == 7))
                        for dt in range(8):
                            mm(psel[64:128, :], mk_bf[:, dt, 0:64],
                               xts[:, dt, :], start=(dt == 0), stop=(dt == 7),
                               tile_position=(0, 64))
                v = sig("pe")
                prev_mm[sel] = v
                wait("dve", "pe", v)
                prev_evac[sel] = vop("tensor_copy", dest_ap, psel[:, :])
                ns.v_q_evac = prev_evac[sel]
                if ekey is not None:
                    ns.evacs[ekey] = prev_evac[sel]

            def proj_sec0(fh):
                for pt in range(4):
                    proj_group(qs[:, fh % 2, pt, :], "q", pt, ("q", fh % 2, pt))
                proj_group(ks[:, 0, :], "k", 0, ("k", 0))
                proj_group(ks64[:, k64phys(3, fh), :], "k64", 3,
                           ("k64", k64phys(3, fh)))
                proj_group(ks64[:, 0, :], "k64", 0, ("k64", 0))

            def gate_chunk(f, tt):
                if tt == 0:
                    wait("pe", "x1t", ns.m_x1T)
                    wait("pe", "dve", ns.v_x1T)
                    wait("pe", "dve", prev_evac[0])
                    wait("pe", "dve", prev_evac[1])
                    wait("pe", "dve", ns.v_p2)
                else:
                    wait("pe", "dve", ns.gfix[tt % 2])
                    wait("pe", "dve", prev_evac[tt % 2])
                psel = ps_d if tt % 2 == 0 else ps_e
                co = 8 * (tt // 2)
                for dt in range(8):
                    mm(psel[:, co:co + 7],
                       x1T[:, dt, 128 * tt:128 * (tt + 1)],
                       wg_bf[:, dt, :], start=(dt == 0), stop=(dt == 7))
                ns.mark_gate[tt] = sig("pe")

            def gate_fix_all(f):
                for tt in range(4):
                    psel = ps_d if tt % 2 == 0 else ps_e
                    co = 8 * (tt // 2)
                    if tt == 0:
                        wait("dve", "pe", ns.mark_gate[3])
                        wait("dve", "act", ns.v_gate_exp)
                    va = vop("tensor_scalar", gate_t[:, :], psel[:, co:co + 7],
                             col(RSTD + tt), None, ALU.mult,
                             dep=ns.v_x1 if tt == 0 else ns.v_gate_fix)
                    ns.v_gate_fix = vop("scalar_tensor_tensor",
                                        gate_p4[:, tt, :], gate_c[:, :],
                                        col(Q + tt), gate_t[:, :],
                                        ALU.mult, ALU.add, dep=va)
                    ns.gfix[tt % 2] = ns.v_gate_fix
                wait("act", "dve", ns.v_gate_fix)
                for tt in range(4):
                    ns.v_gate_exp = aop("activation", gate_e4[:, tt, :],
                                        gate_p4[:, tt, :], AF.Exp, 0.0, 1.0)

            def ffn_sh(f, tt, jc):
                if tt == 0 and jc == 0:
                    wait_full("pe", "wT")
                    wait("pe", "dve", ns.gfix[0])
                wait("pe", "dve", prev_evac[0])
                wait("pe", "dve", ns.v_p2)
                for dt in range(8):
                    mm(ps_d[:, :], x1T[:, dt, 128 * tt:128 * (tt + 1)],
                       wshT[:, dt, 512 * jc:512 * (jc + 1)],
                       start=(dt == 0), stop=False)
                for cb in range(4):
                    mm(ps_d[:, 128 * cb:128 * (cb + 1)],
                       x1T[:, 4 * jc + cb, 128 * tt:128 * (tt + 1)],
                       ident[:, 0, :], start=False, stop=(cb == 3))

            def ffn_ex(f, tt, jc):
                if tt == 0 and jc == 0:
                    wait("pe", "dve", ns.gfix[1])
                    wait("pe", "dve", prev_evac[1])
                for dt in range(8):
                    mm(ps_e[:, :], x1T[:, dt, 128 * tt:128 * (tt + 1)],
                       wexT[:, dt, 512 * jc:512 * (jc + 1)],
                       start=(dt == 0), stop=(dt == 7))
                ns.mark_ffn = sig("pe")
                hs = slice(512 * jc, 512 * (jc + 1))
                wait("dve", "pe", ns.mark_ffn)
                if tt == 0 and jc == 0:
                    wait_full("dve", "out")
                if jc == 0:
                    vq = vop("tensor_tensor", col(QW + tt), col(Q + tt),
                             col(WSUM + tt), ALU.mult,
                             dep=max(ns.v_gatepost, ns.v_x1))
                    vq = vop("tensor_tensor", col(RW + tt), col(RSTD + tt),
                             col(WSUM + tt), ALU.mult, dep=vq)
                else:
                    vq = 0
                c1 = vop("scalar_tensor_tensor", r_buf[:, tt, hs],
                         csh_bc[:, hs], col(Q + tt), bias_bc[:, hs],
                         ALU.mult, ALU.add, dep=vq)
                c2 = vop("scalar_tensor_tensor", r_buf[:, tt, hs],
                         cex_bc[:, hs], col(QW + tt), r_buf[:, tt, hs],
                         ALU.mult, ALU.add, dep=c1)
                v1 = vop("scalar_tensor_tensor", xres[:, tt, hs],
                         ps_d[:, :], col(RSTD + tt), r_buf[:, tt, hs],
                         ALU.mult, ALU.add, dep=c2)
                ns.v_p2 = vop("scalar_tensor_tensor", xres[:, tt, hs],
                              ps_e[:, :], col(RW + tt), xres[:, tt, hs],
                              ALU.mult, ALU.add,
                              accum_out=col((SUMS2 if jc == 0 else SUMS2B) + tt),
                              dep=v1)
                if jc == 1:
                    ns.v_out2[tt] = ns.v_p2
                    ns.v_sq2 = vop("scalar_tensor_tensor", atok[:, 0, :],
                                   xres[:, tt, :], 1.0, xres[:, tt, :],
                                   ALU.mult, ALU.mult, accum_out=col(SSQ2 + tt),
                                   dep=ns.v_p2)

            def c_piece(i):
                onesb = atok[:, 0, 0:1]
                crow = r_buf[0:1, 0, 0:512]
                ones_row = r_buf[0:1, 1, 0:128]
                if i == 0:
                    wait_full("dve", "xvw")
                    v1 = vop("memset", onesb, 1.0)
                    ns.v_ccp = vop("memset", ones_row, 1.0, dep=v1)
                    wait("pe", "dve", prev_evac[0])
                    wait("pe", "dve", prev_evac[1])
                    wait_full("pe", "wT")
                if i < 4:
                    w_sb = wshT if i < 2 else wexT
                    c_dst = csh_bc if i < 2 else cex_bc
                    jc = i % 2
                    wait("pe", "dve", ns.v_ccp)
                    for dt in range(8):
                        mm(ps_d[0:1, :], onesb,
                           w_sb[:, dt, 512 * jc:512 * (jc + 1)],
                           start=(dt == 0), stop=(dt == 7))
                    v = sig("pe")
                    wait("dve", "pe", v)
                    ns.v_ccp = vop("tensor_copy", crow, ps_d[0:1, :])
                    wait("pe", "dve", ns.v_ccp)
                    mm(ps_e[:, :], ones_row, crow, start=True, stop=True)
                    v = sig("pe")
                    wait("dve", "pe", v)
                    ns.v_ccp = vop("tensor_copy",
                                   c_dst[:, 512 * jc:512 * (jc + 1)], ps_e[:, :])
                    if i < 2:
                        ns.v_ccp = vop("tensor_scalar",
                                       c_dst[:, 512 * jc:512 * (jc + 1)],
                                       c_dst[:, 512 * jc:512 * (jc + 1)],
                                       1.0, None, ALU.add, dep=ns.v_ccp)
                else:
                    wait("pe", "dve", ns.v_ccp)
                    for dt in range(8):
                        mm(ps_d[0:1, 0:7], onesb, wg_bf[:, dt, :],
                           start=(dt == 0), stop=(dt == 7))
                    v = sig("pe")
                    wait("dve", "pe", v)
                    ns.v_ccp = vop("tensor_copy", crow[:, 0:7], ps_d[0:1, 0:7])
                    wait("pe", "dve", ns.v_ccp)
                    mm(ps_e[:, 0:7], ones_row, crow[:, 0:7],
                       start=True, stop=True)
                    v = sig("pe")
                    wait("dve", "pe", v)
                    ns.v_ccp = vop("tensor_copy", gate_c[:, :], ps_e[:, 0:7])
                    prev_evac[0] = ns.v_ccp
                    prev_evac[1] = ns.v_ccp

            def seg_for(f, g2):
                segs = [None] * 5
                if f == 0 and g2 == 4:
                    for i in range(4):
                        segs[i] = (lambda j: lambda: c_piece(j))(i)
                    return segs
                if f == 0 and g2 == 5:
                    segs[0] = lambda: c_piece(4)
                    return segs
                if g2 == 0:
                    segs[0] = lambda: proj_group(ks[:, 1, :], "k", 1, ("k", 1))
                    segs[1] = lambda: proj_group(ks[:, 2, :], "k", 2, ("k", 2))
                    segs[2] = lambda: proj_group(ks[:, 3, :], "k", 3, ("k", 3))
                    segs[3] = lambda: proj_group(ks64[:, 1, :], "k64", 1,
                                                 ("k64", 1))

                    def c4():
                        proj_group(ks64[:, 2, :], "k64", 2, ("k64", 2))
                        ns.mark_proj = max(prev_mm[0], prev_mm[1])
                        if f < 7:
                            xts_load(f + 1)
                    segs[4] = c4
                elif g2 == 1 and f < 7:
                    pi = (f + 1) % 2
                    segs[2] = lambda: proj_group(qs[:, pi, 0, :], "q", 0,
                                                 ("q", pi, 0))
                    segs[3] = lambda: proj_group(qs[:, pi, 1, :], "q", 1,
                                                 ("q", pi, 1))
                elif g2 == 2 and f < 7:
                    pi = (f + 1) % 2
                    segs[0] = lambda: proj_group(qs[:, pi, 2, :], "q", 2,
                                                 ("q", pi, 2))
                    segs[1] = lambda: proj_group(qs[:, pi, 3, :], "q", 3,
                                                 ("q", pi, 3))
                    segs[2] = lambda: proj_group(ks[:, 0, :], "k", 0, ("k", 0))
                    segs[3] = lambda: proj_group(
                        ks64[:, k64phys(3, f + 1), :], "k64", 3,
                        ("k64", k64phys(3, f + 1)))
                elif g2 == 3 and f >= 1:
                    for i in range(4):
                        segs[i] = (lambda tt: lambda: gate_chunk(
                            f - 1, tt))(i)
                elif g2 >= 4:
                    if f >= 1:
                        tt = g2 - 4
                        segs[0] = lambda: ffn_sh(f - 1, tt, 0)
                        segs[1] = lambda: ffn_ex(f - 1, tt, 0)
                        segs[2] = lambda: ffn_sh(f - 1, tt, 1)
                        segs[3] = lambda: ffn_ex(f - 1, tt, 1)
                    if g2 == 4 and f < 7:
                        segs[4] = lambda: proj_group(ks64[:, 0, :], "k64", 0,
                                                     ("k64", 0), force_sel=0)
                return segs

            def epilogue1(f):
                wait("gp", "dve", max(ns.v_avadd[0], ns.v_avadd[1]))
                for hp in range(8):
                    ns.m_cast = dma("gp", "cast", att_h[128 * hp:128 * (hp + 1), :],
                                    att_acc[:, hp, :])
                epilogue1b(f)

            def epilogue1b(f):
                wait_full("sync", "cast")
                wait("sync", "dve", ns.v_sq2)
                for tt in range(4):
                    ns.m_atok = tdma("atok", atok[:, tt, :],
                                     att_h[:, 128 * tt:128 * (tt + 1)])

            def ep2a(f, half):
                tts = (0, 1) if half == 0 else (2, 3)
                for tt in tts:
                    if tt == 0:
                        wait_full("dve", "out")
                        wait("dve", "atok", ns.m_atok)
                        wait("dve", "xres", ns.m_xres)
                    ns.v_resid = vop("scalar_tensor_tensor", r_buf[:, tt, :],
                                     atok[:, tt, :], 1.0, xres[:, tt, :],
                                     ALU.mult, ALU.add, accum_out=col(SUMS + tt),
                                     dep=ns.v_out2[3] if tt == 0 else 0)
                v_sq = max(ns.v_resid, ns.v_sq1)
                for tt in tts:
                    v_sq = vop("scalar_tensor_tensor", atok[:, 0, :],
                               r_buf[:, tt, :], 1.0, r_buf[:, tt, :],
                               ALU.mult, ALU.mult, accum_out=col(SSQ + tt),
                               dep=v_sq)
                ns.v_sq1 = v_sq
                if half == 1:
                    ns.v_ve1 = ln_stats_a(SUMS, SSQ, NEGM, v_sq)

            def ep2b_act(f):
                wait("act", "dve", ns.v_ve1)
                vf = aop("activation", col(T2, 4), col(T1, 4), AF.Ln, 0.0, 1.0)
                ns.v_vg1 = aop("activation", col(T3, 4), col(T2, 4), AF.Exp,
                               0.0, -0.5, dep=vf)

            def ep2b(f):
                wait("dve", "act", ns.v_vg1)
                vh = vop("tensor_tensor", col(T2, 4), col(T3, 4), col(T3, 4),
                         ALU.mult, dep=ns.v_ve1)
                vi = vop("tensor_tensor", col(T2, 4), col(T2, 4), col(T1, 4),
                         ALU.mult, dep=vh)
                vj = vop("tensor_scalar", col(T2, 4), col(T2, 4), -0.5, 1.5,
                         ALU.mult, ALU.add, dep=vi)
                ns.v_lnlast = vop("tensor_tensor", col(RSTD, 4), col(T3, 4),
                                  col(T2, 4), ALU.mult, dep=vj)
                ns.v_ln1 = ns.v_lnlast
                ns.v_x1 = vop("tensor_tensor", col(Q, 4), col(RSTD, 4),
                              col(NEGM, 4), ALU.mult, dep=ns.v_ln1)

            def rt_issue(f):
                wait("sync", "pe", ns.mark_ffn)
                for dt in range(8):
                    ns.m_x1T = tdma("x1t", x1T[:, dt, :],
                                    xbf_h[S * f:S * (f + 1),
                                          128 * dt:128 * (dt + 1)])

            def rt_add(f):
                wait("dve", "x1t", ns.m_x1T)
                v = 0
                for dt in range(8):
                    v = vop("tensor_tensor", x1T[:, dt, :], x1T[:, dt, :],
                            att_acc[:, dt, :], ALU.add,
                            dep=max(ns.v_avadd[0], ns.v_avadd[1])
                            if dt == 0 else 0)
                ns.v_x1T = v

            def gate_post(f, tt):
                wait("dve", "act", ns.v_gate_exp)
                vc = vop("reduce_max", col(G_M1N), gate_e4[:, tt, :], AX.X,
                         negate=True, dep=ns.v_gatepost)
                vd = vop("tensor_scalar", gate_t[:, :], gate_e4[:, tt, :],
                         col(G_M1N), None, ALU.add, dep=vc)
                ve = vop("tensor_scalar", gate_t[:, :], gate_t[:, :], 0.0,
                         None, ALU.is_equal, dep=vd)
                vf = vop("scalar_tensor_tensor", gate_t[:, :], gate_t[:, :],
                         -1e30, gate_e4[:, tt, :], ALU.mult, ALU.add, dep=ve)
                vg = vop("reduce_max", col(G_M2N), gate_t[:, :], AX.X,
                         negate=True, dep=vf)
                vh = vop("tensor_tensor", col(G_S), col(G_M1N), col(G_M2N),
                         ALU.add, dep=vg)
                vi = vop("reciprocal", col(G_S), col(G_S), dep=vh)
                vj = vop("tensor_tensor", col(G_A), col(G_M1N), col(G_S),
                         ALU.mult, dep=vi)
                vk = vop("tensor_tensor", col(G_B), col(G_M2N), col(G_S),
                         ALU.mult, dep=vj)
                ns.v_gatepost = vop("tensor_tensor", col(WSUM + tt), col(G_A),
                                    col(G_B), ALU.add, dep=vk)

            def ln2a(f):
                vs2 = vop("tensor_tensor", col(SUMS2, 4), col(SUMS2, 4),
                          col(SUMS2B, 4), ALU.add,
                          dep=max(ns.v_sq2, ns.v_lnlast))
                ns.v_ve2 = ln_stats_a(SUMS2, SSQ2, NEGM2, vs2)

            def ln2b(f):
                v_ln2 = ln_stats_b(RSTD2, ns.v_ve2)
                v_final = 0
                for tt in range(4):
                    v_final = vop("tensor_scalar", r_buf[:, tt, :], xres[:, tt, :],
                                  col(NEGM2 + tt), col(RSTD2 + tt), ALU.add,
                                  ALU.mult, dep=v_ln2 if tt == 0 else 0)
                wait("sync", "dve", v_final)
                dma("sync", "out", out_h[S * f:S * (f + 1), :]
                    .rearrange("(t p) d -> p t d", p=128), r_buf[:, :, :])
                if f + 1 < 8:
                    ns.m_xres = dma("sync", "xres", xres[:, :, :],
                                    x_h[S * (f + 1):S * (f + 2), :]
                                    .rearrange("(t p) d -> p t d", p=128))

            def av_pair(f, g2p, jj):
                a = f * 8 + g2p
                for hp in (2 * jj, 2 * jj + 1):
                    half = hp % 2
                    slot = (8 * g2p + hp) % 10
                    wait("pe", "dve", ns.v_scale)
                    wait("pe", "dve", ns.v_avadd[half])
                    for c2t in range(4):
                        mm(ps_av[:, 512 * half:512 * (half + 1)],
                           xv_sl[:, 4 * hp + c2t, :], A_buf[:, c2t, slot, :],
                           start=(c2t == 0), stop=(c2t == 3))
                    v_g = sig("pe")
                    ns.mark_av_hp[hp] = v_g
                    ns.mark_av_all = v_g
                    wait("dve", "pe", v_g)
                    if g2p == 0:
                        wait_full("dve", "cast")
                        ns.v_avadd[half] = vop(
                            "tensor_copy", att_acc[:, hp, :],
                            ps_av[:, 512 * half:512 * (half + 1)],
                            dep=ns.v_x1T)
                    else:
                        ns.v_avadd[half] = vop(
                            "tensor_tensor", att_acc[:, hp, :], att_acc[:, hp, :],
                            ps_av[:, 512 * half:512 * (half + 1)], ALU.add,
                            dep=ns.v_avadd[half])
                ns.avsig[(a, jj)] = ns.mark_av_all

            def s_g2(f, g2, segs):
                wait("gp", "dve", ns.v_scale)
                if f == 0 and g2 == 0:
                    wait_full("gp", "xvw")
                m_xv = dma("gp", "xvr", xv_raw[:, :, :],
                           xv_h[g2, :, :].rearrange("(t p) v -> p t v", p=128))
                gm = (g2 - 1) % 8
                pi = f % 2
                if g2 % 2 == 0:
                    lhsT0s = ks[0:64, g2 // 2, :]
                    lhsT64s = ks64[64:128, k64phys(gm // 2, f), :]
                else:
                    lhsT0s = ks64[0:64, k64phys(gm // 2, f), :]
                    lhsT64s = ks[64:128, g2 // 2, :]
                ns.ev_lhsT = max(ns.evacs.get(("k", g2 // 2), 0),
                                 ns.evacs.get(("k64", k64phys(gm // 2, f)), 0))
                sub_i = 0
                a_cur = f * 8 + g2
                par = g2 % 2
                if g2 >= 1:
                    wave = (f, g2 - 1)
                    pair_order = (0, 1, 2, 3)
                    war_keys = ((a_cur - 2, 3), (a_cur - 1, 0),
                                (a_cur - 1, 1), (a_cur - 1, 2))
                elif f >= 1:
                    wave = (f - 1, 7)
                    pair_order = (2, 3, 0, 1)
                    war_keys = ((a_cur - 1, 2), (a_cur - 1, 3),
                                (a_cur - 2, 3), (a_cur - 1, 0))
                else:
                    wave = None
                    war_keys = ((-1, 0),) * 4
                def do_av(p):
                    av_pair(wave[0], wave[1], p)
                    if wave[1] == 7:
                        for hp in (2 * p, 2 * p + 1):
                            wait("gp", "dve", ns.v_avadd[hp % 2])
                            ns.m_cast = dma("gp", "cast",
                                            att_h[128 * hp:128 * (hp + 1), :],
                                            att_acc[:, hp, :])

                for jj in range(4):
                    av_late = not (g2 == 0 and jj < 2)
                    if wave is not None and not av_late:
                        do_av(pair_order[jj])
                    sg = (8 * g2 + 2 * jj) % 10
                    v_war = ns.avsig.get(war_keys[jj], 0)
                    for c2t in range(4):
                        cs = slice(128 * c2t, 128 * (c2t + 1))
                        sub = sub_i % 2
                        sub_i += 1
                        wait("pe", "act", ns.v_exp_slot[sub])
                        if c2t == 0:
                            if jj == 0:
                                wait("pe", "dve", ns.ev_lhsT)
                            wait("pe", "dve",
                                 ns.evacs.get(("q", pi, jj), 0))
                        b0 = 1024 * sub
                        mm(ps_a[:, b0:b0 + 512],
                           lhsT0s[:, cs], qs[0:64, pi, jj, :],
                           start=True, stop=True, tile_position=(0, 0))
                        mm(ps_a[:, b0 + 512:b0 + 1024],
                           lhsT64s[:, cs], qs[64:128, pi, jj, :],
                           start=True, stop=True, tile_position=(64, 0))
                        v_slot = sig("pe")
                        wait("act", "pe", v_slot)
                        if c2t == 0:
                            wait("act", "pe", v_war)
                            if jj == 0:
                                wait("act", "dve", ns.v_zfin_par[par])
                        ns.v_exp = aop("activation",
                                       A_buf[:, c2t, sg:sg + 2, :],
                                       ps_a[:, b0:b0 + 1024], AF.Exp, 0.0, 0.125,
                                       accum_out=zacc[:, c2t, jj, par:par + 1])
                        ns.v_exp_slot[sub] = ns.v_exp
                    if wave is not None and av_late:
                        do_av(pair_order[jj])
                    if jj == 3:
                        wait("dve", "act", ns.v_exp)
                        v1 = vop("tensor_tensor", zinv[:, :], zacc[:, :, 0, par],
                                 zacc[:, :, 1, par], ALU.add, dep=ns.v_scale)
                        v1 = vop("tensor_tensor", col(60, 4), zacc[:, :, 2, par],
                                 zacc[:, :, 3, par], ALU.add, dep=v1)
                        v1 = vop("tensor_tensor", zinv[:, :], zinv[:, :],
                                 col(60, 4), ALU.add, dep=v1)
                        ns.v_zfin_par[par] = vop("reciprocal", zinv[:, :],
                                                 zinv[:, :], dep=v1)
                        wait("dve", "xvr", m_xv)
                        wait("dve", "pe", ns.mark_av_all)
                        xvr4 = xv_raw[:, :, :].rearrange("p (a b) v -> p b a v", b=4)
                        xvs4 = xv_sl[:, :, :].rearrange("p (a b) v -> p b a v", b=4)
                        for c2t in range(4):
                            ns.v_scale = vop(
                                "tensor_scalar", xvs4[:, c2t, :, :],
                                xvr4[:, c2t, :, :], zinv[:, c2t:c2t + 1],
                                None, ALU.mult,
                                dep=ns.v_zfin_par[par] if c2t == 0 else 0)
                    if segs[jj] is not None:
                        segs[jj]()
                if segs[4] is not None:
                    segs[4]()

            ns.m_xres = dma("sync", "xres", xres[:, :, :],
                            x_h[0:S, :].rearrange("(t p) d -> p t d", p=128))
            xts_load(0)
            wait("pe", "dve", v_wcp)
            proj_sec0(0)
            for f in range(8):
                for g2 in range(8):
                    s_g2(f, g2, seg_for(f, g2))
                    if f == 0 and g2 == 2:
                        wait_full("sync", "gw")
                        for dt in range(8):
                            tdma("wT", wshT[:, dt, :],
                                 wshbf_h[:, 128 * dt:128 * (dt + 1)])
                        for dt in range(8):
                            tdma("wT", wexT[:, dt, :],
                                 wexbf_h[:, 128 * dt:128 * (dt + 1)])
                    if g2 == 7:
                        if f >= 1:
                            ln2a(f - 1)
                        rt_issue(f)
                    elif f >= 1:
                        if g2 == 0:
                            rt_add(f - 1)
                            if f >= 2:
                                ln2b(f - 2)
                        elif g2 == 1:
                            epilogue1b(f - 1)
                            ep2a(f - 1, 0)
                        elif g2 == 2:
                            ep2a(f - 1, 1)
                            ep2b_act(f - 1)
                        elif 3 <= g2 <= 6:
                            if g2 == 3:
                                ep2b(f - 1)
                                gate_fix_all(f - 1)
                            gate_post(f - 1, g2 - 3)
            ln2b(6)
            for p in range(4):
                av_pair(7, 7, p)
            epilogue1(7)
            rt_add(7)
            ep2a(7, 0)
            ep2a(7, 1)
            ep2b_act(7)
            ep2b(7)
            for tt in range(4):
                gate_chunk(7, tt)
            gate_fix_all(7)
            for tt in range(4):
                gate_post(7, tt)
                ffn_sh(7, tt, 0)
                ffn_ex(7, tt, 0)
                ffn_sh(7, tt, 1)
                ffn_ex(7, tt, 1)
            ln2a(7)
            ln2b(7)

            if taps:
                wait_full("sync", "out")
                dma("sync", "out", taps["o_xv"][:, :, :], xv_h[:, :, :])
                dma("sync", "out", taps["o_attacc"][:, :],
                    att_acc[:, :, :].rearrange("p a b -> p (a b)"))
                dma("sync", "out", taps["o_A"][:, :],
                    A_buf[:, :, :, :].rearrange("p a b c -> p (a b c)"))
                dma("sync", "out", taps["o_zinv"][:, :], zinv[:, :])
                dma("sync", "out", taps["o_qs"][:, :],
                    qs[:, :, :, :].rearrange("p a b c -> p (a b c)"))
                dma("sync", "out", taps["o_ks"][:, :],
                    ks[:, :, :].rearrange("p a b -> p (a b)"))
                dma("sync", "out", taps["o_q64"][:, :],
                    ks64[:, :, :].rearrange("p a b -> p (a b)"))
                dma("sync", "out", taps["o_stat"][:, :], stat[:, :])
                dma("sync", "out", taps["o_xbf"][:, :], xbf_h[:, :])

            wait_full("sync", "out")
            wait_full("gp", "out")

        block.sync(lambda e: prog("sync", e))
        block.gpsimd(lambda e: prog("gp", e))
        block.tensor(lambda e: prog("pe", e))
        block.scalar(lambda e: prog("act", e))
        block.vector(lambda e: prog("dve", e))

        nc.all_engine_barrier()
    return nc


_NC_CACHE = None


def kernel(L0_attn_x, L0_attn_Mq, L0_attn_Mk, L0_attn_Mv, L0_ffn_cond=None,
           L0_ffn_mask=None, L0_ffn_W_shared=None, L0_ffn_W_expert=None,
           L0_ffn_W_gate=None, L0_ffn_bias=None, **_unused):
    global _NC_CACHE
    x = np.ascontiguousarray(np.asarray(L0_attn_x, dtype=np.float32))
    mq = np.ascontiguousarray(np.asarray(L0_attn_Mq, dtype=np.float32))
    mk = np.ascontiguousarray(np.asarray(L0_attn_Mk, dtype=np.float32))
    mv = np.ascontiguousarray(np.asarray(L0_attn_Mv, dtype=np.float32))
    wsh = np.ascontiguousarray(np.asarray(L0_ffn_W_shared, dtype=np.float32))
    wex = np.ascontiguousarray(np.asarray(L0_ffn_W_expert, dtype=np.float32))
    wg = np.ascontiguousarray(np.asarray(L0_ffn_W_gate, dtype=np.float32))
    bias = np.ascontiguousarray(np.asarray(L0_ffn_bias, dtype=np.float32).reshape(1, D))

    if _NC_CACHE is None:
        _NC_CACHE = build_program()
    nc = _NC_CACHE
    in_maps = [{"x": x[b], "mq": mq, "mk": mk, "mv": mv, "wsh": wsh,
                "wex": wex, "wg": wg, "bias": bias} for b in range(NCORES)]
    res = run_bass_kernel_spmd(nc, in_maps, core_ids=list(range(NCORES)))
    out = np.stack([np.asarray(res.results[b]["out"]) for b in range(NCORES)], axis=0)
    return out.astype(np.float32)


# version 37
# speedup vs baseline: 1.0769x; 1.0769x over previous
import numpy as np
import concourse.bass as bass
import concourse.mybir as mybir
from concourse.bass_utils import run_bass_kernel_spmd

F32 = mybir.dt.float32
BF16 = mybir.dt.bfloat16
AF = mybir.ActivationFunctionType
ALU = mybir.AluOpType
AX = mybir.AxisListType

C, D, H, S, KD, V = 4096, 1024, 8, 512, 64, 128
NCORES = 8

DMA_SEMS = ("ld", "gbf", "gb2", "gw", "wT", "xts", "xt2", "xvw", "xvr",
            "cast", "xres", "atok", "x1w", "x1t", "out")
ENG_SEMS = ("pe", "act", "dve", "gpc")


def build_program(debug_taps=False):
    nc = bass.Bass("TRN2", target_bir_lowering=False, dynamic_dma_scratch_size=16384)

    x_h = nc.dram_tensor("x", [C, D], F32, kind="ExternalInput").ap()
    mq_h = nc.dram_tensor("mq", [D, 512], F32, kind="ExternalInput").ap()
    mk_h = nc.dram_tensor("mk", [D, 512], F32, kind="ExternalInput").ap()
    mv_h = nc.dram_tensor("mv", [D, D], F32, kind="ExternalInput").ap()
    wsh_h = nc.dram_tensor("wsh", [D, D], F32, kind="ExternalInput").ap()
    wex_h = nc.dram_tensor("wex", [D, D], F32, kind="ExternalInput").ap()
    wg_h = nc.dram_tensor("wg", [D, 7], F32, kind="ExternalInput").ap()
    bias_h = nc.dram_tensor("bias", [1, D], F32, kind="ExternalInput").ap()
    out_h = nc.dram_tensor("out", [C, D], F32, kind="ExternalOutput").ap()
    xbf_h = nc.dram_tensor("xbf", [C, D], BF16, kind="Internal").ap()
    xv_h = nc.dram_tensor("xvh", [H, C, V], BF16, kind="Internal").ap()
    att_h = nc.dram_tensor("atth", [D, S], BF16, kind="Internal").ap()
    x1_h = nc.dram_tensor("x1h", [S, D], BF16, kind="Internal").ap()
    ident_h = nc.dram_tensor("identh", [1, 16640], BF16, kind="Internal").ap()
    wshbf_h = nc.dram_tensor("wshbf", [D, D], BF16, kind="Internal").ap()
    wexbf_h = nc.dram_tensor("wexbf", [D, D], BF16, kind="Internal").ap()
    taps = {}
    if debug_taps:
        taps["o_xv"] = nc.dram_tensor("o_xv", [H, C, V], BF16, kind="ExternalOutput").ap()
        taps["o_attacc"] = nc.dram_tensor("o_attacc", [128, 8 * 512], F32, kind="ExternalOutput").ap()
        taps["o_A"] = nc.dram_tensor("o_A", [128, 4 * 10 * 512], BF16, kind="ExternalOutput").ap()
        taps["o_zinv"] = nc.dram_tensor("o_zinv", [128, 4], F32, kind="ExternalOutput").ap()
        taps["o_qs"] = nc.dram_tensor("o_qs", [128, 2 * 4 * 512], BF16, kind="ExternalOutput").ap()
        taps["o_ks"] = nc.dram_tensor("o_ks", [128, 4 * 512], BF16, kind="ExternalOutput").ap()
        taps["o_q64"] = nc.dram_tensor("o_q64", [128, 5 * 512], BF16, kind="ExternalOutput").ap()
        taps["o_stat"] = nc.dram_tensor("o_stat", [128, 80], F32, kind="ExternalOutput").ap()
        taps["o_xbf"] = nc.dram_tensor("o_xbf", [C, D], BF16, kind="ExternalOutput").ap()

    from contextlib import ExitStack
    with ExitStack() as ctx:
        E = ctx.enter_context
        mq_bf = E(nc.sbuf_tensor("mq_bf", [128, 8, 512], BF16))
        mk_bf = E(nc.sbuf_tensor("mk_bf", [128, 8, 512], BF16))
        wshT = E(nc.sbuf_tensor("wshT", [128, 8, 1024], BF16))
        wexT = E(nc.sbuf_tensor("wexT", [128, 8, 1024], BF16))
        wg_bf = E(nc.sbuf_tensor("wg_bf", [128, 8, 7], BF16))
        bias_bc = E(nc.sbuf_tensor("bias_bc", [128, D], F32))
        A_buf = E(nc.sbuf_tensor("A_buf", [128, 4, 10, 512], BF16))
        qs = E(nc.sbuf_tensor("qs", [128, 2, 4, 512], BF16))
        ks = E(nc.sbuf_tensor("ks", [128, 4, 512], BF16))
        ks64 = E(nc.sbuf_tensor("ks64", [128, 5, 512], BF16))
        xts = E(nc.sbuf_tensor("xts", [128, 8, 512], BF16))
        xv_raw = E(nc.sbuf_tensor("xv_raw", [128, 32, 128], BF16))
        xv_sl = E(nc.sbuf_tensor("xv_sl", [128, 32, 128], BF16))
        att_acc = E(nc.sbuf_tensor("att_acc", [128, 8, 512], F32))
        atok = E(nc.sbuf_tensor("atok", [128, 4, 1024], BF16))
        x1T = E(nc.sbuf_tensor("x1T", [128, 8, 512], BF16))
        xres = E(nc.sbuf_tensor("xres", [128, 4, 1024], F32))
        r_buf = E(nc.sbuf_tensor("r_buf", [128, 4, 1024], F32))
        zacc = E(nc.sbuf_tensor("zacc", [128, 4, 4, 2], F32))
        zinv = E(nc.sbuf_tensor("zinv", [128, 4], F32))
        stat = E(nc.sbuf_tensor("stat", [128, 80], F32))
        gate_t = E(nc.sbuf_tensor("gate_t", [128, 7], F32))
        gate_p4 = E(nc.sbuf_tensor("gate_p4", [128, 4, 7], F32))
        gate_e4 = E(nc.sbuf_tensor("gate_e4", [128, 4, 7], F32))
        gate_c = E(nc.sbuf_tensor("gate_c", [128, 7], F32))
        csh_bc = E(nc.sbuf_tensor("csh_bc", [128, 1024], BF16))
        cex_bc = E(nc.sbuf_tensor("cex_bc", [128, 1024], BF16))
        ident = E(nc.sbuf_tensor("ident", [128, 1, 128], BF16))
        ps_a = E(nc.psum_tensor("ps_a", [128, 2048], F32))
        ps_av = E(nc.psum_tensor("ps_av", [128, 1024], F32))
        ps_d = E(nc.psum_tensor("ps_d", [128, 512], F32))
        ps_e = E(nc.psum_tensor("ps_e", [128, 512], F32))
        sems = {}
        for name in DMA_SEMS + ENG_SEMS:
            sems[name] = E(nc.semaphore("s_" + name))
        block = E(nc.Block())

        stage_xv = atok[:, :, :].rearrange("p t (g v) -> p t g v", g=8)
        st = stat
        SUMS, SSQ, RSTD, NEGM = 0, 4, 8, 12
        SUMS2, SUMS2B, SSQ2, RSTD2, NEGM2, WSUM = 16, 20, 24, 28, 32, 36
        T1, T2, T3 = 40, 44, 48
        G_NM, G_ES, G_RS, G_M1N, G_M2N, G_S, G_A, G_B = 52, 53, 54, 55, 56, 57, 58, 59
        Q, QW, RW = 64, 68, 72

        def col(c, n=1):
            return st[:, c:c + n]

        def prog(me, eng):
            cnt = {k: 0 for k in sems}
            last = {}

            def dma(q, sem, dst, src, **kw):
                cnt[sem] += 16
                if me == q:
                    eng.dma_start(dst, src, **kw).then_inc(sems[sem], 16)
                return cnt[sem]

            def tdma(sem, dst, src, q="sync"):
                cnt[sem] += 16
                if me == q:
                    eng.dma_start(dst, src, transpose=True).then_inc(sems[sem], 16)
                return cnt[sem]

            def wait(who, semname, val):
                if val > 0 and me == who:
                    eng.wait_ge(sems[semname], val)

            def wait_full(who, semname):
                wait(who, semname, cnt[semname])

            def op(engkey, fn, *a, **kw):
                if me == engkey:
                    last[engkey] = getattr(eng, fn)(*a, **kw)

            def sig(engkey):
                cnt[engkey] += 1
                if me == engkey:
                    last[engkey].then_inc(sems[engkey], 1)
                return cnt[engkey]

            def vop(fn, *a, dep=0, **kw):
                wait("dve", "dve", dep)
                op("dve", fn, *a, **kw)
                return sig("dve")

            def gop(fn, *a, dep=0, **kw):
                cnt["gpc"] += 1
                if me == "gp":
                    if dep:
                        eng.wait_ge(sems["gpc"], dep)
                    getattr(eng, fn)(*a, **kw).then_inc(sems["gpc"], 1)
                return cnt["gpc"]

            def aop(fn, *a, dep=0, serial=True, **kw):
                if serial:
                    wait("act", "act", cnt["act"])
                op("act", fn, *a, **kw)
                return sig("act")

            def vbar():
                wait("dve", "dve", cnt["dve"])

            def mm(out, lhsT, rhs, **kw):
                op("pe", "matmul", out, lhsT, rhs, **kw)

            m_bias = dma("sync", "ld", bias_bc[0:1, :], bias_h[:, :])
            vop("memset", stat[:, :], 0.0, dep=0)

            for ch in range(8):
                dma("gp", "gbf", xbf_h[S * ch:S * (ch + 1), :],
                    x_h[S * ch:S * (ch + 1), :])
            for (w_hbm, wbf_hbm) in ((wsh_h, wshbf_h), (wex_h, wexbf_h)):
                for half in range(2):
                    rows = slice(512 * half, 512 * (half + 1))
                    dma("gp", "gw", wbf_hbm[rows, :], w_hbm[rows, :])

            v_z = vop("memset", atok[:, 0, 0:128], 0.0)
            v_o = vop("memset", atok[0:1, 1, 0:128], 1.0, dep=v_z)
            wait("sync", "dve", v_z)
            dma("sync", "ld", ident_h[0:1, 0:16384]
                .rearrange("o (p c) -> p o c", p=128), atok[:, 0:1, 0:128])
            wait_full("sync", "ld")
            wait("sync", "dve", v_o)
            dma("sync", "ld", ident_h[0:1, 0:16512]
                .rearrange("o (a b) -> o a b", a=128)[0:1, :, 0:1],
                atok[0:1, 1:2, 0:128].rearrange("p a c -> p c a"),
                allow_slow_non_contiguous=True)
            wait_full("sync", "ld")
            dma("sync", "ld", ident[:, :, :],
                ident_h[0:1, 0:16384].rearrange("o (p c) -> p o c", p=128))

            r8 = r_buf[:, :, :].rearrange("p t d -> p (t d)").rearrange(
                "p (a b) -> p a b", a=8)
            mv_view = A_buf[:, :, :, :].rearrange("p a b c -> p (a b c)")[:, 0:8192] \
                .rearrange("p (a b) -> p a b", a=8)
            v_w = 0
            for half in range(2):
                cols = slice(512 * half, 512 * (half + 1))
                m_w = dma("sync", "ld", r8,
                          mv_h.rearrange("(a p) m -> p a m", p=128)[:, :, cols])
                wait("dve", "ld", m_w)
                v_w = vop("tensor_copy", mv_view[:, :, cols], r8, dep=v_w)
                wait("sync", "dve", v_w)

            ones_ap = att_acc[0:1, 0, 0:128]
            v_ones = vop("memset", ones_ap, 1.0)
            wait_full("pe", "ld")
            wait("pe", "dve", v_ones)
            mm(ps_d[:, :], ones_ap, bias_bc[0:1, 0:512], start=True, stop=True)
            sig("pe")
            mm(ps_e[:, :], ones_ap, bias_bc[0:1, 512:1024], start=True, stop=True)
            v_pe = sig("pe")
            wait("dve", "pe", v_pe)
            vop("tensor_copy", bias_bc[:, 0:512], ps_d[:, :])
            v_bias = vop("tensor_copy", bias_bc[:, 512:1024], ps_e[:, :])

            rg = r_buf[:, :, :].rearrange("p t d -> p (t d)")[:, 0:56] \
                .rearrange("p (a b) -> p a b", a=8)
            prev_mm = {0: v_pe, 1: v_pe}
            prev_evac = {0: v_bias, 1: v_bias}
            mark_p1mm = 0
            def mk_simple(w_view, w_sb, stg):
                return (lambda: dma("sync", "ld", stg, w_view),
                        lambda dep: vop("tensor_copy", w_sb, stg, dep=dep))

            wjobs = [
                mk_simple(mq_h.rearrange("(a p) m -> p a m", p=128),
                          mq_bf[:, :, :], r8),
                mk_simple(mk_h.rearrange("(a p) m -> p a m", p=128),
                          mk_bf[:, :, :], r8),
                mk_simple(wg_h.rearrange("(a p) e -> p a e", p=128),
                          wg_bf[:, :, :], rg),
            ]
            v_wcp = v_w
            m_wld = 0
            mark_par = {0: 0, 1: 0}
            for ch in range(8):
                if ch == 0:
                    wait_full("sync", "gbf")
                pbuf = xts if ch % 2 == 0 else x1T
                psem = "xts" if ch % 2 == 0 else "xt2"
                wait("sync", "pe", mark_par[ch % 2])
                for dt in range(8):
                    m_xts = tdma(psem, pbuf[:, dt, :],
                                 xbf_h[S * ch:S * (ch + 1), 128 * dt:128 * (dt + 1)])
                if 1 <= ch <= len(wjobs):
                    wait("dve", "ld", m_wld)
                    v_wcp = wjobs[ch - 1][1](v_wcp)
                if ch < len(wjobs):
                    wait("sync", "dve", v_wcp)
                    m_wld = wjobs[ch][0]()
                vbar()
                for tt in range(4):
                    for dvc in range(2):
                        sel = (tt * 2 + dvc) % 2
                        psel = ps_d if sel == 0 else ps_e
                        wait("pe", psem, m_xts)
                        wait("pe", "dve", prev_evac[sel])
                        for dt in range(8):
                            mm(psel[:, :], pbuf[:, dt, 128 * tt:128 * (tt + 1)],
                               mv_view[:, dt, 512 * dvc:512 * (dvc + 1)],
                               start=(dt == 0), stop=(dt == 7))
                        prev_mm[sel] = sig("pe")
                        wait("dve", "pe", prev_mm[sel])
                        wait_full("dve", "xvw")
                        prev_evac[sel] = vop(
                            "tensor_copy", stage_xv[:, tt, 4 * dvc:4 * (dvc + 1), :],
                            psel[:, :])
                mark_p1mm = max(prev_mm[0], prev_mm[1])
                mark_par[ch % 2] = mark_p1mm
                v_stage = max(prev_evac[0], prev_evac[1])
                wait("gp", "dve", v_stage)
                for g2 in range(8):
                    dma("gp", "xvw", xv_h[g2, S * ch:S * (ch + 1), :]
                        .rearrange("(t p) v -> p t v", p=128), stage_xv[:, :, g2, :])

            class NS:
                pass
            ns = NS()
            ns.v_scale = 0
            ns.v_zfin = 0
            ns.v_avadd = {0: 0, 1: 0}
            ns.mark_proj = mark_p1mm
            ns.mark_av_all = 0
            ns.mark_av_hp = [0] * 8
            ns.v_exp = 0
            ns.v_exp_slot = {0: 0, 1: 0}
            ns.v_out2 = {0: 0, 1: 0, 2: 0, 3: 0}
            ns.v_gate_exp = 0
            ns.v_gatepost = 0
            ns.v_q_evac = 0
            ns.v_sq2 = 0
            ns.v_p2 = 0
            ns.v_resid = 0
            ns.v_x1 = 0
            ns.v_ln1 = 0
            ns.m_xres = 0
            ns.m_xts = 0
            ns.m_xt2 = 0
            ns.m_atok = 0
            ns.m_x1T = 0
            ns.m_cast = 0
            ns.mark_ffn = mark_p1mm
            ns.mark_gate = {}
            ns.v_lnlast = 0
            ns.avsig = {}
            ns.v_zfin_par = {0: 0, 1: 0}
            ns.proj_sel = 0
            ns.v_ve1 = 0
            ns.v_ve2 = 0
            ns.v_x1T = 0
            ns.v_gate_fix = 0
            ns.v_ccp = 0
            ns.evacs = {}
            ns.ev_lhsT = 0

            def ln_stats_a(SUMS_, SSQ_, NEGM_, dve_dep):
                va = vop("tensor_scalar", col(NEGM_, 4), col(SUMS_, 4),
                         -1.0 / D, None, ALU.mult,
                         dep=max(dve_dep, ns.v_lnlast))
                vb = vop("tensor_scalar", col(T1, 4), col(SSQ_, 4),
                         1.0 / D, None, ALU.mult, dep=va)
                vc = vop("tensor_tensor", col(T2, 4), col(NEGM_, 4),
                         col(NEGM_, 4), ALU.mult, dep=va)
                vd = vop("tensor_tensor", col(T1, 4), col(T1, 4), col(T2, 4),
                         ALU.subtract, dep=vc)
                ve = vop("tensor_scalar", col(T1, 4), col(T1, 4), 1e-5, None,
                         ALU.add, dep=vd)
                return ve

            def ln_stats_b(RSTD_, ve):
                wait("act", "dve", ve)
                vf = aop("activation", col(T2, 4), col(T1, 4), AF.Ln, 0.0, 1.0)
                vg = aop("activation", col(T3, 4), col(T2, 4), AF.Exp, 0.0, -0.5,
                         dep=vf)
                wait("dve", "act", vg)
                vh = vop("tensor_tensor", col(T2, 4), col(T3, 4), col(T3, 4),
                         ALU.mult, dep=ve)
                vi = vop("tensor_tensor", col(T2, 4), col(T2, 4), col(T1, 4),
                         ALU.mult, dep=vh)
                vj = vop("tensor_scalar", col(T2, 4), col(T2, 4), -0.5, 1.5,
                         ALU.mult, ALU.add, dep=vi)
                ns.v_lnlast = vop("tensor_tensor", col(RSTD_, 4), col(T3, 4),
                                  col(T2, 4), ALU.mult, dep=vj)
                return ns.v_lnlast

            def ln_stats(SUMS_, SSQ_, RSTD_, NEGM_, dve_dep):
                return ln_stats_b(RSTD_, ln_stats_a(SUMS_, SSQ_, NEGM_, dve_dep))

            def k64phys(i, fh):
                return 3 + (fh % 2) if i == 3 else i

            def xts_load(fh):
                wait("sync", "pe", ns.mark_proj)
                wait("act", "pe", ns.mark_proj)
                wait_full("act", "gbf")
                wait_full("act", "gb2")
                for dt in range(8):
                    if dt % 2 == 0:
                        ns.m_xts = tdma("xts", xts[:, dt, :],
                                        xbf_h[S * fh:S * (fh + 1),
                                              128 * dt:128 * (dt + 1)])
                    else:
                        ns.m_xt2 = tdma("xt2", xts[:, dt, :],
                                        xbf_h[S * fh:S * (fh + 1),
                                              128 * dt:128 * (dt + 1)])

            def proj_group(dest_ap, kind, pt, ekey=None):
                sel = ns.proj_sel
                ns.proj_sel ^= 1
                psel = ps_d if sel == 0 else ps_e
                wait("pe", "xts", ns.m_xts)
                wait("pe", "xt2", ns.m_xt2)
                wait("pe", "dve", prev_evac[sel])
                wait("pe", "dve", ns.v_p2)
                wait("pe", "dve", ns.v_gate_fix)
                if kind == "q":
                    for dt in range(8):
                        mm(psel[:, :], mq_bf[:, dt, 128 * pt:128 * (pt + 1)],
                           xts[:, dt, :], start=(dt == 0), stop=(dt == 7))
                elif kind == "k":
                    for dt in range(8):
                        mm(psel[:, :], mk_bf[:, dt, 128 * pt:128 * (pt + 1)],
                           xts[:, dt, :], start=(dt == 0), stop=(dt == 7))
                else:
                    c0 = 64 + 128 * pt
                    if pt < 3:
                        for dt in range(8):
                            mm(psel[:, :], mk_bf[:, dt, c0:c0 + 128],
                               xts[:, dt, :], start=(dt == 0), stop=(dt == 7))
                    else:
                        for dt in range(8):
                            mm(psel[0:64, :], mk_bf[:, dt, 448:512],
                               xts[:, dt, :], start=(dt == 0), stop=(dt == 7))
                        for dt in range(8):
                            mm(psel[64:128, :], mk_bf[:, dt, 0:64],
                               xts[:, dt, :], start=(dt == 0), stop=(dt == 7),
                               tile_position=(0, 64))
                v = sig("pe")
                prev_mm[sel] = v
                wait("dve", "pe", v)
                prev_evac[sel] = vop("tensor_copy", dest_ap, psel[:, :])
                ns.v_q_evac = prev_evac[sel]
                if ekey is not None:
                    ns.evacs[ekey] = prev_evac[sel]

            def proj_sec0(fh):
                for pt in range(4):
                    proj_group(qs[:, fh % 2, pt, :], "q", pt, ("q", fh % 2, pt))
                proj_group(ks[:, 0, :], "k", 0, ("k", 0))
                proj_group(ks64[:, k64phys(3, fh), :], "k64", 3,
                           ("k64", k64phys(3, fh)))
                proj_group(ks64[:, 0, :], "k64", 0, ("k64", 0))

            def gate_chunk(f, tt):
                if tt == 0:
                    wait("pe", "x1t", ns.m_x1T)
                    wait("pe", "dve", ns.v_x1T)
                    wait("pe", "dve", prev_evac[0])
                    wait("pe", "dve", prev_evac[1])
                    wait("pe", "dve", ns.v_p2)
                else:
                    wait("pe", "dve", ns.v_gate_fix)
                psel = ps_d if tt % 2 == 0 else ps_e
                co = 8 * (tt // 2)
                for dt in range(8):
                    mm(psel[:, co:co + 7],
                       x1T[:, dt, 128 * tt:128 * (tt + 1)],
                       wg_bf[:, dt, :], start=(dt == 0), stop=(dt == 7))
                ns.mark_gate[tt] = sig("pe")
                wait("dve", "pe", ns.mark_gate[tt])
                if tt == 0:
                    wait("dve", "act", ns.v_gate_exp)
                va = vop("tensor_scalar", gate_t[:, :], psel[:, co:co + 7],
                         col(RSTD + tt), None, ALU.mult,
                         dep=ns.v_x1 if tt == 0 else 0)
                ns.v_gate_fix = vop("scalar_tensor_tensor", gate_p4[:, tt, :],
                                    gate_c[:, :], col(Q + tt), gate_t[:, :],
                                    ALU.mult, ALU.add, dep=va)
                wait("act", "dve", ns.v_gate_fix)
                ns.v_gate_exp = aop("activation", gate_e4[:, tt, :],
                                    gate_p4[:, tt, :], AF.Exp, 0.0, 1.0)

            def ffn_sh(f, tt, jc):
                if tt == 0 and jc == 0:
                    wait_full("pe", "wT")
                    wait("pe", "dve", ns.v_gate_fix)
                    wait("pe", "dve", prev_evac[0])
                wait("pe", "dve", ns.v_p2)
                for dt in range(8):
                    mm(ps_d[:, :], x1T[:, dt, 128 * tt:128 * (tt + 1)],
                       wshT[:, dt, 512 * jc:512 * (jc + 1)],
                       start=(dt == 0), stop=False)
                for cb in range(4):
                    mm(ps_d[:, 128 * cb:128 * (cb + 1)],
                       x1T[:, 4 * jc + cb, 128 * tt:128 * (tt + 1)],
                       ident[:, 0, :], start=False, stop=(cb == 3))

            def ffn_ex(f, tt, jc):
                if tt == 0 and jc == 0:
                    wait("pe", "dve", prev_evac[1])
                for dt in range(8):
                    mm(ps_e[:, :], x1T[:, dt, 128 * tt:128 * (tt + 1)],
                       wexT[:, dt, 512 * jc:512 * (jc + 1)],
                       start=(dt == 0), stop=(dt == 7))
                ns.mark_ffn = sig("pe")
                hs = slice(512 * jc, 512 * (jc + 1))
                wait("dve", "pe", ns.mark_ffn)
                if tt == 0 and jc == 0:
                    wait_full("dve", "out")
                if jc == 0:
                    vq = vop("tensor_tensor", col(QW + tt), col(Q + tt),
                             col(WSUM + tt), ALU.mult,
                             dep=max(ns.v_gatepost, ns.v_x1))
                    vq = vop("tensor_tensor", col(RW + tt), col(RSTD + tt),
                             col(WSUM + tt), ALU.mult, dep=vq)
                else:
                    vq = 0
                c1 = vop("scalar_tensor_tensor", r_buf[:, tt, hs],
                         csh_bc[:, hs], col(Q + tt), bias_bc[:, hs],
                         ALU.mult, ALU.add, dep=vq)
                c2 = vop("scalar_tensor_tensor", r_buf[:, tt, hs],
                         cex_bc[:, hs], col(QW + tt), r_buf[:, tt, hs],
                         ALU.mult, ALU.add, dep=c1)
                v1 = vop("scalar_tensor_tensor", xres[:, tt, hs],
                         ps_d[:, :], col(RSTD + tt), r_buf[:, tt, hs],
                         ALU.mult, ALU.add, dep=c2)
                ns.v_p2 = vop("scalar_tensor_tensor", xres[:, tt, hs],
                              ps_e[:, :], col(RW + tt), xres[:, tt, hs],
                              ALU.mult, ALU.add,
                              accum_out=col((SUMS2 if jc == 0 else SUMS2B) + tt),
                              dep=v1)
                if jc == 1:
                    ns.v_out2[tt] = ns.v_p2
                    ns.v_sq2 = vop("scalar_tensor_tensor", atok[:, 0, :],
                                   xres[:, tt, :], 1.0, xres[:, tt, :],
                                   ALU.mult, ALU.mult, accum_out=col(SSQ2 + tt),
                                   dep=ns.v_p2)

            def c_piece(i):
                onesb = atok[:, 0, 0:1]
                crow = r_buf[0:1, 0, 0:512]
                ones_row = r_buf[0:1, 1, 0:128]
                if i == 0:
                    wait_full("dve", "xvw")
                    v1 = vop("memset", onesb, 1.0)
                    ns.v_ccp = vop("memset", ones_row, 1.0, dep=v1)
                    wait("pe", "dve", prev_evac[0])
                    wait("pe", "dve", prev_evac[1])
                    wait_full("pe", "wT")
                if i < 4:
                    w_sb = wshT if i < 2 else wexT
                    c_dst = csh_bc if i < 2 else cex_bc
                    jc = i % 2
                    wait("pe", "dve", ns.v_ccp)
                    for dt in range(8):
                        mm(ps_d[0:1, :], onesb,
                           w_sb[:, dt, 512 * jc:512 * (jc + 1)],
                           start=(dt == 0), stop=(dt == 7))
                    v = sig("pe")
                    wait("dve", "pe", v)
                    ns.v_ccp = vop("tensor_copy", crow, ps_d[0:1, :])
                    wait("pe", "dve", ns.v_ccp)
                    mm(ps_e[:, :], ones_row, crow, start=True, stop=True)
                    v = sig("pe")
                    wait("dve", "pe", v)
                    ns.v_ccp = vop("tensor_copy",
                                   c_dst[:, 512 * jc:512 * (jc + 1)], ps_e[:, :])
                    if i < 2:
                        ns.v_ccp = vop("tensor_scalar",
                                       c_dst[:, 512 * jc:512 * (jc + 1)],
                                       c_dst[:, 512 * jc:512 * (jc + 1)],
                                       1.0, None, ALU.add, dep=ns.v_ccp)
                else:
                    wait("pe", "dve", ns.v_ccp)
                    for dt in range(8):
                        mm(ps_d[0:1, 0:7], onesb, wg_bf[:, dt, :],
                           start=(dt == 0), stop=(dt == 7))
                    v = sig("pe")
                    wait("dve", "pe", v)
                    ns.v_ccp = vop("tensor_copy", crow[:, 0:7], ps_d[0:1, 0:7])
                    wait("pe", "dve", ns.v_ccp)
                    mm(ps_e[:, 0:7], ones_row, crow[:, 0:7],
                       start=True, stop=True)
                    v = sig("pe")
                    wait("dve", "pe", v)
                    ns.v_ccp = vop("tensor_copy", gate_c[:, :], ps_e[:, 0:7])
                    prev_evac[0] = ns.v_ccp
                    prev_evac[1] = ns.v_ccp

            def seg_for(f, g2):
                segs = [None] * 5
                if f == 0 and g2 == 4:
                    for i in range(4):
                        segs[i] = (lambda j: lambda: c_piece(j))(i)
                    return segs
                if f == 0 and g2 == 5:
                    segs[0] = lambda: c_piece(4)
                    return segs
                if g2 == 0:
                    segs[0] = lambda: proj_group(ks[:, 1, :], "k", 1, ("k", 1))
                    segs[1] = lambda: proj_group(ks[:, 2, :], "k", 2, ("k", 2))
                    segs[2] = lambda: proj_group(ks[:, 3, :], "k", 3, ("k", 3))
                    segs[3] = lambda: proj_group(ks64[:, 1, :], "k64", 1,
                                                 ("k64", 1))

                    def c4():
                        proj_group(ks64[:, 2, :], "k64", 2, ("k64", 2))
                        ns.mark_proj = max(prev_mm[0], prev_mm[1])
                        if f < 7:
                            xts_load(f + 1)
                    segs[4] = c4
                elif g2 == 1 and f < 7:
                    pi = (f + 1) % 2
                    segs[2] = lambda: proj_group(qs[:, pi, 0, :], "q", 0,
                                                 ("q", pi, 0))
                    segs[3] = lambda: proj_group(qs[:, pi, 1, :], "q", 1,
                                                 ("q", pi, 1))
                elif g2 == 2 and f < 7:
                    pi = (f + 1) % 2
                    segs[0] = lambda: proj_group(qs[:, pi, 2, :], "q", 2,
                                                 ("q", pi, 2))
                    segs[1] = lambda: proj_group(qs[:, pi, 3, :], "q", 3,
                                                 ("q", pi, 3))
                    segs[2] = lambda: proj_group(ks[:, 0, :], "k", 0, ("k", 0))
                    segs[3] = lambda: proj_group(
                        ks64[:, k64phys(3, f + 1), :], "k64", 3,
                        ("k64", k64phys(3, f + 1)))
                elif g2 == 3:
                    if f >= 1:
                        for i in range(4):
                            segs[i] = (lambda tt: lambda: gate_chunk(
                                f - 1, tt))(i)
                    if f < 7:
                        segs[4] = lambda: proj_group(ks64[:, 0, :], "k64", 0,
                                                     ("k64", 0))
                elif g2 >= 4 and f >= 1:
                    tt = g2 - 4
                    segs[0] = lambda: ffn_sh(f - 1, tt, 0)
                    segs[1] = lambda: ffn_ex(f - 1, tt, 0)
                    segs[2] = lambda: ffn_sh(f - 1, tt, 1)
                    segs[3] = lambda: ffn_ex(f - 1, tt, 1)
                return segs

            def epilogue1(f):
                wait("gp", "dve", max(ns.v_avadd[0], ns.v_avadd[1]))
                for hp in range(8):
                    ns.m_cast = dma("gp", "cast", att_h[128 * hp:128 * (hp + 1), :],
                                    att_acc[:, hp, :])
                epilogue1b(f)

            def epilogue1b(f):
                wait_full("sync", "cast")
                wait("sync", "dve", ns.v_sq2)
                for tt in range(4):
                    ns.m_atok = tdma("atok", atok[:, tt, :],
                                     att_h[:, 128 * tt:128 * (tt + 1)])

            def ep2a(f):
                wait_full("dve", "out")
                for tt in range(4):
                    if tt == 0:
                        wait("dve", "atok", ns.m_atok)
                        wait("dve", "xres", ns.m_xres)
                    ns.v_resid = vop("scalar_tensor_tensor", r_buf[:, tt, :],
                                     atok[:, tt, :], 1.0, xres[:, tt, :],
                                     ALU.mult, ALU.add, accum_out=col(SUMS + tt),
                                     dep=ns.v_out2[3] if tt == 0 else 0)
                v_sq = ns.v_resid
                for tt in range(4):
                    v_sq = vop("scalar_tensor_tensor", atok[:, 0, :],
                               r_buf[:, tt, :], 1.0, r_buf[:, tt, :],
                               ALU.mult, ALU.mult, accum_out=col(SSQ + tt),
                               dep=v_sq)
                ns.v_ve1 = ln_stats_a(SUMS, SSQ, NEGM, v_sq)

            def ep2b(f):
                ns.v_ln1 = ln_stats_b(RSTD, ns.v_ve1)
                ns.v_x1 = vop("tensor_tensor", col(Q, 4), col(RSTD, 4),
                              col(NEGM, 4), ALU.mult, dep=ns.v_ln1)

            def rt_issue(f):
                wait("sync", "pe", ns.mark_ffn)
                for dt in range(8):
                    ns.m_x1T = tdma("x1t", x1T[:, dt, :],
                                    xbf_h[S * f:S * (f + 1),
                                          128 * dt:128 * (dt + 1)])

            def rt_add(f):
                wait("dve", "x1t", ns.m_x1T)
                v = 0
                for dt in range(8):
                    v = vop("tensor_tensor", x1T[:, dt, :], x1T[:, dt, :],
                            att_acc[:, dt, :], ALU.add,
                            dep=max(ns.v_avadd[0], ns.v_avadd[1])
                            if dt == 0 else 0)
                ns.v_x1T = v

            def gate_post(f, tt):
                wait("dve", "act", ns.v_gate_exp)
                vc = vop("reduce_max", col(G_M1N), gate_e4[:, tt, :], AX.X,
                         negate=True, dep=ns.v_gatepost)
                vd = vop("tensor_scalar", gate_t[:, :], gate_e4[:, tt, :],
                         col(G_M1N), None, ALU.add, dep=vc)
                ve = vop("tensor_scalar", gate_t[:, :], gate_t[:, :], 0.0,
                         None, ALU.is_equal, dep=vd)
                vf = vop("scalar_tensor_tensor", gate_t[:, :], gate_t[:, :],
                         -1e30, gate_e4[:, tt, :], ALU.mult, ALU.add, dep=ve)
                vg = vop("reduce_max", col(G_M2N), gate_t[:, :], AX.X,
                         negate=True, dep=vf)
                vh = vop("tensor_tensor", col(G_S), col(G_M1N), col(G_M2N),
                         ALU.add, dep=vg)
                vi = vop("reciprocal", col(G_S), col(G_S), dep=vh)
                vj = vop("tensor_tensor", col(G_A), col(G_M1N), col(G_S),
                         ALU.mult, dep=vi)
                vk = vop("tensor_tensor", col(G_B), col(G_M2N), col(G_S),
                         ALU.mult, dep=vj)
                ns.v_gatepost = vop("tensor_tensor", col(WSUM + tt), col(G_A),
                                    col(G_B), ALU.add, dep=vk)

            def ln2a(f):
                vs2 = vop("tensor_tensor", col(SUMS2, 4), col(SUMS2, 4),
                          col(SUMS2B, 4), ALU.add,
                          dep=max(ns.v_sq2, ns.v_lnlast))
                ns.v_ve2 = ln_stats_a(SUMS2, SSQ2, NEGM2, vs2)

            def ln2b(f):
                v_ln2 = ln_stats_b(RSTD2, ns.v_ve2)
                v_final = 0
                for tt in range(4):
                    v_final = vop("tensor_scalar", r_buf[:, tt, :], xres[:, tt, :],
                                  col(NEGM2 + tt), col(RSTD2 + tt), ALU.add,
                                  ALU.mult, dep=v_ln2 if tt == 0 else 0)
                wait("sync", "dve", v_final)
                dma("sync", "out", out_h[S * f:S * (f + 1), :]
                    .rearrange("(t p) d -> p t d", p=128), r_buf[:, :, :])
                if f + 1 < 8:
                    ns.m_xres = dma("sync", "xres", xres[:, :, :],
                                    x_h[S * (f + 1):S * (f + 2), :]
                                    .rearrange("(t p) d -> p t d", p=128))

            def av_pair(f, g2p, jj):
                a = f * 8 + g2p
                for hp in (2 * jj, 2 * jj + 1):
                    half = hp % 2
                    slot = (8 * g2p + hp) % 10
                    wait("pe", "dve", ns.v_scale)
                    wait("pe", "dve", ns.v_avadd[half])
                    for c2t in range(4):
                        mm(ps_av[:, 512 * half:512 * (half + 1)],
                           xv_sl[:, 4 * hp + c2t, :], A_buf[:, c2t, slot, :],
                           start=(c2t == 0), stop=(c2t == 3))
                    v_g = sig("pe")
                    ns.mark_av_hp[hp] = v_g
                    ns.mark_av_all = v_g
                    wait("dve", "pe", v_g)
                    if g2p == 0:
                        wait_full("dve", "cast")
                        ns.v_avadd[half] = vop(
                            "tensor_copy", att_acc[:, hp, :],
                            ps_av[:, 512 * half:512 * (half + 1)],
                            dep=ns.v_x1T)
                    else:
                        ns.v_avadd[half] = vop(
                            "tensor_tensor", att_acc[:, hp, :], att_acc[:, hp, :],
                            ps_av[:, 512 * half:512 * (half + 1)], ALU.add,
                            dep=ns.v_avadd[half])
                ns.avsig[(a, jj)] = ns.mark_av_all

            def s_g2(f, g2, segs):
                wait("gp", "dve", ns.v_scale)
                if f == 0 and g2 == 0:
                    wait_full("gp", "xvw")
                m_xv = dma("gp", "xvr", xv_raw[:, :, :],
                           xv_h[g2, :, :].rearrange("(t p) v -> p t v", p=128))
                gm = (g2 - 1) % 8
                pi = f % 2
                if g2 % 2 == 0:
                    lhsT0s = ks[0:64, g2 // 2, :]
                    lhsT64s = ks64[64:128, k64phys(gm // 2, f), :]
                else:
                    lhsT0s = ks64[0:64, k64phys(gm // 2, f), :]
                    lhsT64s = ks[64:128, g2 // 2, :]
                ns.ev_lhsT = max(ns.evacs.get(("k", g2 // 2), 0),
                                 ns.evacs.get(("k64", k64phys(gm // 2, f)), 0))
                sub_i = 0
                a_cur = f * 8 + g2
                par = g2 % 2
                if g2 >= 1:
                    wave = (f, g2 - 1)
                    pair_order = (0, 1, 2, 3)
                    war_keys = ((a_cur - 2, 3), (a_cur - 1, 0),
                                (a_cur - 1, 1), (a_cur - 1, 2))
                elif f >= 1:
                    wave = (f - 1, 7)
                    pair_order = (2, 3, 0, 1)
                    war_keys = ((a_cur - 1, 2), (a_cur - 1, 3),
                                (a_cur - 2, 3), (a_cur - 1, 0))
                else:
                    wave = None
                    war_keys = ((-1, 0),) * 4
                def do_av(p):
                    av_pair(wave[0], wave[1], p)
                    if wave[1] == 7:
                        for hp in (2 * p, 2 * p + 1):
                            wait("gp", "dve", ns.v_avadd[hp % 2])
                            ns.m_cast = dma("gp", "cast",
                                            att_h[128 * hp:128 * (hp + 1), :],
                                            att_acc[:, hp, :])

                for jj in range(4):
                    av_late = not (g2 == 0 and jj < 2)
                    if wave is not None and not av_late:
                        do_av(pair_order[jj])
                    sg = (8 * g2 + 2 * jj) % 10
                    v_war = ns.avsig.get(war_keys[jj], 0)
                    for c2t in range(4):
                        cs = slice(128 * c2t, 128 * (c2t + 1))
                        sub = sub_i % 2
                        sub_i += 1
                        wait("pe", "act", ns.v_exp_slot[sub])
                        if c2t == 0:
                            if jj == 0:
                                wait("pe", "dve", ns.ev_lhsT)
                            wait("pe", "dve",
                                 ns.evacs.get(("q", pi, jj), 0))
                        b0 = 1024 * sub
                        mm(ps_a[:, b0:b0 + 512],
                           lhsT0s[:, cs], qs[0:64, pi, jj, :],
                           start=True, stop=True, tile_position=(0, 0))
                        mm(ps_a[:, b0 + 512:b0 + 1024],
                           lhsT64s[:, cs], qs[64:128, pi, jj, :],
                           start=True, stop=True, tile_position=(64, 0))
                        v_slot = sig("pe")
                        wait("act", "pe", v_slot)
                        if c2t == 0:
                            wait("act", "pe", v_war)
                            if jj == 0:
                                wait("act", "dve", ns.v_zfin_par[par])
                        ns.v_exp = aop("activation",
                                       A_buf[:, c2t, sg:sg + 2, :],
                                       ps_a[:, b0:b0 + 1024], AF.Exp, 0.0, 0.125,
                                       accum_out=zacc[:, c2t, jj, par:par + 1],
                                       serial=False)
                        ns.v_exp_slot[sub] = ns.v_exp
                    if wave is not None and av_late:
                        do_av(pair_order[jj])
                    if jj == 3:
                        wait("dve", "act", ns.v_exp)
                        v1 = vop("tensor_tensor", zinv[:, :], zacc[:, :, 0, par],
                                 zacc[:, :, 1, par], ALU.add, dep=ns.v_scale)
                        v1 = vop("tensor_tensor", col(60, 4), zacc[:, :, 2, par],
                                 zacc[:, :, 3, par], ALU.add, dep=v1)
                        v1 = vop("tensor_tensor", zinv[:, :], zinv[:, :],
                                 col(60, 4), ALU.add, dep=v1)
                        ns.v_zfin_par[par] = vop("reciprocal", zinv[:, :],
                                                 zinv[:, :], dep=v1)
                        wait("dve", "xvr", m_xv)
                        wait("dve", "pe", ns.mark_av_all)
                        xvr4 = xv_raw[:, :, :].rearrange("p (a b) v -> p b a v", b=4)
                        xvs4 = xv_sl[:, :, :].rearrange("p (a b) v -> p b a v", b=4)
                        for c2t in range(4):
                            ns.v_scale = vop(
                                "tensor_scalar", xvs4[:, c2t, :, :],
                                xvr4[:, c2t, :, :], zinv[:, c2t:c2t + 1],
                                None, ALU.mult,
                                dep=ns.v_zfin_par[par] if c2t == 0 else 0)
                    if segs[jj] is not None:
                        segs[jj]()
                if segs[4] is not None:
                    segs[4]()

            ns.m_xres = dma("sync", "xres", xres[:, :, :],
                            x_h[0:S, :].rearrange("(t p) d -> p t d", p=128))
            xts_load(0)
            wait("pe", "dve", v_wcp)
            proj_sec0(0)
            for f in range(8):
                for g2 in range(8):
                    s_g2(f, g2, seg_for(f, g2))
                    if f == 0 and g2 == 2:
                        wait_full("sync", "gw")
                        for dt in range(8):
                            tdma("wT", wshT[:, dt, :],
                                 wshbf_h[:, 128 * dt:128 * (dt + 1)])
                        for dt in range(8):
                            tdma("wT", wexT[:, dt, :],
                                 wexbf_h[:, 128 * dt:128 * (dt + 1)])
                    if g2 == 7:
                        if f >= 1:
                            ln2a(f - 1)
                        rt_issue(f)
                    elif f >= 1:
                        if g2 == 0:
                            rt_add(f - 1)
                            if f >= 2:
                                ln2b(f - 2)
                        elif g2 == 1:
                            epilogue1b(f - 1)
                            ep2a(f - 1)
                        elif g2 == 2:
                            ep2b(f - 1)
                        elif 3 <= g2 <= 6:
                            gate_post(f - 1, g2 - 3)
            ln2b(6)
            for p in range(4):
                av_pair(7, 7, p)
            epilogue1(7)
            rt_add(7)
            ep2a(7)
            ep2b(7)
            for tt in range(4):
                gate_chunk(7, tt)
            for tt in range(4):
                gate_post(7, tt)
                ffn_sh(7, tt, 0)
                ffn_ex(7, tt, 0)
                ffn_sh(7, tt, 1)
                ffn_ex(7, tt, 1)
            ln2a(7)
            ln2b(7)

            if taps:
                wait_full("sync", "out")
                dma("sync", "out", taps["o_xv"][:, :, :], xv_h[:, :, :])
                dma("sync", "out", taps["o_attacc"][:, :],
                    att_acc[:, :, :].rearrange("p a b -> p (a b)"))
                dma("sync", "out", taps["o_A"][:, :],
                    A_buf[:, :, :, :].rearrange("p a b c -> p (a b c)"))
                dma("sync", "out", taps["o_zinv"][:, :], zinv[:, :])
                dma("sync", "out", taps["o_qs"][:, :],
                    qs[:, :, :, :].rearrange("p a b c -> p (a b c)"))
                dma("sync", "out", taps["o_ks"][:, :],
                    ks[:, :, :].rearrange("p a b -> p (a b)"))
                dma("sync", "out", taps["o_q64"][:, :],
                    ks64[:, :, :].rearrange("p a b -> p (a b)"))
                dma("sync", "out", taps["o_stat"][:, :], stat[:, :])
                dma("sync", "out", taps["o_xbf"][:, :], xbf_h[:, :])

            wait_full("sync", "out")
            wait_full("gp", "out")

        block.sync(lambda e: prog("sync", e))
        block.gpsimd(lambda e: prog("gp", e))
        block.tensor(lambda e: prog("pe", e))
        block.scalar(lambda e: prog("act", e))
        block.vector(lambda e: prog("dve", e))

        nc.all_engine_barrier()
    return nc


_NC_CACHE = None


def kernel(L0_attn_x, L0_attn_Mq, L0_attn_Mk, L0_attn_Mv, L0_ffn_cond=None,
           L0_ffn_mask=None, L0_ffn_W_shared=None, L0_ffn_W_expert=None,
           L0_ffn_W_gate=None, L0_ffn_bias=None, **_unused):
    global _NC_CACHE
    x = np.ascontiguousarray(np.asarray(L0_attn_x, dtype=np.float32))
    mq = np.ascontiguousarray(np.asarray(L0_attn_Mq, dtype=np.float32))
    mk = np.ascontiguousarray(np.asarray(L0_attn_Mk, dtype=np.float32))
    mv = np.ascontiguousarray(np.asarray(L0_attn_Mv, dtype=np.float32))
    wsh = np.ascontiguousarray(np.asarray(L0_ffn_W_shared, dtype=np.float32))
    wex = np.ascontiguousarray(np.asarray(L0_ffn_W_expert, dtype=np.float32))
    wg = np.ascontiguousarray(np.asarray(L0_ffn_W_gate, dtype=np.float32))
    bias = np.ascontiguousarray(np.asarray(L0_ffn_bias, dtype=np.float32).reshape(1, D))

    if _NC_CACHE is None:
        _NC_CACHE = build_program()
    nc = _NC_CACHE
    in_maps = [{"x": x[b], "mq": mq, "mk": mk, "mv": mv, "wsh": wsh,
                "wex": wex, "wg": wg, "bias": bias} for b in range(NCORES)]
    res = run_bass_kernel_spmd(nc, in_maps, core_ids=list(range(NCORES)))
    out = np.stack([np.asarray(res.results[b]["out"]) for b in range(NCORES)], axis=0)
    return out.astype(np.float32)


# version 38 (final)
# speedup vs baseline: 1.0791x; 1.0020x over previous
import numpy as np
import concourse.bass as bass
import concourse.mybir as mybir
from concourse.bass_utils import run_bass_kernel_spmd

F32 = mybir.dt.float32
BF16 = mybir.dt.bfloat16
AF = mybir.ActivationFunctionType
ALU = mybir.AluOpType
AX = mybir.AxisListType

C, D, H, S, KD, V = 4096, 1024, 8, 512, 64, 128
NCORES = 8

DMA_SEMS = ("ld", "gbf", "gb2", "gw", "wT", "xts", "xt2", "xvw", "xvr",
            "cast", "xres", "atok", "x1w", "x1t", "out")
ENG_SEMS = ("pe", "act", "dve", "gpc")


def build_program(debug_taps=False):
    nc = bass.Bass("TRN2", target_bir_lowering=False, dynamic_dma_scratch_size=16384)

    x_h = nc.dram_tensor("x", [C, D], F32, kind="ExternalInput").ap()
    mq_h = nc.dram_tensor("mq", [D, 512], F32, kind="ExternalInput").ap()
    mk_h = nc.dram_tensor("mk", [D, 512], F32, kind="ExternalInput").ap()
    mv_h = nc.dram_tensor("mv", [D, D], F32, kind="ExternalInput").ap()
    wsh_h = nc.dram_tensor("wsh", [D, D], F32, kind="ExternalInput").ap()
    wex_h = nc.dram_tensor("wex", [D, D], F32, kind="ExternalInput").ap()
    wg_h = nc.dram_tensor("wg", [D, 7], F32, kind="ExternalInput").ap()
    bias_h = nc.dram_tensor("bias", [1, D], F32, kind="ExternalInput").ap()
    out_h = nc.dram_tensor("out", [C, D], F32, kind="ExternalOutput").ap()
    xbf_h = nc.dram_tensor("xbf", [C, D], BF16, kind="Internal").ap()
    xv_h = nc.dram_tensor("xvh", [H, C, V], BF16, kind="Internal").ap()
    att_h = nc.dram_tensor("atth", [D, S], BF16, kind="Internal").ap()
    x1_h = nc.dram_tensor("x1h", [S, D], BF16, kind="Internal").ap()
    ident_h = nc.dram_tensor("identh", [1, 16640], BF16, kind="Internal").ap()
    wshbf_h = nc.dram_tensor("wshbf", [D, D], BF16, kind="Internal").ap()
    wexbf_h = nc.dram_tensor("wexbf", [D, D], BF16, kind="Internal").ap()
    taps = {}
    if debug_taps:
        taps["o_xv"] = nc.dram_tensor("o_xv", [H, C, V], BF16, kind="ExternalOutput").ap()
        taps["o_attacc"] = nc.dram_tensor("o_attacc", [128, 8 * 512], F32, kind="ExternalOutput").ap()
        taps["o_A"] = nc.dram_tensor("o_A", [128, 4 * 10 * 512], BF16, kind="ExternalOutput").ap()
        taps["o_zinv"] = nc.dram_tensor("o_zinv", [128, 4], F32, kind="ExternalOutput").ap()
        taps["o_qs"] = nc.dram_tensor("o_qs", [128, 2 * 4 * 512], BF16, kind="ExternalOutput").ap()
        taps["o_ks"] = nc.dram_tensor("o_ks", [128, 4 * 512], BF16, kind="ExternalOutput").ap()
        taps["o_q64"] = nc.dram_tensor("o_q64", [128, 5 * 512], BF16, kind="ExternalOutput").ap()
        taps["o_stat"] = nc.dram_tensor("o_stat", [128, 80], F32, kind="ExternalOutput").ap()
        taps["o_xbf"] = nc.dram_tensor("o_xbf", [C, D], BF16, kind="ExternalOutput").ap()

    from contextlib import ExitStack
    with ExitStack() as ctx:
        E = ctx.enter_context
        mq_bf = E(nc.sbuf_tensor("mq_bf", [128, 8, 512], BF16))
        mk_bf = E(nc.sbuf_tensor("mk_bf", [128, 8, 512], BF16))
        wshT = E(nc.sbuf_tensor("wshT", [128, 8, 1024], BF16))
        wexT = E(nc.sbuf_tensor("wexT", [128, 8, 1024], BF16))
        wg_bf = E(nc.sbuf_tensor("wg_bf", [128, 8, 7], BF16))
        bias_bc = E(nc.sbuf_tensor("bias_bc", [128, D], F32))
        A_buf = E(nc.sbuf_tensor("A_buf", [128, 4, 10, 512], BF16))
        qs = E(nc.sbuf_tensor("qs", [128, 2, 4, 512], BF16))
        ks = E(nc.sbuf_tensor("ks", [128, 4, 512], BF16))
        ks64 = E(nc.sbuf_tensor("ks64", [128, 5, 512], BF16))
        xts = E(nc.sbuf_tensor("xts", [128, 8, 512], BF16))
        xv_raw = E(nc.sbuf_tensor("xv_raw", [128, 32, 128], BF16))
        xv_sl = E(nc.sbuf_tensor("xv_sl", [128, 32, 128], BF16))
        att_acc = E(nc.sbuf_tensor("att_acc", [128, 8, 512], F32))
        atok = E(nc.sbuf_tensor("atok", [128, 4, 1024], BF16))
        x1T = E(nc.sbuf_tensor("x1T", [128, 8, 512], BF16))
        xres = E(nc.sbuf_tensor("xres", [128, 4, 1024], F32))
        r_buf = E(nc.sbuf_tensor("r_buf", [128, 4, 1024], F32))
        zacc = E(nc.sbuf_tensor("zacc", [128, 4, 4, 2], F32))
        zinv = E(nc.sbuf_tensor("zinv", [128, 4], F32))
        stat = E(nc.sbuf_tensor("stat", [128, 80], F32))
        gate_t = E(nc.sbuf_tensor("gate_t", [128, 7], F32))
        gate_p4 = E(nc.sbuf_tensor("gate_p4", [128, 4, 7], F32))
        gate_e4 = E(nc.sbuf_tensor("gate_e4", [128, 4, 7], F32))
        gate_c = E(nc.sbuf_tensor("gate_c", [128, 7], F32))
        csh_bc = E(nc.sbuf_tensor("csh_bc", [128, 1024], BF16))
        cex_bc = E(nc.sbuf_tensor("cex_bc", [128, 1024], BF16))
        ident = E(nc.sbuf_tensor("ident", [128, 1, 128], BF16))
        ps_a = E(nc.psum_tensor("ps_a", [128, 2048], F32))
        ps_av = E(nc.psum_tensor("ps_av", [128, 1024], F32))
        ps_d = E(nc.psum_tensor("ps_d", [128, 512], F32))
        ps_e = E(nc.psum_tensor("ps_e", [128, 512], F32))
        sems = {}
        for name in DMA_SEMS + ENG_SEMS:
            sems[name] = E(nc.semaphore("s_" + name))
        block = E(nc.Block())

        stage_xv = atok[:, :, :].rearrange("p t (g v) -> p t g v", g=8)
        st = stat
        SUMS, SSQ, RSTD, NEGM = 0, 4, 8, 12
        SUMS2, SUMS2B, SSQ2, RSTD2, NEGM2, WSUM = 16, 20, 24, 28, 32, 36
        T1, T2, T3 = 40, 44, 48
        G_NM, G_ES, G_RS, G_M1N, G_M2N, G_S, G_A, G_B = 52, 53, 54, 55, 56, 57, 58, 59
        Q, QW, RW = 64, 68, 72

        def col(c, n=1):
            return st[:, c:c + n]

        def prog(me, eng):
            cnt = {k: 0 for k in sems}
            last = {}

            def dma(q, sem, dst, src, **kw):
                cnt[sem] += 16
                if me == q:
                    eng.dma_start(dst, src, **kw).then_inc(sems[sem], 16)
                return cnt[sem]

            def tdma(sem, dst, src, q="sync"):
                cnt[sem] += 16
                if me == q:
                    eng.dma_start(dst, src, transpose=True).then_inc(sems[sem], 16)
                return cnt[sem]

            def wait(who, semname, val):
                if val > 0 and me == who:
                    eng.wait_ge(sems[semname], val)

            def wait_full(who, semname):
                wait(who, semname, cnt[semname])

            def op(engkey, fn, *a, **kw):
                if me == engkey:
                    last[engkey] = getattr(eng, fn)(*a, **kw)

            def sig(engkey):
                cnt[engkey] += 1
                if me == engkey:
                    last[engkey].then_inc(sems[engkey], 1)
                return cnt[engkey]

            def vop(fn, *a, dep=0, **kw):
                wait("dve", "dve", dep)
                op("dve", fn, *a, **kw)
                return sig("dve")

            def gop(fn, *a, dep=0, **kw):
                cnt["gpc"] += 1
                if me == "gp":
                    if dep:
                        eng.wait_ge(sems["gpc"], dep)
                    getattr(eng, fn)(*a, **kw).then_inc(sems["gpc"], 1)
                return cnt["gpc"]

            def aop(fn, *a, dep=0, serial=True, **kw):
                if serial:
                    wait("act", "act", cnt["act"])
                op("act", fn, *a, **kw)
                return sig("act")

            def vbar():
                wait("dve", "dve", cnt["dve"])

            def mm(out, lhsT, rhs, **kw):
                op("pe", "matmul", out, lhsT, rhs, **kw)

            m_bias = dma("sync", "ld", bias_bc[0:1, :], bias_h[:, :])
            vop("memset", stat[:, :], 0.0, dep=0)

            for ch in range(8):
                dma("gp", "gbf", xbf_h[S * ch:S * (ch + 1), :],
                    x_h[S * ch:S * (ch + 1), :])
            for (w_hbm, wbf_hbm) in ((wsh_h, wshbf_h), (wex_h, wexbf_h)):
                for half in range(2):
                    rows = slice(512 * half, 512 * (half + 1))
                    dma("gp", "gw", wbf_hbm[rows, :], w_hbm[rows, :])

            v_z = vop("memset", atok[:, 0, 0:128], 0.0)
            v_o = vop("memset", atok[0:1, 1, 0:128], 1.0, dep=v_z)
            wait("sync", "dve", v_z)
            dma("sync", "ld", ident_h[0:1, 0:16384]
                .rearrange("o (p c) -> p o c", p=128), atok[:, 0:1, 0:128])
            wait_full("sync", "ld")
            wait("sync", "dve", v_o)
            dma("sync", "ld", ident_h[0:1, 0:16512]
                .rearrange("o (a b) -> o a b", a=128)[0:1, :, 0:1],
                atok[0:1, 1:2, 0:128].rearrange("p a c -> p c a"),
                allow_slow_non_contiguous=True)
            wait_full("sync", "ld")
            dma("sync", "ld", ident[:, :, :],
                ident_h[0:1, 0:16384].rearrange("o (p c) -> p o c", p=128))

            r8 = r_buf[:, :, :].rearrange("p t d -> p (t d)").rearrange(
                "p (a b) -> p a b", a=8)
            mv_view = A_buf[:, :, :, :].rearrange("p a b c -> p (a b c)")[:, 0:8192] \
                .rearrange("p (a b) -> p a b", a=8)
            v_w = 0
            for half in range(2):
                cols = slice(512 * half, 512 * (half + 1))
                m_w = dma("sync", "ld", r8,
                          mv_h.rearrange("(a p) m -> p a m", p=128)[:, :, cols])
                wait("dve", "ld", m_w)
                v_w = vop("tensor_copy", mv_view[:, :, cols], r8, dep=v_w)
                wait("sync", "dve", v_w)

            ones_ap = att_acc[0:1, 0, 0:128]
            v_ones = vop("memset", ones_ap, 1.0)
            wait_full("pe", "ld")
            wait("pe", "dve", v_ones)
            mm(ps_d[:, :], ones_ap, bias_bc[0:1, 0:512], start=True, stop=True)
            sig("pe")
            mm(ps_e[:, :], ones_ap, bias_bc[0:1, 512:1024], start=True, stop=True)
            v_pe = sig("pe")
            wait("dve", "pe", v_pe)
            vop("tensor_copy", bias_bc[:, 0:512], ps_d[:, :])
            v_bias = vop("tensor_copy", bias_bc[:, 512:1024], ps_e[:, :])

            rg = r_buf[:, :, :].rearrange("p t d -> p (t d)")[:, 0:56] \
                .rearrange("p (a b) -> p a b", a=8)
            prev_mm = {0: v_pe, 1: v_pe}
            prev_evac = {0: v_bias, 1: v_bias}
            mark_p1mm = 0
            def mk_simple(w_view, w_sb, stg):
                return (lambda: dma("sync", "ld", stg, w_view),
                        lambda dep: vop("tensor_copy", w_sb, stg, dep=dep))

            wjobs = [
                mk_simple(mq_h.rearrange("(a p) m -> p a m", p=128),
                          mq_bf[:, :, :], r8),
                mk_simple(mk_h.rearrange("(a p) m -> p a m", p=128),
                          mk_bf[:, :, :], r8),
                mk_simple(wg_h.rearrange("(a p) e -> p a e", p=128),
                          wg_bf[:, :, :], rg),
            ]
            v_wcp = v_w
            m_wld = 0
            mark_par = {0: 0, 1: 0}
            for ch in range(8):
                if ch == 0:
                    wait_full("sync", "gbf")
                pbuf = xts if ch % 2 == 0 else x1T
                psem = "xts" if ch % 2 == 0 else "xt2"
                wait("sync", "pe", mark_par[ch % 2])
                for dt in range(8):
                    m_xts = tdma(psem, pbuf[:, dt, :],
                                 xbf_h[S * ch:S * (ch + 1), 128 * dt:128 * (dt + 1)])
                if 1 <= ch <= len(wjobs):
                    wait("dve", "ld", m_wld)
                    v_wcp = wjobs[ch - 1][1](v_wcp)
                if ch < len(wjobs):
                    wait("sync", "dve", v_wcp)
                    m_wld = wjobs[ch][0]()
                vbar()
                for tt in range(4):
                    for dvc in range(2):
                        sel = (tt * 2 + dvc) % 2
                        psel = ps_d if sel == 0 else ps_e
                        wait("pe", psem, m_xts)
                        wait("pe", "dve", prev_evac[sel])
                        for dt in range(8):
                            mm(psel[:, :], pbuf[:, dt, 128 * tt:128 * (tt + 1)],
                               mv_view[:, dt, 512 * dvc:512 * (dvc + 1)],
                               start=(dt == 0), stop=(dt == 7))
                        prev_mm[sel] = sig("pe")
                        wait("dve", "pe", prev_mm[sel])
                        wait_full("dve", "xvw")
                        prev_evac[sel] = vop(
                            "tensor_copy", stage_xv[:, tt, 4 * dvc:4 * (dvc + 1), :],
                            psel[:, :])
                mark_p1mm = max(prev_mm[0], prev_mm[1])
                mark_par[ch % 2] = mark_p1mm
                v_stage = max(prev_evac[0], prev_evac[1])
                wait("gp", "dve", v_stage)
                for g2 in range(8):
                    dma("gp", "xvw", xv_h[g2, S * ch:S * (ch + 1), :]
                        .rearrange("(t p) v -> p t v", p=128), stage_xv[:, :, g2, :])

            class NS:
                pass
            ns = NS()
            ns.v_scale = 0
            ns.v_zfin = 0
            ns.v_avadd = {0: 0, 1: 0}
            ns.mark_proj = mark_p1mm
            ns.mark_av_all = 0
            ns.mark_av_hp = [0] * 8
            ns.v_exp = 0
            ns.v_exp_slot = {0: 0, 1: 0}
            ns.v_out2 = {0: 0, 1: 0, 2: 0, 3: 0}
            ns.v_gate_exp = 0
            ns.v_gatepost = 0
            ns.v_q_evac = 0
            ns.v_sq2 = 0
            ns.v_p2 = 0
            ns.v_resid = 0
            ns.v_x1 = 0
            ns.v_ln1 = 0
            ns.m_xres = 0
            ns.m_xts = 0
            ns.m_xt2 = 0
            ns.m_atok = 0
            ns.m_x1T = 0
            ns.m_cast = 0
            ns.mark_ffn = mark_p1mm
            ns.mark_gate = {}
            ns.v_lnlast = 0
            ns.avsig = {}
            ns.v_zfin_par = {0: 0, 1: 0}
            ns.proj_sel = 0
            ns.v_ve1 = 0
            ns.v_ve2 = 0
            ns.v_x1T = 0
            ns.v_gate_fix = 0
            ns.v_ccp = 0
            ns.evacs = {}
            ns.ev_lhsT = 0

            def ln_stats_a(SUMS_, SSQ_, NEGM_, dve_dep):
                va = vop("tensor_scalar", col(NEGM_, 4), col(SUMS_, 4),
                         -1.0 / D, None, ALU.mult,
                         dep=max(dve_dep, ns.v_lnlast))
                vb = vop("tensor_scalar", col(T1, 4), col(SSQ_, 4),
                         1.0 / D, None, ALU.mult, dep=va)
                vc = vop("tensor_tensor", col(T2, 4), col(NEGM_, 4),
                         col(NEGM_, 4), ALU.mult, dep=va)
                vd = vop("tensor_tensor", col(T1, 4), col(T1, 4), col(T2, 4),
                         ALU.subtract, dep=vc)
                ve = vop("tensor_scalar", col(T1, 4), col(T1, 4), 1e-5, None,
                         ALU.add, dep=vd)
                return ve

            def ln_stats_b(RSTD_, ve):
                wait("act", "dve", ve)
                vf = aop("activation", col(T2, 4), col(T1, 4), AF.Ln, 0.0, 1.0)
                vg = aop("activation", col(T3, 4), col(T2, 4), AF.Exp, 0.0, -0.5,
                         dep=vf)
                wait("dve", "act", vg)
                vh = vop("tensor_tensor", col(T2, 4), col(T3, 4), col(T3, 4),
                         ALU.mult, dep=ve)
                vi = vop("tensor_tensor", col(T2, 4), col(T2, 4), col(T1, 4),
                         ALU.mult, dep=vh)
                vj = vop("tensor_scalar", col(T2, 4), col(T2, 4), -0.5, 1.5,
                         ALU.mult, ALU.add, dep=vi)
                ns.v_lnlast = vop("tensor_tensor", col(RSTD_, 4), col(T3, 4),
                                  col(T2, 4), ALU.mult, dep=vj)
                return ns.v_lnlast

            def ln_stats(SUMS_, SSQ_, RSTD_, NEGM_, dve_dep):
                return ln_stats_b(RSTD_, ln_stats_a(SUMS_, SSQ_, NEGM_, dve_dep))

            def k64phys(i, fh):
                return 3 + (fh % 2) if i == 3 else i

            def xts_load(fh):
                wait("sync", "pe", ns.mark_proj)
                wait("act", "pe", ns.mark_proj)
                wait_full("act", "gbf")
                wait_full("act", "gb2")
                for dt in range(8):
                    if dt % 2 == 0:
                        ns.m_xts = tdma("xts", xts[:, dt, :],
                                        xbf_h[S * fh:S * (fh + 1),
                                              128 * dt:128 * (dt + 1)])
                    else:
                        ns.m_xt2 = tdma("xt2", xts[:, dt, :],
                                        xbf_h[S * fh:S * (fh + 1),
                                              128 * dt:128 * (dt + 1)])

            def proj_group(dest_ap, kind, pt, ekey=None):
                sel = ns.proj_sel
                ns.proj_sel ^= 1
                psel = ps_d if sel == 0 else ps_e
                wait("pe", "xts", ns.m_xts)
                wait("pe", "xt2", ns.m_xt2)
                wait("pe", "dve", prev_evac[sel])
                wait("pe", "dve", ns.v_p2)
                wait("pe", "dve", ns.v_gate_fix)
                if kind == "q":
                    for dt in range(8):
                        mm(psel[:, :], mq_bf[:, dt, 128 * pt:128 * (pt + 1)],
                           xts[:, dt, :], start=(dt == 0), stop=(dt == 7))
                elif kind == "k":
                    for dt in range(8):
                        mm(psel[:, :], mk_bf[:, dt, 128 * pt:128 * (pt + 1)],
                           xts[:, dt, :], start=(dt == 0), stop=(dt == 7))
                else:
                    c0 = 64 + 128 * pt
                    if pt < 3:
                        for dt in range(8):
                            mm(psel[:, :], mk_bf[:, dt, c0:c0 + 128],
                               xts[:, dt, :], start=(dt == 0), stop=(dt == 7))
                    else:
                        for dt in range(8):
                            mm(psel[0:64, :], mk_bf[:, dt, 448:512],
                               xts[:, dt, :], start=(dt == 0), stop=(dt == 7))
                        for dt in range(8):
                            mm(psel[64:128, :], mk_bf[:, dt, 0:64],
                               xts[:, dt, :], start=(dt == 0), stop=(dt == 7),
                               tile_position=(0, 64))
                v = sig("pe")
                prev_mm[sel] = v
                wait("dve", "pe", v)
                prev_evac[sel] = vop("tensor_copy", dest_ap, psel[:, :])
                ns.v_q_evac = prev_evac[sel]
                if ekey is not None:
                    ns.evacs[ekey] = prev_evac[sel]

            def proj_sec0(fh):
                for pt in range(4):
                    proj_group(qs[:, fh % 2, pt, :], "q", pt, ("q", fh % 2, pt))
                proj_group(ks[:, 0, :], "k", 0, ("k", 0))
                proj_group(ks64[:, k64phys(3, fh), :], "k64", 3,
                           ("k64", k64phys(3, fh)))
                proj_group(ks64[:, 0, :], "k64", 0, ("k64", 0))

            def gate_chunk(f, tt):
                if tt == 0:
                    wait("pe", "x1t", ns.m_x1T)
                    wait("pe", "dve", ns.v_x1T)
                    wait("pe", "dve", prev_evac[0])
                    wait("pe", "dve", prev_evac[1])
                    wait("pe", "dve", ns.v_p2)
                else:
                    wait("pe", "dve", ns.v_gate_fix)
                psel = ps_d if tt % 2 == 0 else ps_e
                co = 8 * (tt // 2)
                for dt in range(8):
                    mm(psel[:, co:co + 7],
                       x1T[:, dt, 128 * tt:128 * (tt + 1)],
                       wg_bf[:, dt, :], start=(dt == 0), stop=(dt == 7))
                ns.mark_gate[tt] = sig("pe")
                wait("dve", "pe", ns.mark_gate[tt])
                if tt == 0:
                    wait("dve", "act", ns.v_gate_exp)
                va = vop("tensor_scalar", gate_t[:, :], psel[:, co:co + 7],
                         col(RSTD + tt), None, ALU.mult,
                         dep=ns.v_x1 if tt == 0 else 0)
                ns.v_gate_fix = vop("scalar_tensor_tensor", gate_p4[:, tt, :],
                                    gate_c[:, :], col(Q + tt), gate_t[:, :],
                                    ALU.mult, ALU.add, dep=va)
                wait("act", "dve", ns.v_gate_fix)
                ns.v_gate_exp = aop("activation", gate_e4[:, tt, :],
                                    gate_p4[:, tt, :], AF.Exp, 0.0, 1.0,
                                    serial=False)

            def ffn_sh(f, tt, jc):
                if tt == 0 and jc == 0:
                    wait_full("pe", "wT")
                    wait("pe", "dve", ns.v_gate_fix)
                    wait("pe", "dve", prev_evac[0])
                wait("pe", "dve", ns.v_p2)
                for dt in range(8):
                    mm(ps_d[:, :], x1T[:, dt, 128 * tt:128 * (tt + 1)],
                       wshT[:, dt, 512 * jc:512 * (jc + 1)],
                       start=(dt == 0), stop=False)
                for cb in range(4):
                    mm(ps_d[:, 128 * cb:128 * (cb + 1)],
                       x1T[:, 4 * jc + cb, 128 * tt:128 * (tt + 1)],
                       ident[:, 0, :], start=False, stop=(cb == 3))

            def ffn_ex(f, tt, jc):
                if tt == 0 and jc == 0:
                    wait("pe", "dve", prev_evac[1])
                for dt in range(8):
                    mm(ps_e[:, :], x1T[:, dt, 128 * tt:128 * (tt + 1)],
                       wexT[:, dt, 512 * jc:512 * (jc + 1)],
                       start=(dt == 0), stop=(dt == 7))
                ns.mark_ffn = sig("pe")
                hs = slice(512 * jc, 512 * (jc + 1))
                wait("dve", "pe", ns.mark_ffn)
                if tt == 0 and jc == 0:
                    wait_full("dve", "out")
                if jc == 0:
                    vq = vop("tensor_tensor", col(QW + tt), col(Q + tt),
                             col(WSUM + tt), ALU.mult,
                             dep=max(ns.v_gatepost, ns.v_x1))
                    vq = vop("tensor_tensor", col(RW + tt), col(RSTD + tt),
                             col(WSUM + tt), ALU.mult, dep=vq)
                else:
                    vq = 0
                c1 = vop("scalar_tensor_tensor", r_buf[:, tt, hs],
                         csh_bc[:, hs], col(Q + tt), bias_bc[:, hs],
                         ALU.mult, ALU.add, dep=vq)
                c2 = vop("scalar_tensor_tensor", r_buf[:, tt, hs],
                         cex_bc[:, hs], col(QW + tt), r_buf[:, tt, hs],
                         ALU.mult, ALU.add, dep=c1)
                v1 = vop("scalar_tensor_tensor", xres[:, tt, hs],
                         ps_d[:, :], col(RSTD + tt), r_buf[:, tt, hs],
                         ALU.mult, ALU.add, dep=c2)
                ns.v_p2 = vop("scalar_tensor_tensor", xres[:, tt, hs],
                              ps_e[:, :], col(RW + tt), xres[:, tt, hs],
                              ALU.mult, ALU.add,
                              accum_out=col((SUMS2 if jc == 0 else SUMS2B) + tt),
                              dep=v1)
                if jc == 1:
                    ns.v_out2[tt] = ns.v_p2
                    ns.v_sq2 = vop("scalar_tensor_tensor", atok[:, 0, :],
                                   xres[:, tt, :], 1.0, xres[:, tt, :],
                                   ALU.mult, ALU.mult, accum_out=col(SSQ2 + tt),
                                   dep=ns.v_p2)

            def c_piece(i):
                onesb = atok[:, 0, 0:1]
                crow = r_buf[0:1, 0, 0:512]
                ones_row = r_buf[0:1, 1, 0:128]
                if i == 0:
                    wait_full("dve", "xvw")
                    v1 = vop("memset", onesb, 1.0)
                    ns.v_ccp = vop("memset", ones_row, 1.0, dep=v1)
                    wait("pe", "dve", prev_evac[0])
                    wait("pe", "dve", prev_evac[1])
                    wait_full("pe", "wT")
                if i < 4:
                    w_sb = wshT if i < 2 else wexT
                    c_dst = csh_bc if i < 2 else cex_bc
                    jc = i % 2
                    wait("pe", "dve", ns.v_ccp)
                    for dt in range(8):
                        mm(ps_d[0:1, :], onesb,
                           w_sb[:, dt, 512 * jc:512 * (jc + 1)],
                           start=(dt == 0), stop=(dt == 7))
                    v = sig("pe")
                    wait("dve", "pe", v)
                    ns.v_ccp = vop("tensor_copy", crow, ps_d[0:1, :])
                    wait("pe", "dve", ns.v_ccp)
                    mm(ps_e[:, :], ones_row, crow, start=True, stop=True)
                    v = sig("pe")
                    wait("dve", "pe", v)
                    ns.v_ccp = vop("tensor_copy",
                                   c_dst[:, 512 * jc:512 * (jc + 1)], ps_e[:, :])
                    if i < 2:
                        ns.v_ccp = vop("tensor_scalar",
                                       c_dst[:, 512 * jc:512 * (jc + 1)],
                                       c_dst[:, 512 * jc:512 * (jc + 1)],
                                       1.0, None, ALU.add, dep=ns.v_ccp)
                else:
                    wait("pe", "dve", ns.v_ccp)
                    for dt in range(8):
                        mm(ps_d[0:1, 0:7], onesb, wg_bf[:, dt, :],
                           start=(dt == 0), stop=(dt == 7))
                    v = sig("pe")
                    wait("dve", "pe", v)
                    ns.v_ccp = vop("tensor_copy", crow[:, 0:7], ps_d[0:1, 0:7])
                    wait("pe", "dve", ns.v_ccp)
                    mm(ps_e[:, 0:7], ones_row, crow[:, 0:7],
                       start=True, stop=True)
                    v = sig("pe")
                    wait("dve", "pe", v)
                    ns.v_ccp = vop("tensor_copy", gate_c[:, :], ps_e[:, 0:7])
                    prev_evac[0] = ns.v_ccp
                    prev_evac[1] = ns.v_ccp

            def seg_for(f, g2):
                segs = [None] * 5
                if f == 0 and g2 == 4:
                    for i in range(4):
                        segs[i] = (lambda j: lambda: c_piece(j))(i)
                    return segs
                if f == 0 and g2 == 5:
                    segs[0] = lambda: c_piece(4)
                    return segs
                if g2 == 0:
                    segs[0] = lambda: proj_group(ks[:, 1, :], "k", 1, ("k", 1))
                    segs[1] = lambda: proj_group(ks[:, 2, :], "k", 2, ("k", 2))
                    segs[2] = lambda: proj_group(ks[:, 3, :], "k", 3, ("k", 3))
                    segs[3] = lambda: proj_group(ks64[:, 1, :], "k64", 1,
                                                 ("k64", 1))

                    def c4():
                        proj_group(ks64[:, 2, :], "k64", 2, ("k64", 2))
                        ns.mark_proj = max(prev_mm[0], prev_mm[1])
                        if f < 7:
                            xts_load(f + 1)
                    segs[4] = c4
                elif g2 == 1 and f < 7:
                    pi = (f + 1) % 2
                    segs[2] = lambda: proj_group(qs[:, pi, 0, :], "q", 0,
                                                 ("q", pi, 0))
                    segs[3] = lambda: proj_group(qs[:, pi, 1, :], "q", 1,
                                                 ("q", pi, 1))
                elif g2 == 2 and f < 7:
                    pi = (f + 1) % 2
                    segs[0] = lambda: proj_group(qs[:, pi, 2, :], "q", 2,
                                                 ("q", pi, 2))
                    segs[1] = lambda: proj_group(qs[:, pi, 3, :], "q", 3,
                                                 ("q", pi, 3))
                    segs[2] = lambda: proj_group(ks[:, 0, :], "k", 0, ("k", 0))
                    segs[3] = lambda: proj_group(
                        ks64[:, k64phys(3, f + 1), :], "k64", 3,
                        ("k64", k64phys(3, f + 1)))
                elif g2 == 3:
                    if f >= 1:
                        for i in range(4):
                            segs[i] = (lambda tt: lambda: gate_chunk(
                                f - 1, tt))(i)
                    if f < 7:
                        segs[4] = lambda: proj_group(ks64[:, 0, :], "k64", 0,
                                                     ("k64", 0))
                elif g2 >= 4 and f >= 1:
                    tt = g2 - 4
                    segs[0] = lambda: ffn_sh(f - 1, tt, 0)
                    segs[1] = lambda: ffn_ex(f - 1, tt, 0)
                    segs[2] = lambda: ffn_sh(f - 1, tt, 1)
                    segs[3] = lambda: ffn_ex(f - 1, tt, 1)
                return segs

            def epilogue1(f):
                wait("gp", "dve", max(ns.v_avadd[0], ns.v_avadd[1]))
                for hp in range(8):
                    ns.m_cast = dma("gp", "cast", att_h[128 * hp:128 * (hp + 1), :],
                                    att_acc[:, hp, :])
                epilogue1b(f)

            def epilogue1b(f):
                wait_full("sync", "cast")
                wait("sync", "dve", ns.v_sq2)
                for tt in range(4):
                    ns.m_atok = tdma("atok", atok[:, tt, :],
                                     att_h[:, 128 * tt:128 * (tt + 1)])

            def ep2a(f):
                wait_full("dve", "out")
                for tt in range(4):
                    if tt == 0:
                        wait("dve", "atok", ns.m_atok)
                        wait("dve", "xres", ns.m_xres)
                    ns.v_resid = vop("scalar_tensor_tensor", r_buf[:, tt, :],
                                     atok[:, tt, :], 1.0, xres[:, tt, :],
                                     ALU.mult, ALU.add, accum_out=col(SUMS + tt),
                                     dep=ns.v_out2[3] if tt == 0 else 0)
                v_sq = ns.v_resid
                for tt in range(4):
                    v_sq = vop("scalar_tensor_tensor", atok[:, 0, :],
                               r_buf[:, tt, :], 1.0, r_buf[:, tt, :],
                               ALU.mult, ALU.mult, accum_out=col(SSQ + tt),
                               dep=v_sq)
                ns.v_ve1 = ln_stats_a(SUMS, SSQ, NEGM, v_sq)

            def ep2b(f):
                ns.v_ln1 = ln_stats_b(RSTD, ns.v_ve1)
                ns.v_x1 = vop("tensor_tensor", col(Q, 4), col(RSTD, 4),
                              col(NEGM, 4), ALU.mult, dep=ns.v_ln1)

            def rt_issue(f):
                wait("sync", "pe", ns.mark_ffn)
                for dt in range(8):
                    ns.m_x1T = tdma("x1t", x1T[:, dt, :],
                                    xbf_h[S * f:S * (f + 1),
                                          128 * dt:128 * (dt + 1)])

            def rt_add(f):
                wait("dve", "x1t", ns.m_x1T)
                v = 0
                for dt in range(8):
                    v = vop("tensor_tensor", x1T[:, dt, :], x1T[:, dt, :],
                            att_acc[:, dt, :], ALU.add,
                            dep=max(ns.v_avadd[0], ns.v_avadd[1])
                            if dt == 0 else 0)
                ns.v_x1T = v

            def gate_post(f, tt):
                wait("dve", "act", ns.v_gate_exp)
                vc = vop("reduce_max", col(G_M1N), gate_e4[:, tt, :], AX.X,
                         negate=True, dep=ns.v_gatepost)
                vd = vop("tensor_scalar", gate_t[:, :], gate_e4[:, tt, :],
                         col(G_M1N), None, ALU.add, dep=vc)
                ve = vop("tensor_scalar", gate_t[:, :], gate_t[:, :], 0.0,
                         None, ALU.is_equal, dep=vd)
                vf = vop("scalar_tensor_tensor", gate_t[:, :], gate_t[:, :],
                         -1e30, gate_e4[:, tt, :], ALU.mult, ALU.add, dep=ve)
                vg = vop("reduce_max", col(G_M2N), gate_t[:, :], AX.X,
                         negate=True, dep=vf)
                vh = vop("tensor_tensor", col(G_S), col(G_M1N), col(G_M2N),
                         ALU.add, dep=vg)
                vi = vop("reciprocal", col(G_S), col(G_S), dep=vh)
                vj = vop("tensor_tensor", col(G_A), col(G_M1N), col(G_S),
                         ALU.mult, dep=vi)
                vk = vop("tensor_tensor", col(G_B), col(G_M2N), col(G_S),
                         ALU.mult, dep=vj)
                ns.v_gatepost = vop("tensor_tensor", col(WSUM + tt), col(G_A),
                                    col(G_B), ALU.add, dep=vk)

            def ln2a(f):
                vs2 = vop("tensor_tensor", col(SUMS2, 4), col(SUMS2, 4),
                          col(SUMS2B, 4), ALU.add,
                          dep=max(ns.v_sq2, ns.v_lnlast))
                ns.v_ve2 = ln_stats_a(SUMS2, SSQ2, NEGM2, vs2)

            def ln2b(f):
                v_ln2 = ln_stats_b(RSTD2, ns.v_ve2)
                v_final = 0
                for tt in range(4):
                    v_final = vop("tensor_scalar", r_buf[:, tt, :], xres[:, tt, :],
                                  col(NEGM2 + tt), col(RSTD2 + tt), ALU.add,
                                  ALU.mult, dep=v_ln2 if tt == 0 else 0)
                wait("sync", "dve", v_final)
                dma("sync", "out", out_h[S * f:S * (f + 1), :]
                    .rearrange("(t p) d -> p t d", p=128), r_buf[:, :, :])
                if f + 1 < 8:
                    ns.m_xres = dma("sync", "xres", xres[:, :, :],
                                    x_h[S * (f + 1):S * (f + 2), :]
                                    .rearrange("(t p) d -> p t d", p=128))

            def av_pair(f, g2p, jj):
                a = f * 8 + g2p
                for hp in (2 * jj, 2 * jj + 1):
                    half = hp % 2
                    slot = (8 * g2p + hp) % 10
                    wait("pe", "dve", ns.v_scale)
                    wait("pe", "dve", ns.v_avadd[half])
                    for c2t in range(4):
                        mm(ps_av[:, 512 * half:512 * (half + 1)],
                           xv_sl[:, 4 * hp + c2t, :], A_buf[:, c2t, slot, :],
                           start=(c2t == 0), stop=(c2t == 3))
                    v_g = sig("pe")
                    ns.mark_av_hp[hp] = v_g
                    ns.mark_av_all = v_g
                    wait("dve", "pe", v_g)
                    if g2p == 0:
                        wait_full("dve", "cast")
                        ns.v_avadd[half] = vop(
                            "tensor_copy", att_acc[:, hp, :],
                            ps_av[:, 512 * half:512 * (half + 1)],
                            dep=ns.v_x1T)
                    else:
                        ns.v_avadd[half] = vop(
                            "tensor_tensor", att_acc[:, hp, :], att_acc[:, hp, :],
                            ps_av[:, 512 * half:512 * (half + 1)], ALU.add,
                            dep=ns.v_avadd[half])
                ns.avsig[(a, jj)] = ns.mark_av_all

            def s_g2(f, g2, segs):
                wait("gp", "dve", ns.v_scale)
                if f == 0 and g2 == 0:
                    wait_full("gp", "xvw")
                m_xv = dma("gp", "xvr", xv_raw[:, :, :],
                           xv_h[g2, :, :].rearrange("(t p) v -> p t v", p=128))
                gm = (g2 - 1) % 8
                pi = f % 2
                if g2 % 2 == 0:
                    lhsT0s = ks[0:64, g2 // 2, :]
                    lhsT64s = ks64[64:128, k64phys(gm // 2, f), :]
                else:
                    lhsT0s = ks64[0:64, k64phys(gm // 2, f), :]
                    lhsT64s = ks[64:128, g2 // 2, :]
                ns.ev_lhsT = max(ns.evacs.get(("k", g2 // 2), 0),
                                 ns.evacs.get(("k64", k64phys(gm // 2, f)), 0))
                sub_i = 0
                a_cur = f * 8 + g2
                par = g2 % 2
                if g2 >= 1:
                    wave = (f, g2 - 1)
                    pair_order = (0, 1, 2, 3)
                    war_keys = ((a_cur - 2, 3), (a_cur - 1, 0),
                                (a_cur - 1, 1), (a_cur - 1, 2))
                elif f >= 1:
                    wave = (f - 1, 7)
                    pair_order = (2, 3, 0, 1)
                    war_keys = ((a_cur - 1, 2), (a_cur - 1, 3),
                                (a_cur - 2, 3), (a_cur - 1, 0))
                else:
                    wave = None
                    war_keys = ((-1, 0),) * 4
                def do_av(p):
                    av_pair(wave[0], wave[1], p)
                    if wave[1] == 7:
                        for hp in (2 * p, 2 * p + 1):
                            wait("gp", "dve", ns.v_avadd[hp % 2])
                            ns.m_cast = dma("gp", "cast",
                                            att_h[128 * hp:128 * (hp + 1), :],
                                            att_acc[:, hp, :])

                for jj in range(4):
                    av_late = not (g2 == 0 and jj < 2)
                    if wave is not None and not av_late:
                        do_av(pair_order[jj])
                    sg = (8 * g2 + 2 * jj) % 10
                    v_war = ns.avsig.get(war_keys[jj], 0)
                    for c2t in range(4):
                        cs = slice(128 * c2t, 128 * (c2t + 1))
                        sub = sub_i % 2
                        sub_i += 1
                        wait("pe", "act", ns.v_exp_slot[sub])
                        if c2t == 0:
                            if jj == 0:
                                wait("pe", "dve", ns.ev_lhsT)
                            wait("pe", "dve",
                                 ns.evacs.get(("q", pi, jj), 0))
                        b0 = 1024 * sub
                        mm(ps_a[:, b0:b0 + 512],
                           lhsT0s[:, cs], qs[0:64, pi, jj, :],
                           start=True, stop=True, tile_position=(0, 0))
                        mm(ps_a[:, b0 + 512:b0 + 1024],
                           lhsT64s[:, cs], qs[64:128, pi, jj, :],
                           start=True, stop=True, tile_position=(64, 0))
                        v_slot = sig("pe")
                        wait("act", "pe", v_slot)
                        if c2t == 0:
                            wait("act", "pe", v_war)
                            if jj == 0:
                                wait("act", "dve", ns.v_zfin_par[par])
                        ns.v_exp = aop("activation",
                                       A_buf[:, c2t, sg:sg + 2, :],
                                       ps_a[:, b0:b0 + 1024], AF.Exp, 0.0, 0.125,
                                       accum_out=zacc[:, c2t, jj, par:par + 1],
                                       serial=False)
                        ns.v_exp_slot[sub] = ns.v_exp
                    if wave is not None and av_late:
                        do_av(pair_order[jj])
                    if jj == 3:
                        wait("dve", "act", ns.v_exp)
                        v1 = vop("tensor_tensor", zinv[:, :], zacc[:, :, 0, par],
                                 zacc[:, :, 1, par], ALU.add, dep=ns.v_scale)
                        v1 = vop("tensor_tensor", col(60, 4), zacc[:, :, 2, par],
                                 zacc[:, :, 3, par], ALU.add, dep=v1)
                        v1 = vop("tensor_tensor", zinv[:, :], zinv[:, :],
                                 col(60, 4), ALU.add, dep=v1)
                        ns.v_zfin_par[par] = vop("reciprocal", zinv[:, :],
                                                 zinv[:, :], dep=v1)
                        wait("dve", "xvr", m_xv)
                        wait("dve", "pe", ns.mark_av_all)
                        xvr4 = xv_raw[:, :, :].rearrange("p (a b) v -> p b a v", b=4)
                        xvs4 = xv_sl[:, :, :].rearrange("p (a b) v -> p b a v", b=4)
                        for c2t in range(4):
                            ns.v_scale = vop(
                                "tensor_scalar", xvs4[:, c2t, :, :],
                                xvr4[:, c2t, :, :], zinv[:, c2t:c2t + 1],
                                None, ALU.mult,
                                dep=ns.v_zfin_par[par] if c2t == 0 else 0)
                    if segs[jj] is not None:
                        segs[jj]()
                if segs[4] is not None:
                    segs[4]()

            ns.m_xres = dma("sync", "xres", xres[:, :, :],
                            x_h[0:S, :].rearrange("(t p) d -> p t d", p=128))
            xts_load(0)
            wait("pe", "dve", v_wcp)
            proj_sec0(0)
            for f in range(8):
                for g2 in range(8):
                    s_g2(f, g2, seg_for(f, g2))
                    if f == 0 and g2 == 2:
                        wait_full("sync", "gw")
                        for dt in range(8):
                            tdma("wT", wshT[:, dt, :],
                                 wshbf_h[:, 128 * dt:128 * (dt + 1)])
                        for dt in range(8):
                            tdma("wT", wexT[:, dt, :],
                                 wexbf_h[:, 128 * dt:128 * (dt + 1)])
                    if g2 == 7:
                        if f >= 1:
                            ln2a(f - 1)
                        rt_issue(f)
                    elif f >= 1:
                        if g2 == 0:
                            rt_add(f - 1)
                            if f >= 2:
                                ln2b(f - 2)
                        elif g2 == 1:
                            epilogue1b(f - 1)
                            ep2a(f - 1)
                        elif g2 == 2:
                            ep2b(f - 1)
                        elif 3 <= g2 <= 6:
                            gate_post(f - 1, g2 - 3)
            ln2b(6)
            for p in range(4):
                av_pair(7, 7, p)
            epilogue1(7)
            rt_add(7)
            ep2a(7)
            ep2b(7)
            for tt in range(4):
                gate_chunk(7, tt)
            for tt in range(4):
                gate_post(7, tt)
                ffn_sh(7, tt, 0)
                ffn_ex(7, tt, 0)
                ffn_sh(7, tt, 1)
                ffn_ex(7, tt, 1)
            ln2a(7)
            ln2b(7)

            if taps:
                wait_full("sync", "out")
                dma("sync", "out", taps["o_xv"][:, :, :], xv_h[:, :, :])
                dma("sync", "out", taps["o_attacc"][:, :],
                    att_acc[:, :, :].rearrange("p a b -> p (a b)"))
                dma("sync", "out", taps["o_A"][:, :],
                    A_buf[:, :, :, :].rearrange("p a b c -> p (a b c)"))
                dma("sync", "out", taps["o_zinv"][:, :], zinv[:, :])
                dma("sync", "out", taps["o_qs"][:, :],
                    qs[:, :, :, :].rearrange("p a b c -> p (a b c)"))
                dma("sync", "out", taps["o_ks"][:, :],
                    ks[:, :, :].rearrange("p a b -> p (a b)"))
                dma("sync", "out", taps["o_q64"][:, :],
                    ks64[:, :, :].rearrange("p a b -> p (a b)"))
                dma("sync", "out", taps["o_stat"][:, :], stat[:, :])
                dma("sync", "out", taps["o_xbf"][:, :], xbf_h[:, :])

            wait_full("sync", "out")
            wait_full("gp", "out")

        block.sync(lambda e: prog("sync", e))
        block.gpsimd(lambda e: prog("gp", e))
        block.tensor(lambda e: prog("pe", e))
        block.scalar(lambda e: prog("act", e))
        block.vector(lambda e: prog("dve", e))

        nc.all_engine_barrier()
    return nc


_NC_CACHE = None


def kernel(L0_attn_x, L0_attn_Mq, L0_attn_Mk, L0_attn_Mv, L0_ffn_cond=None,
           L0_ffn_mask=None, L0_ffn_W_shared=None, L0_ffn_W_expert=None,
           L0_ffn_W_gate=None, L0_ffn_bias=None, **_unused):
    global _NC_CACHE
    x = np.ascontiguousarray(np.asarray(L0_attn_x, dtype=np.float32))
    mq = np.ascontiguousarray(np.asarray(L0_attn_Mq, dtype=np.float32))
    mk = np.ascontiguousarray(np.asarray(L0_attn_Mk, dtype=np.float32))
    mv = np.ascontiguousarray(np.asarray(L0_attn_Mv, dtype=np.float32))
    wsh = np.ascontiguousarray(np.asarray(L0_ffn_W_shared, dtype=np.float32))
    wex = np.ascontiguousarray(np.asarray(L0_ffn_W_expert, dtype=np.float32))
    wg = np.ascontiguousarray(np.asarray(L0_ffn_W_gate, dtype=np.float32))
    bias = np.ascontiguousarray(np.asarray(L0_ffn_bias, dtype=np.float32).reshape(1, D))

    if _NC_CACHE is None:
        _NC_CACHE = build_program()
    nc = _NC_CACHE
    in_maps = [{"x": x[b], "mq": mq, "mk": mk, "mv": mv, "wsh": wsh,
                "wex": wex, "wg": wg, "bias": bias} for b in range(NCORES)]
    res = run_bass_kernel_spmd(nc, in_maps, core_ids=list(range(NCORES)))
    out = np.stack([np.asarray(res.results[b]["out"]) for b in range(NCORES)], axis=0)
    return out.astype(np.float32)


# version 39
# speedup vs baseline: 1.0853x; 1.0058x over previous
import numpy as np
import concourse.bass as bass
import concourse.mybir as mybir
from concourse.bass_utils import run_bass_kernel_spmd

F32 = mybir.dt.float32
BF16 = mybir.dt.bfloat16
AF = mybir.ActivationFunctionType
ALU = mybir.AluOpType
AX = mybir.AxisListType

C, D, H, S, KD, V = 4096, 1024, 8, 512, 64, 128
NCORES = 8

DMA_SEMS = ("ld", "gbf", "gb2", "gw", "wT", "xts", "xt2", "xvw", "xvr",
            "cast", "xres", "atok", "x1w", "x1t", "out")
ENG_SEMS = ("pe", "act", "dve", "gpc")


def build_program(debug_taps=False):
    nc = bass.Bass("TRN2", target_bir_lowering=False, dynamic_dma_scratch_size=16384)

    x_h = nc.dram_tensor("x", [C, D], F32, kind="ExternalInput").ap()
    mq_h = nc.dram_tensor("mq", [D, 512], F32, kind="ExternalInput").ap()
    mk_h = nc.dram_tensor("mk", [D, 512], F32, kind="ExternalInput").ap()
    mv_h = nc.dram_tensor("mv", [D, D], F32, kind="ExternalInput").ap()
    wsh_h = nc.dram_tensor("wsh", [D, D], F32, kind="ExternalInput").ap()
    wex_h = nc.dram_tensor("wex", [D, D], F32, kind="ExternalInput").ap()
    wg_h = nc.dram_tensor("wg", [D, 7], F32, kind="ExternalInput").ap()
    bias_h = nc.dram_tensor("bias", [1, D], F32, kind="ExternalInput").ap()
    out_h = nc.dram_tensor("out", [C, D], F32, kind="ExternalOutput").ap()
    xbf_h = nc.dram_tensor("xbf", [C, D], BF16, kind="Internal").ap()
    xv_h = nc.dram_tensor("xvh", [H, C, V], BF16, kind="Internal").ap()
    att_h = nc.dram_tensor("atth", [D, S], BF16, kind="Internal").ap()
    x1_h = nc.dram_tensor("x1h", [S, D], BF16, kind="Internal").ap()
    ident_h = nc.dram_tensor("identh", [1, 16640], BF16, kind="Internal").ap()
    wshbf_h = nc.dram_tensor("wshbf", [D, D], BF16, kind="Internal").ap()
    wexbf_h = nc.dram_tensor("wexbf", [D, D], BF16, kind="Internal").ap()
    taps = {}
    if debug_taps:
        taps["o_xv"] = nc.dram_tensor("o_xv", [H, C, V], BF16, kind="ExternalOutput").ap()
        taps["o_attacc"] = nc.dram_tensor("o_attacc", [128, 8 * 512], F32, kind="ExternalOutput").ap()
        taps["o_A"] = nc.dram_tensor("o_A", [128, 4 * 10 * 512], BF16, kind="ExternalOutput").ap()
        taps["o_zinv"] = nc.dram_tensor("o_zinv", [128, 4], F32, kind="ExternalOutput").ap()
        taps["o_qs"] = nc.dram_tensor("o_qs", [128, 2 * 4 * 512], BF16, kind="ExternalOutput").ap()
        taps["o_ks"] = nc.dram_tensor("o_ks", [128, 4 * 512], BF16, kind="ExternalOutput").ap()
        taps["o_q64"] = nc.dram_tensor("o_q64", [128, 5 * 512], BF16, kind="ExternalOutput").ap()
        taps["o_stat"] = nc.dram_tensor("o_stat", [128, 80], F32, kind="ExternalOutput").ap()
        taps["o_xbf"] = nc.dram_tensor("o_xbf", [C, D], BF16, kind="ExternalOutput").ap()

    from contextlib import ExitStack
    with ExitStack() as ctx:
        E = ctx.enter_context
        mq_bf = E(nc.sbuf_tensor("mq_bf", [128, 8, 512], BF16))
        mk_bf = E(nc.sbuf_tensor("mk_bf", [128, 8, 512], BF16))
        wshT = E(nc.sbuf_tensor("wshT", [128, 8, 1024], BF16))
        wexT = E(nc.sbuf_tensor("wexT", [128, 8, 1024], BF16))
        wg_bf = E(nc.sbuf_tensor("wg_bf", [128, 8, 7], BF16))
        bias_bc = E(nc.sbuf_tensor("bias_bc", [128, D], F32))
        A_buf = E(nc.sbuf_tensor("A_buf", [128, 4, 10, 512], BF16))
        qs = E(nc.sbuf_tensor("qs", [128, 2, 4, 512], BF16))
        ks = E(nc.sbuf_tensor("ks", [128, 4, 512], BF16))
        ks64 = E(nc.sbuf_tensor("ks64", [128, 5, 512], BF16))
        xts = E(nc.sbuf_tensor("xts", [128, 8, 512], BF16))
        xv_raw = E(nc.sbuf_tensor("xv_raw", [128, 32, 128], BF16))
        xv_sl = E(nc.sbuf_tensor("xv_sl", [128, 32, 128], BF16))
        att_acc = E(nc.sbuf_tensor("att_acc", [128, 8, 512], F32))
        atok = E(nc.sbuf_tensor("atok", [128, 4, 1024], BF16))
        x1T = E(nc.sbuf_tensor("x1T", [128, 8, 512], BF16))
        xres = E(nc.sbuf_tensor("xres", [128, 4, 1024], F32))
        r_buf = E(nc.sbuf_tensor("r_buf", [128, 4, 1024], F32))
        zacc = E(nc.sbuf_tensor("zacc", [128, 4, 4, 2], F32))
        zinv = E(nc.sbuf_tensor("zinv", [128, 4], F32))
        stat = E(nc.sbuf_tensor("stat", [128, 80], F32))
        gate_t = E(nc.sbuf_tensor("gate_t", [128, 7], F32))
        gate_p4 = E(nc.sbuf_tensor("gate_p4", [128, 4, 7], F32))
        gate_e4 = E(nc.sbuf_tensor("gate_e4", [128, 4, 7], F32))
        gate_c = E(nc.sbuf_tensor("gate_c", [128, 7], F32))
        csh_bc = E(nc.sbuf_tensor("csh_bc", [128, 1024], BF16))
        cex_bc = E(nc.sbuf_tensor("cex_bc", [128, 1024], BF16))
        ident = E(nc.sbuf_tensor("ident", [128, 1, 128], BF16))
        ps_a = E(nc.psum_tensor("ps_a", [128, 2048], F32))
        ps_av = E(nc.psum_tensor("ps_av", [128, 1024], F32))
        ps_d = E(nc.psum_tensor("ps_d", [128, 512], F32))
        ps_e = E(nc.psum_tensor("ps_e", [128, 512], F32))
        sems = {}
        for name in DMA_SEMS + ENG_SEMS:
            sems[name] = E(nc.semaphore("s_" + name))
        block = E(nc.Block())

        stage_xv = atok[:, :, :].rearrange("p t (g v) -> p t g v", g=8)
        st = stat
        SUMS, SSQ, RSTD, NEGM = 0, 4, 8, 12
        SUMS2, SUMS2B, SSQ2, RSTD2, NEGM2, WSUM = 16, 20, 24, 28, 32, 36
        T1, T2, T3 = 40, 44, 48
        G_NM, G_ES, G_RS, G_M1N, G_M2N, G_S, G_A, G_B = 52, 53, 54, 55, 56, 57, 58, 59
        Q, QW, RW = 64, 68, 72

        def col(c, n=1):
            return st[:, c:c + n]

        def prog(me, eng):
            cnt = {k: 0 for k in sems}
            last = {}

            def dma(q, sem, dst, src, **kw):
                cnt[sem] += 16
                if me == q:
                    eng.dma_start(dst, src, **kw).then_inc(sems[sem], 16)
                return cnt[sem]

            def tdma(sem, dst, src, q="sync"):
                cnt[sem] += 16
                if me == q:
                    eng.dma_start(dst, src, transpose=True).then_inc(sems[sem], 16)
                return cnt[sem]

            def wait(who, semname, val):
                if val > 0 and me == who:
                    eng.wait_ge(sems[semname], val)

            def wait_full(who, semname):
                wait(who, semname, cnt[semname])

            def op(engkey, fn, *a, **kw):
                if me == engkey:
                    last[engkey] = getattr(eng, fn)(*a, **kw)

            def sig(engkey):
                cnt[engkey] += 1
                if me == engkey:
                    last[engkey].then_inc(sems[engkey], 1)
                return cnt[engkey]

            def vop(fn, *a, dep=0, **kw):
                wait("dve", "dve", dep)
                op("dve", fn, *a, **kw)
                return sig("dve")

            def gop(fn, *a, dep=0, **kw):
                cnt["gpc"] += 1
                if me == "gp":
                    if dep:
                        eng.wait_ge(sems["gpc"], dep)
                    getattr(eng, fn)(*a, **kw).then_inc(sems["gpc"], 1)
                return cnt["gpc"]

            def aop(fn, *a, dep=0, serial=True, **kw):
                if serial:
                    wait("act", "act", cnt["act"])
                op("act", fn, *a, **kw)
                return sig("act")

            def vbar():
                wait("dve", "dve", cnt["dve"])

            def mm(out, lhsT, rhs, **kw):
                op("pe", "matmul", out, lhsT, rhs, **kw)

            m_bias = dma("sync", "ld", bias_bc[0:1, :], bias_h[:, :])
            vop("memset", stat[:, :], 0.0, dep=0)

            for ch in range(8):
                dma("gp", "gbf", xbf_h[S * ch:S * (ch + 1), :],
                    x_h[S * ch:S * (ch + 1), :])
            for (w_hbm, wbf_hbm) in ((wsh_h, wshbf_h), (wex_h, wexbf_h)):
                for half in range(2):
                    rows = slice(512 * half, 512 * (half + 1))
                    dma("gp", "gw", wbf_hbm[rows, :], w_hbm[rows, :])

            v_z = vop("memset", atok[:, 0, 0:128], 0.0)
            v_o = vop("memset", atok[0:1, 1, 0:128], 1.0, dep=v_z)
            wait("sync", "dve", v_z)
            dma("sync", "ld", ident_h[0:1, 0:16384]
                .rearrange("o (p c) -> p o c", p=128), atok[:, 0:1, 0:128])
            wait_full("sync", "ld")
            wait("sync", "dve", v_o)
            dma("sync", "ld", ident_h[0:1, 0:16512]
                .rearrange("o (a b) -> o a b", a=128)[0:1, :, 0:1],
                atok[0:1, 1:2, 0:128].rearrange("p a c -> p c a"),
                allow_slow_non_contiguous=True)
            wait_full("sync", "ld")
            dma("sync", "ld", ident[:, :, :],
                ident_h[0:1, 0:16384].rearrange("o (p c) -> p o c", p=128))

            r8 = r_buf[:, :, :].rearrange("p t d -> p (t d)").rearrange(
                "p (a b) -> p a b", a=8)
            mv_view = A_buf[:, :, :, :].rearrange("p a b c -> p (a b c)")[:, 0:8192] \
                .rearrange("p (a b) -> p a b", a=8)
            v_w = 0
            for half in range(2):
                cols = slice(512 * half, 512 * (half + 1))
                m_w = dma("sync", "ld", r8,
                          mv_h.rearrange("(a p) m -> p a m", p=128)[:, :, cols])
                wait("dve", "ld", m_w)
                v_w = vop("tensor_copy", mv_view[:, :, cols], r8, dep=v_w)
                wait("sync", "dve", v_w)

            ones_ap = att_acc[0:1, 0, 0:128]
            v_ones = vop("memset", ones_ap, 1.0)
            wait_full("pe", "ld")
            wait("pe", "dve", v_ones)
            mm(ps_d[:, :], ones_ap, bias_bc[0:1, 0:512], start=True, stop=True)
            sig("pe")
            mm(ps_e[:, :], ones_ap, bias_bc[0:1, 512:1024], start=True, stop=True)
            v_pe = sig("pe")
            wait("dve", "pe", v_pe)
            vop("tensor_copy", bias_bc[:, 0:512], ps_d[:, :])
            v_bias = vop("tensor_copy", bias_bc[:, 512:1024], ps_e[:, :])

            rg = r_buf[:, :, :].rearrange("p t d -> p (t d)")[:, 0:56] \
                .rearrange("p (a b) -> p a b", a=8)
            prev_mm = {0: v_pe, 1: v_pe}
            prev_evac = {0: v_bias, 1: v_bias}
            mark_p1mm = 0
            def mk_simple(w_view, w_sb, stg):
                return (lambda: dma("sync", "ld", stg, w_view),
                        lambda dep: vop("tensor_copy", w_sb, stg, dep=dep))

            wjobs = [
                mk_simple(mq_h.rearrange("(a p) m -> p a m", p=128),
                          mq_bf[:, :, :], r8),
                mk_simple(mk_h.rearrange("(a p) m -> p a m", p=128),
                          mk_bf[:, :, :], r8),
                mk_simple(wg_h.rearrange("(a p) e -> p a e", p=128),
                          wg_bf[:, :, :], rg),
            ]
            v_wcp = v_w
            m_wld = 0
            mark_par = {0: 0, 1: 0}
            for ch in range(8):
                if ch == 0:
                    wait_full("sync", "gbf")
                pbuf = xts if ch % 2 == 0 else x1T
                psem = "xts" if ch % 2 == 0 else "xt2"
                wait("sync", "pe", mark_par[ch % 2])
                for dt in range(8):
                    m_xts = tdma(psem, pbuf[:, dt, :],
                                 xbf_h[S * ch:S * (ch + 1), 128 * dt:128 * (dt + 1)])
                if 1 <= ch <= len(wjobs):
                    wait("dve", "ld", m_wld)
                    v_wcp = wjobs[ch - 1][1](v_wcp)
                if ch < len(wjobs):
                    wait("sync", "dve", v_wcp)
                    m_wld = wjobs[ch][0]()
                vbar()
                for tt in range(4):
                    for dvc in range(2):
                        sel = (tt * 2 + dvc) % 2
                        psel = ps_d if sel == 0 else ps_e
                        wait("pe", psem, m_xts)
                        wait("pe", "dve", prev_evac[sel])
                        for dt in range(8):
                            mm(psel[:, :], pbuf[:, dt, 128 * tt:128 * (tt + 1)],
                               mv_view[:, dt, 512 * dvc:512 * (dvc + 1)],
                               start=(dt == 0), stop=(dt == 7))
                        prev_mm[sel] = sig("pe")
                        wait("dve", "pe", prev_mm[sel])
                        wait_full("dve", "xvw")
                        prev_evac[sel] = vop(
                            "tensor_copy", stage_xv[:, tt, 4 * dvc:4 * (dvc + 1), :],
                            psel[:, :])
                mark_p1mm = max(prev_mm[0], prev_mm[1])
                mark_par[ch % 2] = mark_p1mm
                v_stage = max(prev_evac[0], prev_evac[1])
                wait("gp", "dve", v_stage)
                for g2 in range(8):
                    dma("gp", "xvw", xv_h[g2, S * ch:S * (ch + 1), :]
                        .rearrange("(t p) v -> p t v", p=128), stage_xv[:, :, g2, :])

            class NS:
                pass
            ns = NS()
            ns.v_scale = 0
            ns.v_zfin = 0
            ns.v_avadd = {0: 0, 1: 0}
            ns.mark_proj = mark_p1mm
            ns.mark_av_all = 0
            ns.mark_av_hp = [0] * 8
            ns.v_exp = 0
            ns.v_exp_slot = {0: 0, 1: 0}
            ns.v_out2 = {0: 0, 1: 0, 2: 0, 3: 0}
            ns.v_gate_exp = 0
            ns.v_gatepost = 0
            ns.v_q_evac = 0
            ns.v_sq2 = 0
            ns.v_p2 = 0
            ns.v_resid = 0
            ns.v_x1 = 0
            ns.v_ln1 = 0
            ns.m_xres = 0
            ns.m_xts = 0
            ns.m_xt2 = 0
            ns.m_atok = 0
            ns.m_x1T = 0
            ns.m_cast = 0
            ns.mark_ffn = mark_p1mm
            ns.mark_gate = {}
            ns.v_lnlast = 0
            ns.avsig = {}
            ns.v_zfin_par = {0: 0, 1: 0}
            ns.proj_sel = 0
            ns.v_ve1 = 0
            ns.v_ve2 = 0
            ns.v_x1T = 0
            ns.v_gate_fix = 0
            ns.v_ccp = 0
            ns.evacs = {}
            ns.ev_lhsT = 0

            def ln_stats_a(SUMS_, SSQ_, NEGM_, dve_dep):
                va = vop("tensor_scalar", col(NEGM_, 4), col(SUMS_, 4),
                         -1.0 / D, None, ALU.mult,
                         dep=max(dve_dep, ns.v_lnlast))
                vb = vop("tensor_scalar", col(T1, 4), col(SSQ_, 4),
                         1.0 / D, None, ALU.mult, dep=va)
                vc = vop("tensor_tensor", col(T2, 4), col(NEGM_, 4),
                         col(NEGM_, 4), ALU.mult, dep=va)
                vd = vop("tensor_tensor", col(T1, 4), col(T1, 4), col(T2, 4),
                         ALU.subtract, dep=vc)
                ve = vop("tensor_scalar", col(T1, 4), col(T1, 4), 1e-5, None,
                         ALU.add, dep=vd)
                return ve

            def ln_stats_b(RSTD_, ve):
                wait("act", "dve", ve)
                vf = aop("activation", col(T2, 4), col(T1, 4), AF.Ln, 0.0, 1.0,
                         serial=False)
                vg = aop("activation", col(T3, 4), col(T2, 4), AF.Exp, 0.0, -0.5,
                         dep=vf)
                wait("dve", "act", vg)
                vh = vop("tensor_tensor", col(T2, 4), col(T3, 4), col(T3, 4),
                         ALU.mult, dep=ve)
                vi = vop("tensor_tensor", col(T2, 4), col(T2, 4), col(T1, 4),
                         ALU.mult, dep=vh)
                vj = vop("tensor_scalar", col(T2, 4), col(T2, 4), -0.5, 1.5,
                         ALU.mult, ALU.add, dep=vi)
                ns.v_lnlast = vop("tensor_tensor", col(RSTD_, 4), col(T3, 4),
                                  col(T2, 4), ALU.mult, dep=vj)
                return ns.v_lnlast

            def ln_stats(SUMS_, SSQ_, RSTD_, NEGM_, dve_dep):
                return ln_stats_b(RSTD_, ln_stats_a(SUMS_, SSQ_, NEGM_, dve_dep))

            def k64phys(i, fh):
                return 3 + (fh % 2) if i == 3 else i

            def xts_load(fh):
                wait("sync", "pe", ns.mark_proj)
                wait("act", "pe", ns.mark_proj)
                wait_full("act", "gbf")
                wait_full("act", "gb2")
                for dt in range(8):
                    if dt % 2 == 0:
                        ns.m_xts = tdma("xts", xts[:, dt, :],
                                        xbf_h[S * fh:S * (fh + 1),
                                              128 * dt:128 * (dt + 1)])
                    else:
                        ns.m_xt2 = tdma("xt2", xts[:, dt, :],
                                        xbf_h[S * fh:S * (fh + 1),
                                              128 * dt:128 * (dt + 1)])

            def proj_group(dest_ap, kind, pt, ekey=None):
                sel = ns.proj_sel
                ns.proj_sel ^= 1
                psel = ps_d if sel == 0 else ps_e
                wait("pe", "xts", ns.m_xts)
                wait("pe", "xt2", ns.m_xt2)
                wait("pe", "dve", prev_evac[sel])
                wait("pe", "dve", ns.v_p2)
                wait("pe", "dve", ns.v_gate_fix)
                if kind == "q":
                    for dt in range(8):
                        mm(psel[:, :], mq_bf[:, dt, 128 * pt:128 * (pt + 1)],
                           xts[:, dt, :], start=(dt == 0), stop=(dt == 7))
                elif kind == "k":
                    for dt in range(8):
                        mm(psel[:, :], mk_bf[:, dt, 128 * pt:128 * (pt + 1)],
                           xts[:, dt, :], start=(dt == 0), stop=(dt == 7))
                else:
                    c0 = 64 + 128 * pt
                    if pt < 3:
                        for dt in range(8):
                            mm(psel[:, :], mk_bf[:, dt, c0:c0 + 128],
                               xts[:, dt, :], start=(dt == 0), stop=(dt == 7))
                    else:
                        for dt in range(8):
                            mm(psel[0:64, :], mk_bf[:, dt, 448:512],
                               xts[:, dt, :], start=(dt == 0), stop=(dt == 7))
                        for dt in range(8):
                            mm(psel[64:128, :], mk_bf[:, dt, 0:64],
                               xts[:, dt, :], start=(dt == 0), stop=(dt == 7),
                               tile_position=(0, 64))
                v = sig("pe")
                prev_mm[sel] = v
                wait("dve", "pe", v)
                prev_evac[sel] = vop("tensor_copy", dest_ap, psel[:, :])
                ns.v_q_evac = prev_evac[sel]
                if ekey is not None:
                    ns.evacs[ekey] = prev_evac[sel]

            def proj_sec0(fh):
                for pt in range(4):
                    proj_group(qs[:, fh % 2, pt, :], "q", pt, ("q", fh % 2, pt))
                proj_group(ks[:, 0, :], "k", 0, ("k", 0))
                proj_group(ks64[:, k64phys(3, fh), :], "k64", 3,
                           ("k64", k64phys(3, fh)))
                proj_group(ks64[:, 0, :], "k64", 0, ("k64", 0))

            def gate_chunk(f, tt):
                if tt == 0:
                    wait("pe", "x1t", ns.m_x1T)
                    wait("pe", "dve", ns.v_x1T)
                    wait("pe", "dve", prev_evac[0])
                    wait("pe", "dve", prev_evac[1])
                    wait("pe", "dve", ns.v_p2)
                else:
                    wait("pe", "dve", ns.v_gate_fix)
                psel = ps_d if tt % 2 == 0 else ps_e
                co = 8 * (tt // 2)
                for dt in range(8):
                    mm(psel[:, co:co + 7],
                       x1T[:, dt, 128 * tt:128 * (tt + 1)],
                       wg_bf[:, dt, :], start=(dt == 0), stop=(dt == 7))
                ns.mark_gate[tt] = sig("pe")
                wait("dve", "pe", ns.mark_gate[tt])
                if tt == 0:
                    wait("dve", "act", ns.v_gate_exp)
                va = vop("tensor_scalar", gate_t[:, :], psel[:, co:co + 7],
                         col(RSTD + tt), None, ALU.mult,
                         dep=ns.v_x1 if tt == 0 else 0)
                ns.v_gate_fix = vop("scalar_tensor_tensor", gate_p4[:, tt, :],
                                    gate_c[:, :], col(Q + tt), gate_t[:, :],
                                    ALU.mult, ALU.add, dep=va)
                wait("act", "dve", ns.v_gate_fix)
                ns.v_gate_exp = aop("activation", gate_e4[:, tt, :],
                                    gate_p4[:, tt, :], AF.Exp, 0.0, 1.0,
                                    serial=False)

            def ffn_sh(f, tt, jc):
                if tt == 0 and jc == 0:
                    wait_full("pe", "wT")
                    wait("pe", "dve", ns.v_gate_fix)
                    wait("pe", "dve", prev_evac[0])
                wait("pe", "dve", ns.v_p2)
                for dt in range(8):
                    mm(ps_d[:, :], x1T[:, dt, 128 * tt:128 * (tt + 1)],
                       wshT[:, dt, 512 * jc:512 * (jc + 1)],
                       start=(dt == 0), stop=False)
                for cb in range(4):
                    mm(ps_d[:, 128 * cb:128 * (cb + 1)],
                       x1T[:, 4 * jc + cb, 128 * tt:128 * (tt + 1)],
                       ident[:, 0, :], start=False, stop=(cb == 3))

            def ffn_ex(f, tt, jc):
                if tt == 0 and jc == 0:
                    wait("pe", "dve", prev_evac[1])
                for dt in range(8):
                    mm(ps_e[:, :], x1T[:, dt, 128 * tt:128 * (tt + 1)],
                       wexT[:, dt, 512 * jc:512 * (jc + 1)],
                       start=(dt == 0), stop=(dt == 7))
                ns.mark_ffn = sig("pe")
                hs = slice(512 * jc, 512 * (jc + 1))
                wait("dve", "pe", ns.mark_ffn)
                if tt == 0 and jc == 0:
                    wait_full("dve", "out")
                if jc == 0:
                    vq = vop("tensor_tensor", col(QW + tt), col(Q + tt),
                             col(WSUM + tt), ALU.mult,
                             dep=max(ns.v_gatepost, ns.v_x1))
                    vq = vop("tensor_tensor", col(RW + tt), col(RSTD + tt),
                             col(WSUM + tt), ALU.mult, dep=vq)
                else:
                    vq = 0
                c1 = vop("scalar_tensor_tensor", r_buf[:, tt, hs],
                         csh_bc[:, hs], col(Q + tt), bias_bc[:, hs],
                         ALU.mult, ALU.add, dep=vq)
                c2 = vop("scalar_tensor_tensor", r_buf[:, tt, hs],
                         cex_bc[:, hs], col(QW + tt), r_buf[:, tt, hs],
                         ALU.mult, ALU.add, dep=c1)
                v1 = vop("scalar_tensor_tensor", xres[:, tt, hs],
                         ps_d[:, :], col(RSTD + tt), r_buf[:, tt, hs],
                         ALU.mult, ALU.add, dep=c2)
                ns.v_p2 = vop("scalar_tensor_tensor", xres[:, tt, hs],
                              ps_e[:, :], col(RW + tt), xres[:, tt, hs],
                              ALU.mult, ALU.add,
                              accum_out=col((SUMS2 if jc == 0 else SUMS2B) + tt),
                              dep=v1)
                if jc == 1:
                    ns.v_out2[tt] = ns.v_p2
                    ns.v_sq2 = vop("scalar_tensor_tensor", atok[:, 0, :],
                                   xres[:, tt, :], 1.0, xres[:, tt, :],
                                   ALU.mult, ALU.mult, accum_out=col(SSQ2 + tt),
                                   dep=ns.v_p2)

            def c_piece(i):
                onesb = atok[:, 0, 0:1]
                crow = r_buf[0:1, 0, 0:512]
                ones_row = r_buf[0:1, 1, 0:128]
                if i == 0:
                    wait_full("dve", "xvw")
                    v1 = vop("memset", onesb, 1.0)
                    ns.v_ccp = vop("memset", ones_row, 1.0, dep=v1)
                    wait("pe", "dve", prev_evac[0])
                    wait("pe", "dve", prev_evac[1])
                    wait_full("pe", "wT")
                if i < 4:
                    w_sb = wshT if i < 2 else wexT
                    c_dst = csh_bc if i < 2 else cex_bc
                    jc = i % 2
                    wait("pe", "dve", ns.v_ccp)
                    for dt in range(8):
                        mm(ps_d[0:1, :], onesb,
                           w_sb[:, dt, 512 * jc:512 * (jc + 1)],
                           start=(dt == 0), stop=(dt == 7))
                    v = sig("pe")
                    wait("dve", "pe", v)
                    ns.v_ccp = vop("tensor_copy", crow, ps_d[0:1, :])
                    wait("pe", "dve", ns.v_ccp)
                    mm(ps_e[:, :], ones_row, crow, start=True, stop=True)
                    v = sig("pe")
                    wait("dve", "pe", v)
                    ns.v_ccp = vop("tensor_copy",
                                   c_dst[:, 512 * jc:512 * (jc + 1)], ps_e[:, :])
                    if i < 2:
                        ns.v_ccp = vop("tensor_scalar",
                                       c_dst[:, 512 * jc:512 * (jc + 1)],
                                       c_dst[:, 512 * jc:512 * (jc + 1)],
                                       1.0, None, ALU.add, dep=ns.v_ccp)
                else:
                    wait("pe", "dve", ns.v_ccp)
                    for dt in range(8):
                        mm(ps_d[0:1, 0:7], onesb, wg_bf[:, dt, :],
                           start=(dt == 0), stop=(dt == 7))
                    v = sig("pe")
                    wait("dve", "pe", v)
                    ns.v_ccp = vop("tensor_copy", crow[:, 0:7], ps_d[0:1, 0:7])
                    wait("pe", "dve", ns.v_ccp)
                    mm(ps_e[:, 0:7], ones_row, crow[:, 0:7],
                       start=True, stop=True)
                    v = sig("pe")
                    wait("dve", "pe", v)
                    ns.v_ccp = vop("tensor_copy", gate_c[:, :], ps_e[:, 0:7])
                    prev_evac[0] = ns.v_ccp
                    prev_evac[1] = ns.v_ccp

            def seg_for(f, g2):
                segs = [None] * 5
                if f == 0 and g2 == 4:
                    for i in range(4):
                        segs[i] = (lambda j: lambda: c_piece(j))(i)
                    return segs
                if f == 0 and g2 == 5:
                    segs[0] = lambda: c_piece(4)
                    return segs
                if g2 == 0:
                    segs[0] = lambda: proj_group(ks[:, 1, :], "k", 1, ("k", 1))
                    segs[1] = lambda: proj_group(ks[:, 2, :], "k", 2, ("k", 2))
                    segs[2] = lambda: proj_group(ks[:, 3, :], "k", 3, ("k", 3))
                    segs[3] = lambda: proj_group(ks64[:, 1, :], "k64", 1,
                                                 ("k64", 1))

                    def c4():
                        proj_group(ks64[:, 2, :], "k64", 2, ("k64", 2))
                        ns.mark_proj = max(prev_mm[0], prev_mm[1])
                        if f < 7:
                            xts_load(f + 1)
                    segs[4] = c4
                elif g2 == 1 and f < 7:
                    pi = (f + 1) % 2
                    segs[2] = lambda: proj_group(qs[:, pi, 0, :], "q", 0,
                                                 ("q", pi, 0))
                    segs[3] = lambda: proj_group(qs[:, pi, 1, :], "q", 1,
                                                 ("q", pi, 1))
                elif g2 == 2 and f < 7:
                    pi = (f + 1) % 2
                    segs[0] = lambda: proj_group(qs[:, pi, 2, :], "q", 2,
                                                 ("q", pi, 2))
                    segs[1] = lambda: proj_group(qs[:, pi, 3, :], "q", 3,
                                                 ("q", pi, 3))
                    segs[2] = lambda: proj_group(ks[:, 0, :], "k", 0, ("k", 0))
                    segs[3] = lambda: proj_group(
                        ks64[:, k64phys(3, f + 1), :], "k64", 3,
                        ("k64", k64phys(3, f + 1)))
                elif g2 == 3:
                    if f >= 1:
                        for i in range(4):
                            segs[i] = (lambda tt: lambda: gate_chunk(
                                f - 1, tt))(i)
                    if f < 7:
                        segs[4] = lambda: proj_group(ks64[:, 0, :], "k64", 0,
                                                     ("k64", 0))
                elif g2 >= 4 and f >= 1:
                    tt = g2 - 4
                    segs[0] = lambda: ffn_sh(f - 1, tt, 0)
                    segs[1] = lambda: ffn_ex(f - 1, tt, 0)
                    segs[2] = lambda: ffn_sh(f - 1, tt, 1)
                    segs[3] = lambda: ffn_ex(f - 1, tt, 1)
                return segs

            def epilogue1(f):
                wait("gp", "dve", max(ns.v_avadd[0], ns.v_avadd[1]))
                for hp in range(8):
                    ns.m_cast = dma("gp", "cast", att_h[128 * hp:128 * (hp + 1), :],
                                    att_acc[:, hp, :])
                epilogue1b(f)

            def epilogue1b(f):
                wait_full("sync", "cast")
                wait("sync", "dve", ns.v_sq2)
                for tt in range(4):
                    ns.m_atok = tdma("atok", atok[:, tt, :],
                                     att_h[:, 128 * tt:128 * (tt + 1)])

            def ep2a(f):
                wait_full("dve", "out")
                for tt in range(4):
                    if tt == 0:
                        wait("dve", "atok", ns.m_atok)
                        wait("dve", "xres", ns.m_xres)
                    ns.v_resid = vop("scalar_tensor_tensor", r_buf[:, tt, :],
                                     atok[:, tt, :], 1.0, xres[:, tt, :],
                                     ALU.mult, ALU.add, accum_out=col(SUMS + tt),
                                     dep=ns.v_out2[3] if tt == 0 else 0)
                v_sq = ns.v_resid
                for tt in range(4):
                    v_sq = vop("scalar_tensor_tensor", atok[:, 0, :],
                               r_buf[:, tt, :], 1.0, r_buf[:, tt, :],
                               ALU.mult, ALU.mult, accum_out=col(SSQ + tt),
                               dep=v_sq)
                ns.v_ve1 = ln_stats_a(SUMS, SSQ, NEGM, v_sq)

            def ep2b(f):
                ns.v_ln1 = ln_stats_b(RSTD, ns.v_ve1)
                ns.v_x1 = vop("tensor_tensor", col(Q, 4), col(RSTD, 4),
                              col(NEGM, 4), ALU.mult, dep=ns.v_ln1)

            def rt_issue(f):
                wait("sync", "pe", ns.mark_ffn)
                for dt in range(8):
                    ns.m_x1T = tdma("x1t", x1T[:, dt, :],
                                    xbf_h[S * f:S * (f + 1),
                                          128 * dt:128 * (dt + 1)])

            def rt_add(f):
                wait("dve", "x1t", ns.m_x1T)
                v = 0
                for dt in range(8):
                    v = vop("tensor_tensor", x1T[:, dt, :], x1T[:, dt, :],
                            att_acc[:, dt, :], ALU.add,
                            dep=max(ns.v_avadd[0], ns.v_avadd[1])
                            if dt == 0 else 0)
                ns.v_x1T = v

            def gate_post(f, tt):
                wait("dve", "act", ns.v_gate_exp)
                vc = vop("reduce_max", col(G_M1N), gate_e4[:, tt, :], AX.X,
                         negate=True, dep=ns.v_gatepost)
                vd = vop("tensor_scalar", gate_t[:, :], gate_e4[:, tt, :],
                         col(G_M1N), None, ALU.add, dep=vc)
                ve = vop("tensor_scalar", gate_t[:, :], gate_t[:, :], 0.0,
                         None, ALU.is_equal, dep=vd)
                vf = vop("scalar_tensor_tensor", gate_t[:, :], gate_t[:, :],
                         -1e30, gate_e4[:, tt, :], ALU.mult, ALU.add, dep=ve)
                vg = vop("reduce_max", col(G_M2N), gate_t[:, :], AX.X,
                         negate=True, dep=vf)
                vh = vop("tensor_tensor", col(G_S), col(G_M1N), col(G_M2N),
                         ALU.add, dep=vg)
                vi = vop("reciprocal", col(G_S), col(G_S), dep=vh)
                vj = vop("tensor_tensor", col(G_A), col(G_M1N), col(G_S),
                         ALU.mult, dep=vi)
                vk = vop("tensor_tensor", col(G_B), col(G_M2N), col(G_S),
                         ALU.mult, dep=vj)
                ns.v_gatepost = vop("tensor_tensor", col(WSUM + tt), col(G_A),
                                    col(G_B), ALU.add, dep=vk)

            def ln2a(f):
                vs2 = vop("tensor_tensor", col(SUMS2, 4), col(SUMS2, 4),
                          col(SUMS2B, 4), ALU.add,
                          dep=max(ns.v_sq2, ns.v_lnlast))
                ns.v_ve2 = ln_stats_a(SUMS2, SSQ2, NEGM2, vs2)

            def ln2b(f):
                v_ln2 = ln_stats_b(RSTD2, ns.v_ve2)
                v_final = 0
                oview = out_h[S * f:S * (f + 1), :].rearrange(
                    "(t p) d -> p t d", p=128)
                for tt in range(4):
                    v_final = vop("tensor_scalar", r_buf[:, tt, :], xres[:, tt, :],
                                  col(NEGM2 + tt), col(RSTD2 + tt), ALU.add,
                                  ALU.mult, dep=v_ln2 if tt == 0 else 0)
                    wait("sync", "dve", v_final)
                    dma("sync", "out", oview[:, tt:tt + 1, :],
                        r_buf[:, tt:tt + 1, :])
                if f + 1 < 8:
                    ns.m_xres = dma("sync", "xres", xres[:, :, :],
                                    x_h[S * (f + 1):S * (f + 2), :]
                                    .rearrange("(t p) d -> p t d", p=128))

            def av_pair(f, g2p, jj):
                a = f * 8 + g2p
                for hp in (2 * jj, 2 * jj + 1):
                    half = hp % 2
                    slot = (8 * g2p + hp) % 10
                    wait("pe", "dve", ns.v_scale)
                    wait("pe", "dve", ns.v_avadd[half])
                    for c2t in range(4):
                        mm(ps_av[:, 512 * half:512 * (half + 1)],
                           xv_sl[:, 4 * hp + c2t, :], A_buf[:, c2t, slot, :],
                           start=(c2t == 0), stop=(c2t == 3))
                    v_g = sig("pe")
                    ns.mark_av_hp[hp] = v_g
                    ns.mark_av_all = v_g
                    wait("dve", "pe", v_g)
                    if g2p == 0:
                        wait_full("dve", "cast")
                        ns.v_avadd[half] = vop(
                            "tensor_copy", att_acc[:, hp, :],
                            ps_av[:, 512 * half:512 * (half + 1)],
                            dep=ns.v_x1T)
                    else:
                        ns.v_avadd[half] = vop(
                            "tensor_tensor", att_acc[:, hp, :], att_acc[:, hp, :],
                            ps_av[:, 512 * half:512 * (half + 1)], ALU.add,
                            dep=ns.v_avadd[half])
                ns.avsig[(a, jj)] = ns.mark_av_all

            def s_g2(f, g2, segs):
                wait("gp", "dve", ns.v_scale)
                if f == 0 and g2 == 0:
                    wait_full("gp", "xvw")
                m_xv = dma("gp", "xvr", xv_raw[:, :, :],
                           xv_h[g2, :, :].rearrange("(t p) v -> p t v", p=128))
                gm = (g2 - 1) % 8
                pi = f % 2
                if g2 % 2 == 0:
                    lhsT0s = ks[0:64, g2 // 2, :]
                    lhsT64s = ks64[64:128, k64phys(gm // 2, f), :]
                else:
                    lhsT0s = ks64[0:64, k64phys(gm // 2, f), :]
                    lhsT64s = ks[64:128, g2 // 2, :]
                ns.ev_lhsT = max(ns.evacs.get(("k", g2 // 2), 0),
                                 ns.evacs.get(("k64", k64phys(gm // 2, f)), 0))
                sub_i = 0
                a_cur = f * 8 + g2
                par = g2 % 2
                if g2 >= 1:
                    wave = (f, g2 - 1)
                    pair_order = (0, 1, 2, 3)
                    war_keys = ((a_cur - 2, 3), (a_cur - 1, 0),
                                (a_cur - 1, 1), (a_cur - 1, 2))
                elif f >= 1:
                    wave = (f - 1, 7)
                    pair_order = (2, 3, 0, 1)
                    war_keys = ((a_cur - 1, 2), (a_cur - 1, 3),
                                (a_cur - 2, 3), (a_cur - 1, 0))
                else:
                    wave = None
                    war_keys = ((-1, 0),) * 4
                def do_av(p):
                    av_pair(wave[0], wave[1], p)
                    if wave[1] == 7:
                        for hp in (2 * p, 2 * p + 1):
                            wait("gp", "dve", ns.v_avadd[hp % 2])
                            ns.m_cast = dma("gp", "cast",
                                            att_h[128 * hp:128 * (hp + 1), :],
                                            att_acc[:, hp, :])

                for jj in range(4):
                    av_late = not (g2 == 0 and jj < 2)
                    if wave is not None and not av_late:
                        do_av(pair_order[jj])
                    sg = (8 * g2 + 2 * jj) % 10
                    v_war = ns.avsig.get(war_keys[jj], 0)
                    for c2t in range(4):
                        cs = slice(128 * c2t, 128 * (c2t + 1))
                        sub = sub_i % 2
                        sub_i += 1
                        wait("pe", "act", ns.v_exp_slot[sub])
                        if c2t == 0:
                            if jj == 0:
                                wait("pe", "dve", ns.ev_lhsT)
                            wait("pe", "dve",
                                 ns.evacs.get(("q", pi, jj), 0))
                        b0 = 1024 * sub
                        mm(ps_a[:, b0:b0 + 512],
                           lhsT0s[:, cs], qs[0:64, pi, jj, :],
                           start=True, stop=True, tile_position=(0, 0))
                        mm(ps_a[:, b0 + 512:b0 + 1024],
                           lhsT64s[:, cs], qs[64:128, pi, jj, :],
                           start=True, stop=True, tile_position=(64, 0))
                        v_slot = sig("pe")
                        wait("act", "pe", v_slot)
                        if c2t == 0:
                            wait("act", "pe", v_war)
                            if jj == 0:
                                wait("act", "dve", ns.v_zfin_par[par])
                        ns.v_exp = aop("activation",
                                       A_buf[:, c2t, sg:sg + 2, :],
                                       ps_a[:, b0:b0 + 1024], AF.Exp, 0.0, 0.125,
                                       accum_out=zacc[:, c2t, jj, par:par + 1],
                                       serial=False)
                        ns.v_exp_slot[sub] = ns.v_exp
                    if wave is not None and av_late:
                        do_av(pair_order[jj])
                    if jj == 3:
                        wait("dve", "act", ns.v_exp)
                        v1 = vop("tensor_tensor", zinv[:, :], zacc[:, :, 0, par],
                                 zacc[:, :, 1, par], ALU.add, dep=ns.v_scale)
                        v1 = vop("tensor_tensor", col(60, 4), zacc[:, :, 2, par],
                                 zacc[:, :, 3, par], ALU.add, dep=v1)
                        v1 = vop("tensor_tensor", zinv[:, :], zinv[:, :],
                                 col(60, 4), ALU.add, dep=v1)
                        ns.v_zfin_par[par] = vop("reciprocal", zinv[:, :],
                                                 zinv[:, :], dep=v1)
                        wait("dve", "xvr", m_xv)
                        wait("dve", "pe", ns.mark_av_all)
                        xvr4 = xv_raw[:, :, :].rearrange("p (a b) v -> p b a v", b=4)
                        xvs4 = xv_sl[:, :, :].rearrange("p (a b) v -> p b a v", b=4)
                        for c2t in range(4):
                            ns.v_scale = vop(
                                "tensor_scalar", xvs4[:, c2t, :, :],
                                xvr4[:, c2t, :, :], zinv[:, c2t:c2t + 1],
                                None, ALU.mult,
                                dep=ns.v_zfin_par[par] if c2t == 0 else 0)
                    if segs[jj] is not None:
                        segs[jj]()
                if segs[4] is not None:
                    segs[4]()

            ns.m_xres = dma("sync", "xres", xres[:, :, :],
                            x_h[0:S, :].rearrange("(t p) d -> p t d", p=128))
            xts_load(0)
            wait("pe", "dve", v_wcp)
            proj_sec0(0)
            for f in range(8):
                for g2 in range(8):
                    s_g2(f, g2, seg_for(f, g2))
                    if f == 0 and g2 == 2:
                        wait_full("sync", "gw")
                        for dt in range(8):
                            tdma("wT", wshT[:, dt, :],
                                 wshbf_h[:, 128 * dt:128 * (dt + 1)])
                        for dt in range(8):
                            tdma("wT", wexT[:, dt, :],
                                 wexbf_h[:, 128 * dt:128 * (dt + 1)])
                    if g2 == 7:
                        if f >= 1:
                            ln2a(f - 1)
                        rt_issue(f)
                    elif f >= 1:
                        if g2 == 0:
                            rt_add(f - 1)
                            if f >= 2:
                                ln2b(f - 2)
                        elif g2 == 1:
                            epilogue1b(f - 1)
                            ep2a(f - 1)
                        elif g2 == 2:
                            ep2b(f - 1)
                        elif 3 <= g2 <= 6:
                            gate_post(f - 1, g2 - 3)
            ln2b(6)
            for p in range(4):
                av_pair(7, 7, p)
            epilogue1(7)
            rt_add(7)
            ep2a(7)
            ep2b(7)
            for tt in range(4):
                gate_chunk(7, tt)
            for tt in range(4):
                gate_post(7, tt)
                ffn_sh(7, tt, 0)
                ffn_ex(7, tt, 0)
                ffn_sh(7, tt, 1)
                ffn_ex(7, tt, 1)
            ln2a(7)
            ln2b(7)

            if taps:
                wait_full("sync", "out")
                dma("sync", "out", taps["o_xv"][:, :, :], xv_h[:, :, :])
                dma("sync", "out", taps["o_attacc"][:, :],
                    att_acc[:, :, :].rearrange("p a b -> p (a b)"))
                dma("sync", "out", taps["o_A"][:, :],
                    A_buf[:, :, :, :].rearrange("p a b c -> p (a b c)"))
                dma("sync", "out", taps["o_zinv"][:, :], zinv[:, :])
                dma("sync", "out", taps["o_qs"][:, :],
                    qs[:, :, :, :].rearrange("p a b c -> p (a b c)"))
                dma("sync", "out", taps["o_ks"][:, :],
                    ks[:, :, :].rearrange("p a b -> p (a b)"))
                dma("sync", "out", taps["o_q64"][:, :],
                    ks64[:, :, :].rearrange("p a b -> p (a b)"))
                dma("sync", "out", taps["o_stat"][:, :], stat[:, :])
                dma("sync", "out", taps["o_xbf"][:, :], xbf_h[:, :])

            wait_full("sync", "out")
            wait_full("gp", "out")

        block.sync(lambda e: prog("sync", e))
        block.gpsimd(lambda e: prog("gp", e))
        block.tensor(lambda e: prog("pe", e))
        block.scalar(lambda e: prog("act", e))
        block.vector(lambda e: prog("dve", e))

        nc.all_engine_barrier()
    return nc


_NC_CACHE = None


def kernel(L0_attn_x, L0_attn_Mq, L0_attn_Mk, L0_attn_Mv, L0_ffn_cond=None,
           L0_ffn_mask=None, L0_ffn_W_shared=None, L0_ffn_W_expert=None,
           L0_ffn_W_gate=None, L0_ffn_bias=None, **_unused):
    global _NC_CACHE
    x = np.ascontiguousarray(np.asarray(L0_attn_x, dtype=np.float32))
    mq = np.ascontiguousarray(np.asarray(L0_attn_Mq, dtype=np.float32))
    mk = np.ascontiguousarray(np.asarray(L0_attn_Mk, dtype=np.float32))
    mv = np.ascontiguousarray(np.asarray(L0_attn_Mv, dtype=np.float32))
    wsh = np.ascontiguousarray(np.asarray(L0_ffn_W_shared, dtype=np.float32))
    wex = np.ascontiguousarray(np.asarray(L0_ffn_W_expert, dtype=np.float32))
    wg = np.ascontiguousarray(np.asarray(L0_ffn_W_gate, dtype=np.float32))
    bias = np.ascontiguousarray(np.asarray(L0_ffn_bias, dtype=np.float32).reshape(1, D))

    if _NC_CACHE is None:
        _NC_CACHE = build_program()
    nc = _NC_CACHE
    in_maps = [{"x": x[b], "mq": mq, "mk": mk, "mv": mv, "wsh": wsh,
                "wex": wex, "wg": wg, "bias": bias} for b in range(NCORES)]
    res = run_bass_kernel_spmd(nc, in_maps, core_ids=list(range(NCORES)))
    out = np.stack([np.asarray(res.results[b]["out"]) for b in range(NCORES)], axis=0)
    return out.astype(np.float32)


# version 40
# speedup vs baseline: 1.0893x; 1.0037x over previous
import numpy as np
import concourse.bass as bass
import concourse.mybir as mybir
from concourse.bass_utils import run_bass_kernel_spmd

F32 = mybir.dt.float32
BF16 = mybir.dt.bfloat16
AF = mybir.ActivationFunctionType
ALU = mybir.AluOpType
AX = mybir.AxisListType

C, D, H, S, KD, V = 4096, 1024, 8, 512, 64, 128
NCORES = 8

DMA_SEMS = ("ld", "gbf", "gb2", "gw", "wT", "xts", "xt2", "xvw", "xvr",
            "cast", "xres", "atok", "x1w", "x1t", "out")
ENG_SEMS = ("pe", "act", "dve", "gpc")


def build_program(debug_taps=False):
    nc = bass.Bass("TRN2", target_bir_lowering=False, dynamic_dma_scratch_size=16384)

    x_h = nc.dram_tensor("x", [C, D], F32, kind="ExternalInput").ap()
    mq_h = nc.dram_tensor("mq", [D, 512], F32, kind="ExternalInput").ap()
    mk_h = nc.dram_tensor("mk", [D, 512], F32, kind="ExternalInput").ap()
    mv_h = nc.dram_tensor("mv", [D, D], F32, kind="ExternalInput").ap()
    wsh_h = nc.dram_tensor("wsh", [D, D], F32, kind="ExternalInput").ap()
    wex_h = nc.dram_tensor("wex", [D, D], F32, kind="ExternalInput").ap()
    wg_h = nc.dram_tensor("wg", [D, 7], F32, kind="ExternalInput").ap()
    bias_h = nc.dram_tensor("bias", [1, D], F32, kind="ExternalInput").ap()
    out_h = nc.dram_tensor("out", [C, D], F32, kind="ExternalOutput").ap()
    xbf_h = nc.dram_tensor("xbf", [C, D], BF16, kind="Internal").ap()
    xv_h = nc.dram_tensor("xvh", [H, C, V], BF16, kind="Internal").ap()
    att_h = nc.dram_tensor("atth", [D, S], BF16, kind="Internal").ap()
    x1_h = nc.dram_tensor("x1h", [S, D], BF16, kind="Internal").ap()
    ident_h = nc.dram_tensor("identh", [1, 16640], BF16, kind="Internal").ap()
    wshbf_h = nc.dram_tensor("wshbf", [D, D], BF16, kind="Internal").ap()
    wexbf_h = nc.dram_tensor("wexbf", [D, D], BF16, kind="Internal").ap()
    taps = {}
    if debug_taps:
        taps["o_xv"] = nc.dram_tensor("o_xv", [H, C, V], BF16, kind="ExternalOutput").ap()
        taps["o_attacc"] = nc.dram_tensor("o_attacc", [128, 8 * 512], F32, kind="ExternalOutput").ap()
        taps["o_A"] = nc.dram_tensor("o_A", [128, 4 * 10 * 512], BF16, kind="ExternalOutput").ap()
        taps["o_zinv"] = nc.dram_tensor("o_zinv", [128, 4], F32, kind="ExternalOutput").ap()
        taps["o_qs"] = nc.dram_tensor("o_qs", [128, 2 * 4 * 512], BF16, kind="ExternalOutput").ap()
        taps["o_ks"] = nc.dram_tensor("o_ks", [128, 4 * 512], BF16, kind="ExternalOutput").ap()
        taps["o_q64"] = nc.dram_tensor("o_q64", [128, 5 * 512], BF16, kind="ExternalOutput").ap()
        taps["o_stat"] = nc.dram_tensor("o_stat", [128, 80], F32, kind="ExternalOutput").ap()
        taps["o_xbf"] = nc.dram_tensor("o_xbf", [C, D], BF16, kind="ExternalOutput").ap()

    from contextlib import ExitStack
    with ExitStack() as ctx:
        E = ctx.enter_context
        mq_bf = E(nc.sbuf_tensor("mq_bf", [128, 8, 512], BF16))
        mk_bf = E(nc.sbuf_tensor("mk_bf", [128, 8, 512], BF16))
        wshT = E(nc.sbuf_tensor("wshT", [128, 8, 1024], BF16))
        wexT = E(nc.sbuf_tensor("wexT", [128, 8, 1024], BF16))
        wg_bf = E(nc.sbuf_tensor("wg_bf", [128, 8, 7], BF16))
        bias_bc = E(nc.sbuf_tensor("bias_bc", [128, D], F32))
        A_buf = E(nc.sbuf_tensor("A_buf", [128, 4, 10, 512], BF16))
        qs = E(nc.sbuf_tensor("qs", [128, 2, 4, 512], BF16))
        ks = E(nc.sbuf_tensor("ks", [128, 4, 512], BF16))
        ks64 = E(nc.sbuf_tensor("ks64", [128, 5, 512], BF16))
        xts = E(nc.sbuf_tensor("xts", [128, 8, 512], BF16))
        xv_raw = E(nc.sbuf_tensor("xv_raw", [128, 32, 128], BF16))
        xv_sl = E(nc.sbuf_tensor("xv_sl", [128, 32, 128], BF16))
        att_acc = E(nc.sbuf_tensor("att_acc", [128, 8, 512], F32))
        atok = E(nc.sbuf_tensor("atok", [128, 4, 1024], BF16))
        x1T = E(nc.sbuf_tensor("x1T", [128, 8, 512], BF16))
        xres = E(nc.sbuf_tensor("xres", [128, 4, 1024], F32))
        r_buf = E(nc.sbuf_tensor("r_buf", [128, 4, 1024], F32))
        zacc = E(nc.sbuf_tensor("zacc", [128, 4, 4, 2], F32))
        zinv = E(nc.sbuf_tensor("zinv", [128, 4], F32))
        stat = E(nc.sbuf_tensor("stat", [128, 80], F32))
        gate_t = E(nc.sbuf_tensor("gate_t", [128, 7], F32))
        gate_p4 = E(nc.sbuf_tensor("gate_p4", [128, 4, 7], F32))
        gate_e4 = E(nc.sbuf_tensor("gate_e4", [128, 4, 7], F32))
        gate_c = E(nc.sbuf_tensor("gate_c", [128, 7], F32))
        csh_bc = E(nc.sbuf_tensor("csh_bc", [128, 1024], BF16))
        cex_bc = E(nc.sbuf_tensor("cex_bc", [128, 1024], BF16))
        ident = E(nc.sbuf_tensor("ident", [128, 1, 128], BF16))
        ps_a = E(nc.psum_tensor("ps_a", [128, 2048], F32))
        ps_av = E(nc.psum_tensor("ps_av", [128, 1024], F32))
        ps_d = E(nc.psum_tensor("ps_d", [128, 512], F32))
        ps_e = E(nc.psum_tensor("ps_e", [128, 512], F32))
        sems = {}
        for name in DMA_SEMS + ENG_SEMS:
            sems[name] = E(nc.semaphore("s_" + name))
        block = E(nc.Block())

        stage_xv = atok[:, :, :].rearrange("p t (g v) -> p t g v", g=8)
        st = stat
        SUMS, SSQ, RSTD, NEGM = 0, 4, 8, 12
        SUMS2, SUMS2B, SSQ2, RSTD2, NEGM2, WSUM = 16, 20, 24, 28, 32, 36
        T1, T2, T3 = 40, 44, 48
        G_NM, G_ES, G_RS, G_M1N, G_M2N, G_S, G_A, G_B = 52, 53, 54, 55, 56, 57, 58, 59
        Q, QW, RW = 64, 68, 72

        def col(c, n=1):
            return st[:, c:c + n]

        def prog(me, eng):
            cnt = {k: 0 for k in sems}
            last = {}

            def dma(q, sem, dst, src, **kw):
                cnt[sem] += 16
                if me == q:
                    eng.dma_start(dst, src, **kw).then_inc(sems[sem], 16)
                return cnt[sem]

            def tdma(sem, dst, src, q="sync"):
                cnt[sem] += 16
                if me == q:
                    eng.dma_start(dst, src, transpose=True).then_inc(sems[sem], 16)
                return cnt[sem]

            def wait(who, semname, val):
                if val > 0 and me == who:
                    eng.wait_ge(sems[semname], val)

            def wait_full(who, semname):
                wait(who, semname, cnt[semname])

            def op(engkey, fn, *a, **kw):
                if me == engkey:
                    last[engkey] = getattr(eng, fn)(*a, **kw)

            def sig(engkey):
                cnt[engkey] += 1
                if me == engkey:
                    last[engkey].then_inc(sems[engkey], 1)
                return cnt[engkey]

            def vop(fn, *a, dep=0, **kw):
                wait("dve", "dve", dep)
                op("dve", fn, *a, **kw)
                return sig("dve")

            def gop(fn, *a, dep=0, **kw):
                cnt["gpc"] += 1
                if me == "gp":
                    if dep:
                        eng.wait_ge(sems["gpc"], dep)
                    getattr(eng, fn)(*a, **kw).then_inc(sems["gpc"], 1)
                return cnt["gpc"]

            def aop(fn, *a, dep=0, serial=True, **kw):
                if serial:
                    wait("act", "act", cnt["act"])
                op("act", fn, *a, **kw)
                return sig("act")

            def vbar():
                wait("dve", "dve", cnt["dve"])

            def mm(out, lhsT, rhs, **kw):
                op("pe", "matmul", out, lhsT, rhs, **kw)

            m_bias = dma("sync", "ld", bias_bc[0:1, :], bias_h[:, :])
            vop("memset", stat[:, :], 0.0, dep=0)

            for ch in range(8):
                dma("gp", "gbf", xbf_h[S * ch:S * (ch + 1), :],
                    x_h[S * ch:S * (ch + 1), :])
            for (w_hbm, wbf_hbm) in ((wsh_h, wshbf_h), (wex_h, wexbf_h)):
                for half in range(2):
                    rows = slice(512 * half, 512 * (half + 1))
                    dma("gp", "gw", wbf_hbm[rows, :], w_hbm[rows, :])

            v_z = vop("memset", atok[:, 0, 0:128], 0.0)
            v_o = vop("memset", atok[0:1, 1, 0:128], 1.0, dep=v_z)
            wait("sync", "dve", v_z)
            dma("sync", "ld", ident_h[0:1, 0:16384]
                .rearrange("o (p c) -> p o c", p=128), atok[:, 0:1, 0:128])
            wait_full("sync", "ld")
            wait("sync", "dve", v_o)
            dma("sync", "ld", ident_h[0:1, 0:16512]
                .rearrange("o (a b) -> o a b", a=128)[0:1, :, 0:1],
                atok[0:1, 1:2, 0:128].rearrange("p a c -> p c a"),
                allow_slow_non_contiguous=True)
            wait_full("sync", "ld")
            dma("sync", "ld", ident[:, :, :],
                ident_h[0:1, 0:16384].rearrange("o (p c) -> p o c", p=128))

            r8 = r_buf[:, :, :].rearrange("p t d -> p (t d)").rearrange(
                "p (a b) -> p a b", a=8)
            mv_view = A_buf[:, :, :, :].rearrange("p a b c -> p (a b c)")[:, 0:8192] \
                .rearrange("p (a b) -> p a b", a=8)
            v_w = 0
            for half in range(2):
                cols = slice(512 * half, 512 * (half + 1))
                m_w = dma("sync", "ld", r8,
                          mv_h.rearrange("(a p) m -> p a m", p=128)[:, :, cols])
                wait("dve", "ld", m_w)
                v_w = vop("tensor_copy", mv_view[:, :, cols], r8, dep=v_w)
                wait("sync", "dve", v_w)

            ones_ap = att_acc[0:1, 0, 0:128]
            v_ones = vop("memset", ones_ap, 1.0)
            wait_full("pe", "ld")
            wait("pe", "dve", v_ones)
            mm(ps_d[:, :], ones_ap, bias_bc[0:1, 0:512], start=True, stop=True)
            sig("pe")
            mm(ps_e[:, :], ones_ap, bias_bc[0:1, 512:1024], start=True, stop=True)
            v_pe = sig("pe")
            wait("dve", "pe", v_pe)
            vop("tensor_copy", bias_bc[:, 0:512], ps_d[:, :])
            v_bias = vop("tensor_copy", bias_bc[:, 512:1024], ps_e[:, :])

            rg = r_buf[:, :, :].rearrange("p t d -> p (t d)")[:, 0:56] \
                .rearrange("p (a b) -> p a b", a=8)
            prev_mm = {0: v_pe, 1: v_pe}
            prev_evac = {0: v_bias, 1: v_bias}
            mark_p1mm = 0
            def mk_simple(w_view, w_sb, stg):
                return (lambda: dma("sync", "ld", stg, w_view),
                        lambda dep: vop("tensor_copy", w_sb, stg, dep=dep))

            wjobs = [
                mk_simple(mq_h.rearrange("(a p) m -> p a m", p=128),
                          mq_bf[:, :, :], r8),
                mk_simple(mk_h.rearrange("(a p) m -> p a m", p=128),
                          mk_bf[:, :, :], r8),
                mk_simple(wg_h.rearrange("(a p) e -> p a e", p=128),
                          wg_bf[:, :, :], rg),
            ]
            v_wcp = v_w
            m_wld = 0
            mark_par = {0: 0, 1: 0}
            for ch in range(8):
                if ch == 0:
                    wait_full("sync", "gbf")
                pbuf = xts if ch % 2 == 0 else x1T
                psem = "xts" if ch % 2 == 0 else "xt2"
                wait("sync", "pe", mark_par[ch % 2])
                for dt in range(8):
                    m_xts = tdma(psem, pbuf[:, dt, :],
                                 xbf_h[S * ch:S * (ch + 1), 128 * dt:128 * (dt + 1)])
                if 1 <= ch <= len(wjobs):
                    wait("dve", "ld", m_wld)
                    v_wcp = wjobs[ch - 1][1](v_wcp)
                if ch < len(wjobs):
                    wait("sync", "dve", v_wcp)
                    m_wld = wjobs[ch][0]()
                vbar()
                for tt in range(4):
                    for dvc in range(2):
                        sel = (tt * 2 + dvc) % 2
                        psel = ps_d if sel == 0 else ps_e
                        wait("pe", psem, m_xts)
                        wait("pe", "dve", prev_evac[sel])
                        for dt in range(8):
                            mm(psel[:, :], pbuf[:, dt, 128 * tt:128 * (tt + 1)],
                               mv_view[:, dt, 512 * dvc:512 * (dvc + 1)],
                               start=(dt == 0), stop=(dt == 7))
                        prev_mm[sel] = sig("pe")
                        wait("dve", "pe", prev_mm[sel])
                        wait_full("dve", "xvw")
                        prev_evac[sel] = vop(
                            "tensor_copy", stage_xv[:, tt, 4 * dvc:4 * (dvc + 1), :],
                            psel[:, :])
                mark_p1mm = max(prev_mm[0], prev_mm[1])
                mark_par[ch % 2] = mark_p1mm
                v_stage = max(prev_evac[0], prev_evac[1])
                wait("gp", "dve", v_stage)
                for g2 in range(8):
                    dma("gp", "xvw", xv_h[g2, S * ch:S * (ch + 1), :]
                        .rearrange("(t p) v -> p t v", p=128), stage_xv[:, :, g2, :])

            class NS:
                pass
            ns = NS()
            ns.v_scale = 0
            ns.v_zfin = 0
            ns.v_avadd = {0: 0, 1: 0}
            ns.mark_proj = mark_p1mm
            ns.mark_av_all = 0
            ns.mark_av_hp = [0] * 8
            ns.v_exp = 0
            ns.v_exp_slot = {0: 0, 1: 0}
            ns.v_out2 = {0: 0, 1: 0, 2: 0, 3: 0}
            ns.v_gate_exp = 0
            ns.v_gatepost = 0
            ns.v_q_evac = 0
            ns.v_sq2 = 0
            ns.v_p2 = 0
            ns.v_resid = 0
            ns.v_x1 = 0
            ns.v_ln1 = 0
            ns.m_xres = 0
            ns.m_xts = 0
            ns.m_xt2 = 0
            ns.m_atok = 0
            ns.m_x1T = 0
            ns.m_cast = 0
            ns.mark_ffn = mark_p1mm
            ns.mark_gate = {}
            ns.v_lnlast = 0
            ns.avsig = {}
            ns.v_zfin_par = {0: 0, 1: 0}
            ns.proj_sel = 0
            ns.v_ve1 = 0
            ns.v_ve2 = 0
            ns.v_x1T = 0
            ns.v_gate_fix = 0
            ns.v_ccp = 0
            ns.evacs = {}
            ns.ev_lhsT = 0

            def ln_stats_a(SUMS_, SSQ_, NEGM_, dve_dep):
                va = vop("tensor_scalar", col(NEGM_, 4), col(SUMS_, 4),
                         -1.0 / D, None, ALU.mult,
                         dep=max(dve_dep, ns.v_lnlast))
                vb = vop("tensor_scalar", col(T1, 4), col(SSQ_, 4),
                         1.0 / D, None, ALU.mult, dep=va)
                vc = vop("tensor_tensor", col(T2, 4), col(NEGM_, 4),
                         col(NEGM_, 4), ALU.mult, dep=va)
                vd = vop("tensor_tensor", col(T1, 4), col(T1, 4), col(T2, 4),
                         ALU.subtract, dep=vc)
                ve = vop("tensor_scalar", col(T1, 4), col(T1, 4), 1e-5, None,
                         ALU.add, dep=vd)
                return ve

            def ln_stats_b(RSTD_, ve):
                wait("act", "dve", ve)
                vf = aop("activation", col(T2, 4), col(T1, 4), AF.Ln, 0.0, 1.0)
                vg = aop("activation", col(T3, 4), col(T2, 4), AF.Exp, 0.0, -0.5,
                         dep=vf)
                wait("dve", "act", vg)
                vh = vop("tensor_tensor", col(T2, 4), col(T3, 4), col(T3, 4),
                         ALU.mult, dep=ve)
                vi = vop("tensor_tensor", col(T2, 4), col(T2, 4), col(T1, 4),
                         ALU.mult, dep=vh)
                vj = vop("tensor_scalar", col(T2, 4), col(T2, 4), -0.5, 1.5,
                         ALU.mult, ALU.add, dep=vi)
                ns.v_lnlast = vop("tensor_tensor", col(RSTD_, 4), col(T3, 4),
                                  col(T2, 4), ALU.mult, dep=vj)
                return ns.v_lnlast

            def ln_stats(SUMS_, SSQ_, RSTD_, NEGM_, dve_dep):
                return ln_stats_b(RSTD_, ln_stats_a(SUMS_, SSQ_, NEGM_, dve_dep))

            def k64phys(i, fh):
                return 3 + (fh % 2) if i == 3 else i

            def xts_load(fh):
                wait("sync", "pe", ns.mark_proj)
                wait("act", "pe", ns.mark_proj)
                wait_full("act", "gbf")
                wait_full("act", "gb2")
                for dt in range(8):
                    if dt % 2 == 0:
                        ns.m_xts = tdma("xts", xts[:, dt, :],
                                        xbf_h[S * fh:S * (fh + 1),
                                              128 * dt:128 * (dt + 1)])
                    else:
                        ns.m_xt2 = tdma("xt2", xts[:, dt, :],
                                        xbf_h[S * fh:S * (fh + 1),
                                              128 * dt:128 * (dt + 1)])

            def proj_group(dest_ap, kind, pt, ekey=None):
                sel = ns.proj_sel
                ns.proj_sel ^= 1
                psel = ps_d if sel == 0 else ps_e
                wait("pe", "xts", ns.m_xts)
                wait("pe", "xt2", ns.m_xt2)
                wait("pe", "dve", prev_evac[sel])
                wait("pe", "dve", ns.v_p2)
                wait("pe", "dve", ns.v_gate_fix)
                if kind == "q":
                    for dt in range(8):
                        mm(psel[:, :], mq_bf[:, dt, 128 * pt:128 * (pt + 1)],
                           xts[:, dt, :], start=(dt == 0), stop=(dt == 7))
                elif kind == "k":
                    for dt in range(8):
                        mm(psel[:, :], mk_bf[:, dt, 128 * pt:128 * (pt + 1)],
                           xts[:, dt, :], start=(dt == 0), stop=(dt == 7))
                else:
                    c0 = 64 + 128 * pt
                    if pt < 3:
                        for dt in range(8):
                            mm(psel[:, :], mk_bf[:, dt, c0:c0 + 128],
                               xts[:, dt, :], start=(dt == 0), stop=(dt == 7))
                    else:
                        for dt in range(8):
                            mm(psel[0:64, :], mk_bf[:, dt, 448:512],
                               xts[:, dt, :], start=(dt == 0), stop=(dt == 7))
                        for dt in range(8):
                            mm(psel[64:128, :], mk_bf[:, dt, 0:64],
                               xts[:, dt, :], start=(dt == 0), stop=(dt == 7),
                               tile_position=(0, 64))
                v = sig("pe")
                prev_mm[sel] = v
                wait("dve", "pe", v)
                prev_evac[sel] = vop("tensor_copy", dest_ap, psel[:, :])
                ns.v_q_evac = prev_evac[sel]
                if ekey is not None:
                    ns.evacs[ekey] = prev_evac[sel]

            def proj_sec0(fh):
                for pt in range(4):
                    proj_group(qs[:, fh % 2, pt, :], "q", pt, ("q", fh % 2, pt))
                proj_group(ks[:, 0, :], "k", 0, ("k", 0))
                proj_group(ks64[:, k64phys(3, fh), :], "k64", 3,
                           ("k64", k64phys(3, fh)))
                proj_group(ks64[:, 0, :], "k64", 0, ("k64", 0))

            def gate_chunk(f, tt):
                if tt == 0:
                    wait("pe", "x1t", ns.m_x1T)
                    wait("pe", "dve", ns.v_x1T)
                    wait("pe", "dve", prev_evac[0])
                    wait("pe", "dve", prev_evac[1])
                    wait("pe", "dve", ns.v_p2)
                else:
                    wait("pe", "dve", ns.v_gate_fix)
                psel = ps_d if tt % 2 == 0 else ps_e
                co = 8 * (tt // 2)
                for dt in range(8):
                    mm(psel[:, co:co + 7],
                       x1T[:, dt, 128 * tt:128 * (tt + 1)],
                       wg_bf[:, dt, :], start=(dt == 0), stop=(dt == 7))
                ns.mark_gate[tt] = sig("pe")
                wait("dve", "pe", ns.mark_gate[tt])
                if tt == 0:
                    wait("dve", "act", ns.v_gate_exp)
                va = vop("tensor_scalar", gate_t[:, :], psel[:, co:co + 7],
                         col(RSTD + tt), None, ALU.mult,
                         dep=ns.v_x1 if tt == 0 else 0)
                ns.v_gate_fix = vop("scalar_tensor_tensor", gate_p4[:, tt, :],
                                    gate_c[:, :], col(Q + tt), gate_t[:, :],
                                    ALU.mult, ALU.add, dep=va)
                wait("act", "dve", ns.v_gate_fix)
                ns.v_gate_exp = aop("activation", gate_e4[:, tt, :],
                                    gate_p4[:, tt, :], AF.Exp, 0.0, 1.0,
                                    serial=False)

            def ffn_sh(f, tt, jc):
                if tt == 0 and jc == 0:
                    wait_full("pe", "wT")
                    wait("pe", "dve", ns.v_gate_fix)
                    wait("pe", "dve", prev_evac[0])
                wait("pe", "dve", ns.v_p2)
                for dt in range(8):
                    mm(ps_d[:, :], x1T[:, dt, 128 * tt:128 * (tt + 1)],
                       wshT[:, dt, 512 * jc:512 * (jc + 1)],
                       start=(dt == 0), stop=False)
                for cb in range(4):
                    mm(ps_d[:, 128 * cb:128 * (cb + 1)],
                       x1T[:, 4 * jc + cb, 128 * tt:128 * (tt + 1)],
                       ident[:, 0, :], start=False, stop=(cb == 3))

            def ffn_ex(f, tt, jc):
                if tt == 0 and jc == 0:
                    wait("pe", "dve", prev_evac[1])
                for dt in range(8):
                    mm(ps_e[:, :], x1T[:, dt, 128 * tt:128 * (tt + 1)],
                       wexT[:, dt, 512 * jc:512 * (jc + 1)],
                       start=(dt == 0), stop=(dt == 7))
                ns.mark_ffn = sig("pe")
                hs = slice(512 * jc, 512 * (jc + 1))
                wait("dve", "pe", ns.mark_ffn)
                if tt == 0 and jc == 0:
                    wait_full("dve", "out")
                if jc == 0:
                    vq = vop("tensor_tensor", col(QW + tt), col(Q + tt),
                             col(WSUM + tt), ALU.mult,
                             dep=max(ns.v_gatepost, ns.v_x1))
                    vq = vop("tensor_tensor", col(RW + tt), col(RSTD + tt),
                             col(WSUM + tt), ALU.mult, dep=vq)
                else:
                    vq = 0
                c1 = vop("scalar_tensor_tensor", r_buf[:, tt, hs],
                         csh_bc[:, hs], col(Q + tt), bias_bc[:, hs],
                         ALU.mult, ALU.add, dep=vq)
                c2 = vop("scalar_tensor_tensor", r_buf[:, tt, hs],
                         cex_bc[:, hs], col(QW + tt), r_buf[:, tt, hs],
                         ALU.mult, ALU.add, dep=c1)
                v1 = vop("scalar_tensor_tensor", xres[:, tt, hs],
                         ps_d[:, :], col(RSTD + tt), r_buf[:, tt, hs],
                         ALU.mult, ALU.add, dep=c2)
                ns.v_p2 = vop("scalar_tensor_tensor", xres[:, tt, hs],
                              ps_e[:, :], col(RW + tt), xres[:, tt, hs],
                              ALU.mult, ALU.add,
                              accum_out=col((SUMS2 if jc == 0 else SUMS2B) + tt),
                              dep=v1)
                if jc == 1:
                    ns.v_out2[tt] = ns.v_p2
                    ns.v_sq2 = vop("scalar_tensor_tensor", atok[:, 0, :],
                                   xres[:, tt, :], 1.0, xres[:, tt, :],
                                   ALU.mult, ALU.mult, accum_out=col(SSQ2 + tt),
                                   dep=ns.v_p2)

            def c_piece(i):
                onesb = atok[:, 0, 0:1]
                crow = r_buf[0:1, 0, 0:512]
                ones_row = r_buf[0:1, 1, 0:128]
                if i == 0:
                    wait_full("dve", "xvw")
                    v1 = vop("memset", onesb, 1.0)
                    ns.v_ccp = vop("memset", ones_row, 1.0, dep=v1)
                    wait("pe", "dve", prev_evac[0])
                    wait("pe", "dve", prev_evac[1])
                    wait_full("pe", "wT")
                if i < 4:
                    w_sb = wshT if i < 2 else wexT
                    c_dst = csh_bc if i < 2 else cex_bc
                    jc = i % 2
                    wait("pe", "dve", ns.v_ccp)
                    for dt in range(8):
                        mm(ps_d[0:1, :], onesb,
                           w_sb[:, dt, 512 * jc:512 * (jc + 1)],
                           start=(dt == 0), stop=(dt == 7))
                    v = sig("pe")
                    wait("dve", "pe", v)
                    ns.v_ccp = vop("tensor_copy", crow, ps_d[0:1, :])
                    wait("pe", "dve", ns.v_ccp)
                    mm(ps_e[:, :], ones_row, crow, start=True, stop=True)
                    v = sig("pe")
                    wait("dve", "pe", v)
                    ns.v_ccp = vop("tensor_copy",
                                   c_dst[:, 512 * jc:512 * (jc + 1)], ps_e[:, :])
                    if i < 2:
                        ns.v_ccp = vop("tensor_scalar",
                                       c_dst[:, 512 * jc:512 * (jc + 1)],
                                       c_dst[:, 512 * jc:512 * (jc + 1)],
                                       1.0, None, ALU.add, dep=ns.v_ccp)
                else:
                    wait("pe", "dve", ns.v_ccp)
                    for dt in range(8):
                        mm(ps_d[0:1, 0:7], onesb, wg_bf[:, dt, :],
                           start=(dt == 0), stop=(dt == 7))
                    v = sig("pe")
                    wait("dve", "pe", v)
                    ns.v_ccp = vop("tensor_copy", crow[:, 0:7], ps_d[0:1, 0:7])
                    wait("pe", "dve", ns.v_ccp)
                    mm(ps_e[:, 0:7], ones_row, crow[:, 0:7],
                       start=True, stop=True)
                    v = sig("pe")
                    wait("dve", "pe", v)
                    ns.v_ccp = vop("tensor_copy", gate_c[:, :], ps_e[:, 0:7])
                    prev_evac[0] = ns.v_ccp
                    prev_evac[1] = ns.v_ccp

            def seg_for(f, g2):
                segs = [None] * 5
                if f == 0 and g2 == 4:
                    for i in range(4):
                        segs[i] = (lambda j: lambda: c_piece(j))(i)
                    return segs
                if f == 0 and g2 == 5:
                    segs[0] = lambda: c_piece(4)
                    return segs
                if g2 == 0:
                    segs[0] = lambda: proj_group(ks[:, 1, :], "k", 1, ("k", 1))
                    segs[1] = lambda: proj_group(ks[:, 2, :], "k", 2, ("k", 2))
                    segs[2] = lambda: proj_group(ks[:, 3, :], "k", 3, ("k", 3))
                    segs[3] = lambda: proj_group(ks64[:, 1, :], "k64", 1,
                                                 ("k64", 1))

                    def c4():
                        proj_group(ks64[:, 2, :], "k64", 2, ("k64", 2))
                        ns.mark_proj = max(prev_mm[0], prev_mm[1])
                        if f < 7:
                            xts_load(f + 1)
                    segs[4] = c4
                elif g2 == 1 and f < 7:
                    pi = (f + 1) % 2
                    segs[2] = lambda: proj_group(qs[:, pi, 0, :], "q", 0,
                                                 ("q", pi, 0))
                    segs[3] = lambda: proj_group(qs[:, pi, 1, :], "q", 1,
                                                 ("q", pi, 1))
                elif g2 == 2 and f < 7:
                    pi = (f + 1) % 2
                    segs[0] = lambda: proj_group(qs[:, pi, 2, :], "q", 2,
                                                 ("q", pi, 2))
                    segs[1] = lambda: proj_group(qs[:, pi, 3, :], "q", 3,
                                                 ("q", pi, 3))
                    segs[2] = lambda: proj_group(ks[:, 0, :], "k", 0, ("k", 0))
                    segs[3] = lambda: proj_group(
                        ks64[:, k64phys(3, f + 1), :], "k64", 3,
                        ("k64", k64phys(3, f + 1)))
                elif g2 == 3:
                    if f >= 1:
                        for i in range(4):
                            segs[i] = (lambda tt: lambda: gate_chunk(
                                f - 1, tt))(i)
                    if f < 7:
                        segs[4] = lambda: proj_group(ks64[:, 0, :], "k64", 0,
                                                     ("k64", 0))
                elif g2 >= 4 and f >= 1:
                    tt = g2 - 4
                    segs[0] = lambda: ffn_sh(f - 1, tt, 0)
                    segs[1] = lambda: ffn_ex(f - 1, tt, 0)
                    segs[2] = lambda: ffn_sh(f - 1, tt, 1)
                    segs[3] = lambda: ffn_ex(f - 1, tt, 1)
                return segs

            def epilogue1(f):
                wait("gp", "dve", max(ns.v_avadd[0], ns.v_avadd[1]))
                for hp in range(8):
                    ns.m_cast = dma("gp", "cast", att_h[128 * hp:128 * (hp + 1), :],
                                    att_acc[:, hp, :])
                epilogue1b(f)

            def epilogue1b(f):
                wait_full("sync", "cast")
                wait("sync", "dve", ns.v_sq2)
                for tt in range(4):
                    ns.m_atok = tdma("atok", atok[:, tt, :],
                                     att_h[:, 128 * tt:128 * (tt + 1)])

            def ep2a(f):
                wait_full("dve", "out")
                for tt in range(4):
                    if tt == 0:
                        wait("dve", "atok", ns.m_atok)
                        wait("dve", "xres", ns.m_xres)
                    ns.v_resid = vop("scalar_tensor_tensor", r_buf[:, tt, :],
                                     atok[:, tt, :], 1.0, xres[:, tt, :],
                                     ALU.mult, ALU.add, accum_out=col(SUMS + tt),
                                     dep=ns.v_out2[3] if tt == 0 else 0)
                v_sq = ns.v_resid
                for tt in range(4):
                    v_sq = vop("scalar_tensor_tensor", atok[:, 0, :],
                               r_buf[:, tt, :], 1.0, r_buf[:, tt, :],
                               ALU.mult, ALU.mult, accum_out=col(SSQ + tt),
                               dep=v_sq)
                ns.v_ve1 = ln_stats_a(SUMS, SSQ, NEGM, v_sq)

            def ep2b(f):
                ns.v_ln1 = ln_stats_b(RSTD, ns.v_ve1)
                ns.v_x1 = vop("tensor_tensor", col(Q, 4), col(RSTD, 4),
                              col(NEGM, 4), ALU.mult, dep=ns.v_ln1)

            def rt_issue(f):
                wait("sync", "pe", ns.mark_ffn)
                for dt in range(8):
                    ns.m_x1T = tdma("x1t", x1T[:, dt, :],
                                    xbf_h[S * f:S * (f + 1),
                                          128 * dt:128 * (dt + 1)])

            def rt_add(f):
                wait("dve", "x1t", ns.m_x1T)
                v = 0
                for dt in range(8):
                    v = vop("tensor_tensor", x1T[:, dt, :], x1T[:, dt, :],
                            att_acc[:, dt, :], ALU.add,
                            dep=max(ns.v_avadd[0], ns.v_avadd[1])
                            if dt == 0 else 0)
                ns.v_x1T = v

            def gate_post(f, tt):
                wait("dve", "act", ns.v_gate_exp)
                vc = vop("reduce_max", col(G_M1N), gate_e4[:, tt, :], AX.X,
                         negate=True, dep=ns.v_gatepost)
                vd = vop("tensor_scalar", gate_t[:, :], gate_e4[:, tt, :],
                         col(G_M1N), None, ALU.add, dep=vc)
                ve = vop("tensor_scalar", gate_t[:, :], gate_t[:, :], 0.0,
                         None, ALU.is_equal, dep=vd)
                vf = vop("scalar_tensor_tensor", gate_t[:, :], gate_t[:, :],
                         -1e30, gate_e4[:, tt, :], ALU.mult, ALU.add, dep=ve)
                vg = vop("reduce_max", col(G_M2N), gate_t[:, :], AX.X,
                         negate=True, dep=vf)
                vh = vop("tensor_tensor", col(G_S), col(G_M1N), col(G_M2N),
                         ALU.add, dep=vg)
                vi = vop("reciprocal", col(G_S), col(G_S), dep=vh)
                vj = vop("tensor_tensor", col(G_A), col(G_M1N), col(G_S),
                         ALU.mult, dep=vi)
                vk = vop("tensor_tensor", col(G_B), col(G_M2N), col(G_S),
                         ALU.mult, dep=vj)
                ns.v_gatepost = vop("tensor_tensor", col(WSUM + tt), col(G_A),
                                    col(G_B), ALU.add, dep=vk)

            def ln2a(f):
                vs2 = vop("tensor_tensor", col(SUMS2, 4), col(SUMS2, 4),
                          col(SUMS2B, 4), ALU.add,
                          dep=max(ns.v_sq2, ns.v_lnlast))
                ns.v_ve2 = ln_stats_a(SUMS2, SSQ2, NEGM2, vs2)

            def ln2b(f):
                v_ln2 = ln_stats_b(RSTD2, ns.v_ve2)
                v_final = 0
                for tt in range(4):
                    v_final = vop("tensor_scalar", r_buf[:, tt, :], xres[:, tt, :],
                                  col(NEGM2 + tt), col(RSTD2 + tt), ALU.add,
                                  ALU.mult, dep=v_ln2 if tt == 0 else 0)
                wait("sync", "dve", v_final)
                dma("sync", "out", out_h[S * f:S * (f + 1), :]
                    .rearrange("(t p) d -> p t d", p=128), r_buf[:, :, :])
                if f + 1 < 8:
                    ns.m_xres = dma("sync", "xres", xres[:, :, :],
                                    x_h[S * (f + 1):S * (f + 2), :]
                                    .rearrange("(t p) d -> p t d", p=128))

            def av_pair(f, g2p, jj):
                a = f * 8 + g2p
                for hp in (2 * jj, 2 * jj + 1):
                    half = hp % 2
                    slot = (8 * g2p + hp) % 10
                    wait("pe", "dve", ns.v_scale)
                    wait("pe", "dve", ns.v_avadd[half])
                    for c2t in range(4):
                        mm(ps_av[:, 512 * half:512 * (half + 1)],
                           xv_sl[:, 4 * hp + c2t, :], A_buf[:, c2t, slot, :],
                           start=(c2t == 0), stop=(c2t == 3))
                    v_g = sig("pe")
                    ns.mark_av_hp[hp] = v_g
                    ns.mark_av_all = v_g
                    wait("dve", "pe", v_g)
                    if g2p == 0:
                        wait_full("dve", "cast")
                        ns.v_avadd[half] = vop(
                            "tensor_copy", att_acc[:, hp, :],
                            ps_av[:, 512 * half:512 * (half + 1)],
                            dep=ns.v_x1T)
                    else:
                        ns.v_avadd[half] = vop(
                            "tensor_tensor", att_acc[:, hp, :], att_acc[:, hp, :],
                            ps_av[:, 512 * half:512 * (half + 1)], ALU.add,
                            dep=ns.v_avadd[half])
                ns.avsig[(a, jj)] = ns.mark_av_all

            def s_g2(f, g2, segs):
                wait("gp", "dve", ns.v_scale)
                if f == 0 and g2 == 0:
                    wait_full("gp", "xvw")
                m_xv = dma("gp", "xvr", xv_raw[:, :, :],
                           xv_h[g2, :, :].rearrange("(t p) v -> p t v", p=128))
                gm = (g2 - 1) % 8
                pi = f % 2
                if g2 % 2 == 0:
                    lhsT0s = ks[0:64, g2 // 2, :]
                    lhsT64s = ks64[64:128, k64phys(gm // 2, f), :]
                else:
                    lhsT0s = ks64[0:64, k64phys(gm // 2, f), :]
                    lhsT64s = ks[64:128, g2 // 2, :]
                ns.ev_lhsT = max(ns.evacs.get(("k", g2 // 2), 0),
                                 ns.evacs.get(("k64", k64phys(gm // 2, f)), 0))
                sub_i = 0
                a_cur = f * 8 + g2
                par = g2 % 2
                if g2 >= 1:
                    wave = (f, g2 - 1)
                    pair_order = (0, 1, 2, 3)
                    war_keys = ((a_cur - 2, 3), (a_cur - 1, 0),
                                (a_cur - 1, 1), (a_cur - 1, 2))
                elif f >= 1:
                    wave = (f - 1, 7)
                    pair_order = (2, 3, 0, 1)
                    war_keys = ((a_cur - 1, 2), (a_cur - 1, 3),
                                (a_cur - 2, 3), (a_cur - 1, 0))
                else:
                    wave = None
                    war_keys = ((-1, 0),) * 4
                def do_av(p):
                    av_pair(wave[0], wave[1], p)
                    if wave[1] == 7:
                        for hp in (2 * p, 2 * p + 1):
                            wait("gp", "dve", ns.v_avadd[hp % 2])
                            ns.m_cast = dma("gp", "cast",
                                            att_h[128 * hp:128 * (hp + 1), :],
                                            att_acc[:, hp, :])

                for jj in range(4):
                    av_late = not (g2 == 0 and jj < 2)
                    if wave is not None and not av_late:
                        do_av(pair_order[jj])
                    sg = (8 * g2 + 2 * jj) % 10
                    v_war = ns.avsig.get(war_keys[jj], 0)
                    for c2t in range(4):
                        cs = slice(128 * c2t, 128 * (c2t + 1))
                        sub = sub_i % 2
                        sub_i += 1
                        wait("pe", "act", ns.v_exp_slot[sub])
                        if c2t == 0:
                            if jj == 0:
                                wait("pe", "dve", ns.ev_lhsT)
                            wait("pe", "dve",
                                 ns.evacs.get(("q", pi, jj), 0))
                        b0 = 1024 * sub
                        mm(ps_a[:, b0:b0 + 512],
                           lhsT0s[:, cs], qs[0:64, pi, jj, :],
                           start=True, stop=True, tile_position=(0, 0))
                        mm(ps_a[:, b0 + 512:b0 + 1024],
                           lhsT64s[:, cs], qs[64:128, pi, jj, :],
                           start=True, stop=True, tile_position=(64, 0))
                        v_slot = sig("pe")
                        wait("act", "pe", v_slot)
                        if c2t == 0:
                            pass
                            if jj == 0:
                                wait("act", "dve", ns.v_zfin_par[par])
                        ns.v_exp = aop("activation",
                                       A_buf[:, c2t, sg:sg + 2, :],
                                       ps_a[:, b0:b0 + 1024], AF.Exp, 0.0, 0.125,
                                       accum_out=zacc[:, c2t, jj, par:par + 1],
                                       serial=False)
                        ns.v_exp_slot[sub] = ns.v_exp
                    if wave is not None and av_late:
                        do_av(pair_order[jj])
                    if jj == 3:
                        wait("dve", "act", ns.v_exp)
                        v1 = vop("tensor_tensor", zinv[:, :], zacc[:, :, 0, par],
                                 zacc[:, :, 1, par], ALU.add, dep=ns.v_scale)
                        v1 = vop("tensor_tensor", col(60, 4), zacc[:, :, 2, par],
                                 zacc[:, :, 3, par], ALU.add, dep=v1)
                        v1 = vop("tensor_tensor", zinv[:, :], zinv[:, :],
                                 col(60, 4), ALU.add, dep=v1)
                        ns.v_zfin_par[par] = vop("reciprocal", zinv[:, :],
                                                 zinv[:, :], dep=v1)
                        wait("dve", "xvr", m_xv)
                        wait("dve", "pe", ns.mark_av_all)
                        xvr4 = xv_raw[:, :, :].rearrange("p (a b) v -> p b a v", b=4)
                        xvs4 = xv_sl[:, :, :].rearrange("p (a b) v -> p b a v", b=4)
                        for c2t in range(4):
                            ns.v_scale = vop(
                                "tensor_scalar", xvs4[:, c2t, :, :],
                                xvr4[:, c2t, :, :], zinv[:, c2t:c2t + 1],
                                None, ALU.mult,
                                dep=ns.v_zfin_par[par] if c2t == 0 else 0)
                    if segs[jj] is not None:
                        segs[jj]()
                if segs[4] is not None:
                    segs[4]()

            ns.m_xres = dma("sync", "xres", xres[:, :, :],
                            x_h[0:S, :].rearrange("(t p) d -> p t d", p=128))
            xts_load(0)
            wait("pe", "dve", v_wcp)
            proj_sec0(0)
            for f in range(8):
                for g2 in range(8):
                    s_g2(f, g2, seg_for(f, g2))
                    if f == 0 and g2 == 2:
                        wait_full("sync", "gw")
                        for dt in range(8):
                            tdma("wT", wshT[:, dt, :],
                                 wshbf_h[:, 128 * dt:128 * (dt + 1)])
                        for dt in range(8):
                            tdma("wT", wexT[:, dt, :],
                                 wexbf_h[:, 128 * dt:128 * (dt + 1)])
                    if g2 == 7:
                        if f >= 1:
                            ln2a(f - 1)
                        rt_issue(f)
                    elif f >= 1:
                        if g2 == 0:
                            rt_add(f - 1)
                            if f >= 2:
                                ln2b(f - 2)
                        elif g2 == 1:
                            epilogue1b(f - 1)
                            ep2a(f - 1)
                        elif g2 == 2:
                            ep2b(f - 1)
                        elif 3 <= g2 <= 6:
                            gate_post(f - 1, g2 - 3)
            ln2b(6)
            for p in range(4):
                av_pair(7, 7, p)
            epilogue1(7)
            rt_add(7)
            ep2a(7)
            ep2b(7)
            for tt in range(4):
                gate_chunk(7, tt)
            for tt in range(4):
                gate_post(7, tt)
                ffn_sh(7, tt, 0)
                ffn_ex(7, tt, 0)
                ffn_sh(7, tt, 1)
                ffn_ex(7, tt, 1)
            ln2a(7)
            ln2b(7)

            if taps:
                wait_full("sync", "out")
                dma("sync", "out", taps["o_xv"][:, :, :], xv_h[:, :, :])
                dma("sync", "out", taps["o_attacc"][:, :],
                    att_acc[:, :, :].rearrange("p a b -> p (a b)"))
                dma("sync", "out", taps["o_A"][:, :],
                    A_buf[:, :, :, :].rearrange("p a b c -> p (a b c)"))
                dma("sync", "out", taps["o_zinv"][:, :], zinv[:, :])
                dma("sync", "out", taps["o_qs"][:, :],
                    qs[:, :, :, :].rearrange("p a b c -> p (a b c)"))
                dma("sync", "out", taps["o_ks"][:, :],
                    ks[:, :, :].rearrange("p a b -> p (a b)"))
                dma("sync", "out", taps["o_q64"][:, :],
                    ks64[:, :, :].rearrange("p a b -> p (a b)"))
                dma("sync", "out", taps["o_stat"][:, :], stat[:, :])
                dma("sync", "out", taps["o_xbf"][:, :], xbf_h[:, :])

            wait_full("sync", "out")
            wait_full("gp", "out")

        block.sync(lambda e: prog("sync", e))
        block.gpsimd(lambda e: prog("gp", e))
        block.tensor(lambda e: prog("pe", e))
        block.scalar(lambda e: prog("act", e))
        block.vector(lambda e: prog("dve", e))

        nc.all_engine_barrier()
    return nc


_NC_CACHE = None


def kernel(L0_attn_x, L0_attn_Mq, L0_attn_Mk, L0_attn_Mv, L0_ffn_cond=None,
           L0_ffn_mask=None, L0_ffn_W_shared=None, L0_ffn_W_expert=None,
           L0_ffn_W_gate=None, L0_ffn_bias=None, **_unused):
    global _NC_CACHE
    x = np.ascontiguousarray(np.asarray(L0_attn_x, dtype=np.float32))
    mq = np.ascontiguousarray(np.asarray(L0_attn_Mq, dtype=np.float32))
    mk = np.ascontiguousarray(np.asarray(L0_attn_Mk, dtype=np.float32))
    mv = np.ascontiguousarray(np.asarray(L0_attn_Mv, dtype=np.float32))
    wsh = np.ascontiguousarray(np.asarray(L0_ffn_W_shared, dtype=np.float32))
    wex = np.ascontiguousarray(np.asarray(L0_ffn_W_expert, dtype=np.float32))
    wg = np.ascontiguousarray(np.asarray(L0_ffn_W_gate, dtype=np.float32))
    bias = np.ascontiguousarray(np.asarray(L0_ffn_bias, dtype=np.float32).reshape(1, D))

    if _NC_CACHE is None:
        _NC_CACHE = build_program()
    nc = _NC_CACHE
    in_maps = [{"x": x[b], "mq": mq, "mk": mk, "mv": mv, "wsh": wsh,
                "wex": wex, "wg": wg, "bias": bias} for b in range(NCORES)]
    res = run_bass_kernel_spmd(nc, in_maps, core_ids=list(range(NCORES)))
    out = np.stack([np.asarray(res.results[b]["out"]) for b in range(NCORES)], axis=0)
    return out.astype(np.float32)
